# version 03
# baseline (speedup 1.0000x reference)
import numpy as np
import contextlib
import concourse.bass as bass
import concourse.mybir as mybir
from concourse.bass_utils import run_bass_kernel_spmd

F32 = mybir.dt.float32
BF16 = mybir.dt.bfloat16
AF = mybir.ActivationFunctionType
ALU = mybir.AluOpType
AX = mybir.AxisListType

NCORES = 8
T = 2048
D = 1024
NT = T // 128
ND = D // 128
MARGIN = 20.0



def prep_weights(inp):
    w = {}
    H, Dh = 16, 64
    Wq = inp["L0_attn_Wq"].reshape(H, Dh, D)
    Wk = inp["L0_attn_Wk"].reshape(H, Dh, D)
    Wv = inp["L0_attn_Wv"].reshape(H, Dh, D)
    blocks, bvec = [], []
    for p in range(8):
        h1, h2 = 2 * p, 2 * p + 1
        blocks += [Wq[h1].T, Wq[h2].T, Wk[h1].T, Wk[h2].T, Wv[h1].T, Wv[h2].T]
        bvec += [inp["L0_attn_bq"][h1], inp["L0_attn_bq"][h2],
                 inp["L0_attn_bk"][h1], inp["L0_attn_bk"][h2],
                 inp["L0_attn_bv"][h1], inp["L0_attn_bv"][h2]]
    w["l0a_wqkv"] = np.ascontiguousarray(np.concatenate(blocks, axis=1))
    w["l0a_bqkv"] = np.ascontiguousarray(np.concatenate(bvec))
    w["l0a_proj"] = np.ascontiguousarray(inp["L0_attn_proj"])
    w["l0f_w1x"] = np.ascontiguousarray(inp["L0_ffn_w1"][:D])
    w["l0f_w1s"] = np.ascontiguousarray(inp["L0_ffn_w1"][D:])
    w["l0f_b1"] = np.ascontiguousarray(inp["L0_ffn_b1"])
    w["l0f_w2"] = np.ascontiguousarray(inp["L0_ffn_w2"])
    w["l0f_b2"] = np.ascontiguousarray(inp["L0_ffn_b2"])
    w["l0f_tau"] = np.ascontiguousarray(inp["L0_ffn_tau"])
    w["l0f_wihT"] = np.ascontiguousarray(inp["L0_ffn_Wih"].T)
    w["l0f_whhT"] = np.ascontiguousarray(inp["L0_ffn_Whh"].T)
    w["l0f_bih"] = np.ascontiguousarray(inp["L0_ffn_bih"])
    w["l0f_bhh"] = np.ascontiguousarray(inp["L0_ffn_bhh"])
    w["l0f_wq"] = np.ascontiguousarray(inp["L0_ffn_wq"])
    w["l0f_wk"] = np.ascontiguousarray(inp["L0_ffn_wk"])
    w["l0f_wv"] = np.ascontiguousarray(inp["L0_ffn_wv"])
    w["l0f_bq"] = np.ascontiguousarray(inp["L0_ffn_bq"])
    w["l0f_bk"] = np.ascontiguousarray(inp["L0_ffn_bk"])
    w["l0f_bv"] = np.ascontiguousarray(inp["L0_ffn_bv"])
    w["l0f_lng"] = np.ascontiguousarray(inp["L0_ffn_ln_g"])
    w["l0f_lnb"] = np.ascontiguousarray(inp["L0_ffn_ln_b"])
    w["l0f_wvp"] = np.ascontiguousarray(inp["L0_ffn_wvp"])
    w["l0f_bvp"] = np.ascontiguousarray(inp["L0_ffn_bvp"])
    w["l0f_wo"] = np.ascontiguousarray(inp["L0_ffn_wo"])
    w["l0f_bo"] = np.ascontiguousarray(inp["L0_ffn_bo"])
    w["l0f_slot"] = np.ascontiguousarray(inp["L0_ffn_slot_init"])
    Wqkv = inp["L1_attn_W_qkv"].reshape(6, H, Dh, D)
    blocks = []
    for h in range(H):
        for d6 in range(6):
            blocks.append(Wqkv[d6, h].T)
    w["l1a_wqkv"] = np.ascontiguousarray(np.concatenate(blocks, axis=1))
    w["l1a_projT"] = np.ascontiguousarray(inp["L1_attn_W_proj"].T)
    w["l1a_lam"] = np.ascontiguousarray(np.stack([
        inp["L1_attn_lambda_q1"], inp["L1_attn_lambda_k1"],
        inp["L1_attn_lambda_q2"], inp["L1_attn_lambda_k2"]], axis=1))
    w["l1f_w1"] = np.ascontiguousarray(inp["L1_ffn_W1"])
    w["l1f_b1"] = np.ascontiguousarray(inp["L1_ffn_b1"])
    w["l1f_w2"] = np.ascontiguousarray(inp["L1_ffn_W2"])
    w["l1f_b2"] = np.ascontiguousarray(inp["L1_ffn_b2"])
    w["c_ident"] = np.eye(128, dtype=np.float32)
    tt = np.arange(128)[:, None]
    qq = np.arange(512)[None, :]
    w["c_mask"] = np.ascontiguousarray(np.concatenate(
        [(tt + o <= qq).astype(np.float32) for o in (0, 128, 256, 384)], axis=1))
    w["c_tri"] = (np.arange(128)[:, None] <= np.arange(128)[None, :]).astype(np.float32)
    hd = 64
    inv_freq = 1.0 / 10000.0 ** (np.arange(0, 2 * hd, 2, dtype=np.float32) / (2 * hd))
    freqs = np.arange(T, dtype=np.float32)[:, None] * inv_freq[None, :]
    emb = np.concatenate((freqs, freqs), axis=-1)
    cos = np.cos(emb).astype(np.float32)
    sin = np.sin(emb).astype(np.float32)
    sin_signed = sin.copy()
    sin_signed[:, :hd] *= -1.0
    w["c_cos"] = np.ascontiguousarray(cos.T)
    w["c_sin"] = np.ascontiguousarray(sin_signed.T)
    return w


WEIGHT_SPECS = [
    ("l0a_wqkv", [1024, 3072]), ("l0a_bqkv", [3072]), ("l0a_proj", [1024, 1024]),
    ("l0f_w1x", [1024, 512]), ("l0f_w1s", [1024, 512]), ("l0f_b1", [512]),
    ("l0f_w2", [512, 64]), ("l0f_b2", [64]), ("l0f_tau", [1]),
    ("l0f_wihT", [1024, 3072]), ("l0f_whhT", [1024, 3072]),
    ("l0f_bih", [3072]), ("l0f_bhh", [3072]),
    ("l0f_wq", [1024, 256]), ("l0f_wk", [1024, 256]), ("l0f_wv", [1024, 1024]),
    ("l0f_bq", [256]), ("l0f_bk", [256]), ("l0f_bv", [1024]),
    ("l0f_lng", [1024]), ("l0f_lnb", [1024]),
    ("l0f_wvp", [1024, 1024]), ("l0f_bvp", [1024]),
    ("l0f_wo", [1024, 1024]), ("l0f_bo", [1024]), ("l0f_slot", [64, 1024]),
    ("l1a_wqkv", [1024, 6144]), ("l1a_projT", [2048, 1024]), ("l1a_lam", [64, 4]),
    ("l1f_w1", [1024, 4096]), ("l1f_b1", [4096]),
    ("l1f_w2", [4096, 1024]), ("l1f_b2", [1024]),
    ("c_ident", [128, 128]), ("c_mask", [128, 2048]), ("c_tri", [128, 128]),
    ("c_cos", [128, 2048]), ("c_sin", [128, 2048]),
]



class Sem:
    def __init__(self, env, name):
        self.h = env.es.enter_context(env.nc.semaphore(name))
        self.n = 0

    def inc(self, instr, k=1):
        if k:
            instr.then_inc(self.h, k)
            self.n += k
        return instr

    def dma(self, instr):
        return self.inc(instr, 16)


class Env:
    def __init__(self, nc):
        self.nc = nc
        self.es = contextlib.ExitStack()
        self.hbm = {}
        self._semi = 0

    def sbuf(self, name, shape, dtype, stack=None):
        return (stack or self.es).enter_context(
            self.nc.sbuf_tensor(name, shape, dtype))

    def psum(self, name, shape, dtype=F32, stack=None):
        return (stack or self.es).enter_context(
            self.nc.psum_tensor(name, shape, dtype))

    def sem(self, name=None):
        if name is None:
            self._semi += 1
            name = f"sem_{self._semi}"
        if not hasattr(self, "_semcache"):
            self._semcache = {}
        if name not in self._semcache:
            self._semcache[name] = Sem(self, name)
        return self._semcache[name]

    def uname(self, base):
        self._semi += 1
        return f"{base}_{self._semi}"


def dbg(env, name, ap, shape):
    if not getattr(env, "debug", False):
        return
    nc = env.nc
    name = name + getattr(env, "dbg_tag", "")
    d = nc.dram_tensor("dbg_" + name, shape, ap.tensor.dtype,
                       kind="ExternalOutput").ap()
    s = env.sem("dbg")
    with nc.Block() as blk:
        @blk.sync
        def _(e):
            s.dma(e.dma_start(d, ap))
            e.wait_ge(s.h, s.n)


def declare_hbm(env, x_batches=2):
    nc = env.nc
    env.x_in = nc.dram_tensor("x", [x_batches, T, D], F32, kind="ExternalInput").ap()
    env.x_out = nc.dram_tensor("out", [x_batches, T, D], F32,
                               kind="ExternalOutput").ap()
    for name, shape in WEIGHT_SPECS:
        env.hbm[name] = nc.dram_tensor(name, shape, F32, kind="ExternalInput").ap()


def alloc_core(env):
    env.X = env.sbuf("Xres", [128, NT * 1024], F32)
    env.BIGPOOL = env.sbuf("BIGPOOL", [128, NT * 1024], BF16)
    env.XT = env.BIGPOOL
    env.IDENT = env.sbuf("IDENT", [128, 128], BF16)
    env.IDF32 = env.sbuf("IDF32", [128, 128], F32)
    env.ONES = env.sbuf("ONESBF", [128, 128], BF16)
    env.MASK = env.sbuf("MASKBF", [128, 4 * 512], BF16)
    env.TRI = env.sbuf("TRIBF", [128, 128], BF16)
    env.STAT = env.sbuf("STAT", [128, 64], F32)
    env.SCR = env.sbuf("SCR", [128, 1024], F32)


def tl(buf, ti, w=1024):
    return buf[:, ti * w:(ti + 1) * w]



def emit_load_x(env, seq):
    nc = env.nc
    s = env.sem("ldx")
    with nc.Block() as blk:
        @blk.sync
        def _(e):
            for ti in range(NT):
                s.dma(e.dma_start(tl(env.X, ti),
                                  env.x_in[seq, ti * 128:(ti + 1) * 128, :]))
            e.wait_ge(s.h, s.n)


def emit_store_x(env, seq):
    nc = env.nc
    s = env.sem("stx")
    with nc.Block() as blk:
        @blk.sync
        def _(e):
            for ti in range(NT):
                s.dma(e.dma_start(env.x_out[seq, ti * 128:(ti + 1) * 128, :],
                                  tl(env.X, ti)))
            e.wait_ge(s.h, s.n)


def emit_constants(env):
    nc = env.nc
    s = env.sem("ldc")
    with nc.Block() as blk:
        @blk.gpsimd
        def _(g):
            s.dma(g.dma_start(env.IDENT[:], env.hbm["c_ident"]))
            s.dma(g.dma_start(env.IDF32[:], env.hbm["c_ident"]))
            s.dma(g.dma_start(env.MASK[:], env.hbm["c_mask"]))
            s.dma(g.dma_start(env.TRI[:], env.hbm["c_tri"]))
            g.wait_ge(s.h, s.n)
        @blk.vector
        def _(v):
            v.memset(env.ONES[:], 1.0)


def emit_colload(env, hbm_name, n, dst):
    nc = env.nc
    s = env.sem("clod")
    b_s = s.n
    with contextlib.ExitStack() as ls:
        stg = env.sbuf(env.uname("clstg"), [max(1, n), 128], F32, stack=ls)
        pt = env.psum(env.uname("clpt"), [128, max(n, 1)], F32, stack=ls)
        with nc.Block() as blk:
            @blk.sync
            def _(e):
                s.dma(e.dma_start(stg[0:n, :],
                      env.hbm[hbm_name].rearrange("(p a) -> p a", p=n)))
            @blk.tensor
            def _(t):
                t.wait_ge(s.h, b_s + 16)
                t.transpose(pt[:, 0:n], stg[0:n, :], env.IDF32[0:n, 0:n])
        with nc.Block() as blk:
            @blk.vector
            def _(v):
                v.tensor_copy(dst[:, 0:n], pt[:, 0:n])


def emit_shadow(env, stack, do_xt=True):
    nc = env.nc
    xf = env.sbuf(env.uname("XF"), [128, ND * 2048], BF16, stack=stack)
    if do_xt:
        with nc.Block() as blk:
            @blk.vector
            def _(v):
                for ti in range(NT):
                    v.tensor_copy(tl(env.XT, ti), tl(env.X, ti))
    st = env.sem("sh_t")
    se = env.sem("sh_e")
    with contextlib.ExitStack() as ls:
        ptr = env.psum(env.uname("ps_shtr"), [128, 2048], BF16, stack=ls)
        b_st, b_se = st.n, se.n
        with nc.Block() as blk:
            @blk.tensor
            def _(t):
                grp = 0
                for dc in range(ND):
                    for g in range(4):
                        half = grp % 2
                        if grp >= 2:
                            t.wait_ge(se.h, b_se + grp - 1)
                        for j in range(4):
                            ti = g * 4 + j
                            ins = t.transpose(
                                ptr[:, half * 1024 + j * 128:half * 1024 + (j + 1) * 128],
                                tl(env.XT, ti)[:, dc * 128:(dc + 1) * 128],
                                env.IDENT[:])
                            st.inc(ins, 1 if j == 3 else 0)
                        grp += 1

            @blk.vector
            def _(v):
                grp = 0
                for dc in range(ND):
                    for g in range(4):
                        half = grp % 2
                        v.wait_ge(st.h, b_st + grp + 1)
                        dst = xf[:, dc * 2048 + g * 512:dc * 2048 + (g + 1) * 512]
                        se.inc(v.tensor_copy(dst, ptr[:, half * 1024:half * 1024 + 512]))
                        grp += 1
    return xf


def emit_ln_epilogue(env):
    nc = env.nc
    s1 = env.sem("ln_a")
    b1 = s1.n
    with nc.Block() as blk:
        @blk.vector
        def _(v):
            for ti in range(NT):
                v.tensor_scalar(env.SCR[:, 0:1024], tl(env.X, ti), 1.0, 0.0,
                                op0=ALU.mult, op1=ALU.add,
                                accum_out=env.STAT[:, ti:ti + 1])
            for ti in range(NT):
                v.scalar_tensor_tensor(env.SCR[:, 0:1024], tl(env.X, ti), 1.0,
                                       tl(env.X, ti), op0=ALU.mult, op1=ALU.mult,
                                       accum_out=env.STAT[:, 16 + ti:17 + ti])
            v.drain()
            v.tensor_scalar(env.STAT[:, 32:48], env.STAT[:, 0:16], 1.0 / 1024, None,
                            op0=ALU.mult)
            v.tensor_scalar(env.STAT[:, 16:32], env.STAT[:, 16:32], 1.0 / 1024,
                            1e-5, op0=ALU.mult, op1=ALU.add)
            v.drain()
            v.scalar_tensor_tensor(env.STAT[:, 48:64], env.STAT[:, 32:48], 1.0,
                                   env.STAT[:, 32:48], op0=ALU.mult, op1=ALU.mult)
            v.drain()
            s1.inc(v.tensor_tensor(env.STAT[:, 16:32], env.STAT[:, 16:32],
                                   env.STAT[:, 48:64], op=ALU.subtract))

        @blk.scalar
        def _(sc):
            sc.wait_ge(s1.h, b1 + 1)
            sc.activation(env.STAT[:, 48:64], env.STAT[:, 16:32], AF.Sqrt,
                          bias=0.0, scale=1.0)
    with nc.Block() as blk:
        @blk.vector
        def _(v):
            v.reciprocal(env.STAT[:, 16:32], env.STAT[:, 48:64])
            v.drain()
            for ti in range(NT):
                v.tensor_scalar(tl(env.X, ti), tl(env.X, ti),
                                env.STAT[:, 32 + ti:33 + ti],
                                env.STAT[:, 16 + ti:17 + ti],
                                op0=ALU.subtract, op1=ALU.mult)


def emit_bcast_row(env, blk_scope, dst_sb, src_row_bf, width):
    nc = env.nc
    with contextlib.ExitStack() as ls:
        pb = env.psum(env.uname("ps_bc"), [128, ((width + 511) // 512) * 512], F32, stack=ls)
        with nc.Block() as blk:
            @blk.tensor
            def _(t):
                for c0 in range(0, width, 512):
                    cw = min(512, width - c0)
                    t.matmul(pb[:, c0:c0 + cw], env.ONES[0:1, :],
                             src_row_bf[0:1, c0:c0 + cw], start=True, stop=True)
        with nc.Block() as blk:
            @blk.vector
            def _(v):
                v.tensor_copy(dst_sb[:, 0:width], pb[:, 0:width])



def emit_l1_ffn(env):
    nc = env.nc
    with contextlib.ExitStack() as ls:
        XF = emit_shadow(env, ls)
        w1m = env.sbuf(env.uname("w1m"), [128, 2 * 1024], BF16, stack=ls)
        w2s = env.sbuf(env.uname("w2s"), [128, 2 * 512], BF16, stack=ls)
        hbuf = env.sbuf(env.uname("hbuf"), [128, 32 * 512], BF16, stack=ls)
        b1c = env.sbuf(env.uname("b1c"), [128, 32], F32, stack=ls)
        b2bc = env.sbuf(env.uname("b2bc"), [128, 1024], F32, stack=ls)
        b2row = env.sbuf(env.uname("b2row"), [1, 1024], BF16, stack=ls)
        ph = env.psum(env.uname("l1f_ph"), [128, 2 * 512], F32, stack=ls)
        po = env.psum(env.uname("l1f_po"), [128, 4 * 512], F32, stack=ls)

        sw1 = env.sem("l1f_w1")
        sw2 = env.sem("l1f_w2")
        sh = env.sem("l1f_h")
        she = env.sem("l1f_he")
        s2 = env.sem("l1f_o")
        sev = env.sem("l1f_ev")

        sc0 = env.sem("l1f_c")
        emit_colload(env, "l1f_b1", 32, b1c)
        with nc.Block() as blk:
            @blk.gpsimd
            def _(g):
                sc0.dma(g.dma_start(b2row[0:1, :],
                        env.hbm["l1f_b2"].rearrange("(a b) -> a b", a=1)))
                g.wait_ge(sc0.h, sc0.n)
        emit_bcast_row(env, None, b2bc, b2row, 1024)

        w1r = env.hbm["l1f_w1"].rearrange("(kc p) m -> p kc m", p=128)
        for tc in range(4):
            b_sw1, b_sw2 = sw1.n, sw2.n
            b_sh, b_she, b_s2, b_sev = sh.n, she.n, s2.n, sev.n
            with nc.Block() as blk:
                @blk.gpsimd
                def _(g):
                    for m in range(32):
                        if m >= 2:
                            g.wait_ge(sh.h, b_sh + m - 1)
                        half = m % 2
                        sw1.dma(g.dma_start(
                            w1m[:, half * 1024:(half + 1) * 1024]
                                .rearrange("p (a b) -> p a b", b=128),
                            w1r[:, :, m * 128:(m + 1) * 128]))
                    for e in range(2):
                        for m in range(32):
                            g2 = e * 32 + m
                            if g2 >= 2:
                                g.wait_ge(s2.h, b_s2 + g2 - 1)
                            half = g2 % 2
                            sw2.dma(g.dma_start(
                                w2s[:, half * 512:(half + 1) * 512],
                                env.hbm["l1f_w2"][m * 128:(m + 1) * 128,
                                                  e * 512:(e + 1) * 512]))

                @blk.tensor
                def _(t):
                    for m in range(32):
                        half = m % 2
                        t.wait_ge(sw1.h, b_sw1 + (m + 1) * 16)
                        if m >= 2:
                            t.wait_ge(she.h, b_she + m - 1)
                        for k in range(8):
                            ins = t.matmul(
                                ph[:, half * 512:(half + 1) * 512],
                                w1m[:, half * 1024 + k * 128:half * 1024 + (k + 1) * 128],
                                XF[:, k * 2048 + tc * 512:k * 2048 + (tc + 1) * 512],
                                start=(k == 0), stop=(k == 7))
                            sh.inc(ins, 1 if k == 7 else 0)
                    t.wait_ge(she.h, b_she + 32)
                    for e in range(2):
                        if e == 1:
                            t.wait_ge(sev.h, b_sev + 4)
                        for m in range(32):
                            g2 = e * 32 + m
                            half = g2 % 2
                            t.wait_ge(sw2.h, b_sw2 + (g2 + 1) * 16)
                            for ti4 in range(4):
                                ins = t.matmul(
                                    po[:, ti4 * 512:(ti4 + 1) * 512],
                                    hbuf[:, m * 512 + ti4 * 128:m * 512 + (ti4 + 1) * 128],
                                    w2s[:, half * 512:(half + 1) * 512],
                                    start=(m == 0), stop=(m == 31))
                                s2.inc(ins, 1 if ti4 == 3 else 0)

                @blk.scalar
                def _(sc):
                    for m in range(32):
                        half = m % 2
                        sc.wait_ge(sh.h, b_sh + m + 1)
                        she.inc(sc.activation(hbuf[:, m * 512:(m + 1) * 512],
                                              ph[:, half * 512:(half + 1) * 512],
                                              AF.Relu, bias=b1c[:, m:m + 1],
                                              scale=1.0))

                @blk.vector
                def _(v):
                    for e in range(2):
                        v.wait_ge(s2.h, b_s2 + (e + 1) * 32)
                        for ti4 in range(4):
                            ti = tc * 4 + ti4
                            xs = tl(env.X, ti)[:, e * 512:(e + 1) * 512]
                            v.tensor_tensor(env.SCR[:, 0:512],
                                            po[:, ti4 * 512:(ti4 + 1) * 512],
                                            b2bc[:, e * 512:(e + 1) * 512],
                                            op=ALU.add)
                            sev.inc(v.tensor_tensor(xs, xs, env.SCR[:, 0:512],
                                                    op=ALU.add))
    emit_ln_epilogue(env)



def ap3(ap2, group, inner):
    return ap2.rearrange("p (a b) -> p a b", b=inner)


def apcols(t, offset, step, count):
    a = t[:, 0:1]
    return bass.AP(a.tensor, a.offset + offset, [list(a.ap[0]), [step, count]])


def apcols3(t, offset, step, count, inner):
    a = t[:, 0:1]
    return bass.AP(a.tensor, a.offset + offset,
                   [list(a.ap[0]), [step, count], [1, inner]])


def apbcast(t, offset, step, count, inner):
    a = t[:, 0:1]
    return bass.AP(a.tensor, a.offset + offset,
                   [list(a.ap[0]), [step, count], [0, inner]])


def emit_l0_attn(env):
    nc = env.nc
    Pb = env.BIGPOOL
    with contextlib.ExitStack() as ls:
        XF = emit_shadow(env, ls)
        wks = env.sbuf(env.uname("wks"), [128, 2 * 384], BF16, stack=ls)
        qkvp = env.sbuf(env.uname("qkvp"), [128, 3 * 2048], BF16, stack=ls)
        vaug = env.sbuf(env.uname("vaug"), [128, 2 * 1040], BF16, stack=ls)
        Zcat = env.sbuf(env.uname("Zcat"), [128, 8 * 2048], BF16, stack=ls)
        bqkv = env.sbuf(env.uname("bqkv"), [128, 24], F32, stack=ls)
        cselfS = env.sbuf(env.uname("cselfS"), [128, 2], F32, stack=ls)
        cmax2 = env.sbuf(env.uname("cmax2"), [2, 4], F32, stack=ls)
        b2bf = env.sbuf(env.uname("b2bf"), [1, 2], BF16, stack=ls)
        biascol = env.sbuf(env.uname("biascol"), [128, 2], F32, stack=ls)
        linv4 = env.sbuf(env.uname("linv4"), [128, 8], F32, stack=ls)
        ztmp = env.sbuf(env.uname("ztmp"), [128, 512], BF16, stack=ls)

        emit_colload(env, "l0a_bqkv", 24, bqkv)
        with nc.Block() as blk:
            @blk.vector
            def _(v):
                v.memset(apcols(vaug, 64, 65, 32), 1.0)

        wr = env.hbm["l0a_wqkv"].rearrange("(kc p) m -> p kc m", p=128)
        swk = env.sem("l0a_wk")
        sk = env.sem("l0a_kc")
        sq = env.sem("l0a_q")
        sqe = env.sem("l0a_qe")
        stv = env.sem("l0a_tv")
        sve = env.sem("l0a_ve")
        skq = env.sem("l0a_kq")
        ssf = env.sem("l0a_sf")
        scs = env.sem("l0a_cs")
        sct = env.sem("l0a_ct")
        scb = env.sem("l0a_cb")
        sbc = env.sem("l0a_bc")
        sS = env.sem("l0a_S")
        sexp = env.sem("l0a_e")
        smask = env.sem("l0a_m")
        sZ = env.sem("l0a_Z")
        sZn = env.sem("l0a_Zn")
        sZt = env.sem("l0a_Zt")
        sZe = env.sem("l0a_Ze")

        for p in range(8):
            with contextlib.ExitStack() as ps1:
                psQ = env.psum(env.uname("psQ"), [128, 6 * 512], F32, stack=ps1)
                b_swk, b_sk, b_sq, b_sqe = swk.n, sk.n, sq.n, sqe.n
                with nc.Block() as blk:
                    @blk.gpsimd
                    def _(g):
                        for half in range(2):
                            for k in range(8):
                                idx = half * 8 + k
                                if idx >= 2:
                                    g.wait_ge(sk.h, b_sk + idx - 1)
                                swk.dma(g.dma_start(
                                    ap3(wks[:, (idx % 2) * 384:(idx % 2 + 1) * 384],
                                        1, 384),
                                    wr[:, k:k + 1, 384 * p:384 * (p + 1)]))

                    @blk.tensor
                    def _(t):
                        for half in range(2):
                            for k in range(8):
                                idx = half * 8 + k
                                t.wait_ge(swk.h, b_swk + (idx + 1) * 16)
                                if half == 1 and k == 0:
                                    t.wait_ge(sqe.h, b_sqe + 6)
                                for m in range(3):
                                    for q2 in range(2):
                                        gi = m * 2 + q2
                                        ins = t.matmul(
                                            psQ[:, gi * 512:(gi + 1) * 512],
                                            wks[:, (idx % 2) * 384 + m * 128:
                                                (idx % 2) * 384 + (m + 1) * 128],
                                            XF[:, k * 2048 + half * 1024 +
                                                   q2 * 512:k * 2048 + half * 1024 +
                                                   (q2 + 1) * 512],
                                            start=(k == 0), stop=(k == 7))
                                        if k == 7:
                                            sq.inc(ins)
                                            if m == 2 and q2 == 1:
                                                sk.inc(t.nop())
                                        elif m == 2 and q2 == 1:
                                            sk.inc(ins)

                    @blk.scalar
                    def _(sc):
                        for half in range(2):
                            for m in range(3):
                                for q2 in range(2):
                                    gi = m * 2 + q2
                                    sc.wait_ge(sq.h, b_sq + half * 6 + gi + 1)
                                    sqe.inc(sc.activation(
                                        qkvp[:, m * 2048 + half * 1024 + q2 * 512:
                                             m * 2048 + half * 1024 + (q2 + 1) * 512],
                                        psQ[:, gi * 512:(gi + 1) * 512],
                                        AF.Identity, bias=bqkv[:, p * 3 + m:p * 3 + m + 1],
                                        scale=1.0))

            with contextlib.ExitStack() as ps2:
                psT = env.psum(env.uname("psT"), [128, 2048], BF16, stack=ps2)
                psSelf = env.psum(env.uname("psSelf"), [128, 128], F32, stack=ps2)
                b_stv, b_sve, b_skq, b_ssf = stv.n, sve.n, skq.n, ssf.n
                b_scs, b_sct, b_scb, b_sbc = scs.n, sct.n, scb.n, sbc.n
                kqscr = Pb[:, 14 * 512:18 * 512]
                with nc.Block() as blk:
                    @blk.tensor
                    def _(t):
                        for h2 in range(2):
                            for g in range(4):
                                grp = h2 * 4 + g
                                half = grp % 2
                                if grp >= 2:
                                    t.wait_ge(sve.h, b_sve + grp - 1)
                                for j in range(4):
                                    ti = g * 4 + j
                                    ins = t.transpose(
                                        psT[:, half * 1024 + j * 64:
                                            half * 1024 + (j + 1) * 64],
                                        qkvp[64 * h2:64 * h2 + 64,
                                             2 * 2048 + ti * 128:2 * 2048 + (ti + 1) * 128],
                                        env.IDENT[64 * h2:64 * h2 + 64,
                                                  64 * h2:64 * h2 + 64])
                                    stv.inc(ins, 1 if j == 3 else 0)
                        for h2 in range(2):
                            t.wait_ge(skq.h, b_skq + h2 + 1)
                            for ti in range(16):
                                ins = t.matmul(
                                    psSelf[:, h2 * 16 + ti:h2 * 16 + ti + 1],
                                    kqscr[64 * h2:64 * h2 + 64, ti * 128:(ti + 1) * 128],
                                    env.ONES[64 * h2:64 * h2 + 64, 0:1],
                                    start=True, stop=True)
                                ssf.inc(ins, 1 if ti == 15 else 0)
                        t.wait_ge(scs.h, b_scs + 2)
                        sct.inc(t.transpose(psSelf[0:2, 0:128], cselfS[:, 0:2],
                                            env.IDF32[:]))
                        t.wait_ge(scb.h, b_scb + 1)
                        sct.inc(t.transpose(psSelf[0:1, 100:102], cmax2[0:2, 1:2],
                                            env.IDF32[0:2, 0:2]))
                        t.wait_ge(scb.h, b_scb + 2)
                        for h2 in range(2):
                            sbc.inc(t.matmul(psSelf[:, 64 + h2:65 + h2],
                                             env.ONES[0:1, :],
                                             b2bf[0:1, h2:h2 + 1],
                                             start=True, stop=True))

                    @blk.vector
                    def _(v):
                        for h2 in range(2):
                            skq.inc(v.tensor_tensor(
                                kqscr[64 * h2:64 * h2 + 64, :],
                                qkvp[64 * h2:64 * h2 + 64, 2048:4096],
                                qkvp[64 * h2:64 * h2 + 64, 0:2048], op=ALU.mult))
                        for h2 in range(2):
                            for g in range(4):
                                grp = h2 * 4 + g
                                half = grp % 2
                                v.wait_ge(stv.h, b_stv + grp + 1)
                                sve.inc(v.tensor_scalar(
                                    apcols3(vaug, h2 * 1040 + g * 260, 65, 4, 64),
                                    ap3(psT[:, half * 1024:half * 1024 + 256], 4, 64),
                                    0.125, None, op0=ALU.mult))
                        v.wait_ge(ssf.h, b_ssf + 2)
                        for h2 in range(2):
                            ins = v.reduce_max(cselfS[:, h2:h2 + 1],
                                               psSelf[:, h2 * 16:(h2 + 1) * 16],
                                               axis=AX.X)
                            scs.inc(ins)
                        v.wait_ge(sct.h, b_sct + 1)
                        v.reduce_max(cmax2[0:2, 0:1], psSelf[0:2, 0:128], axis=AX.X)
                        v.drain()
                        scb.inc(v.tensor_scalar(cmax2[0:2, 1:2], cmax2[0:2, 0:1],
                                                -0.125, -MARGIN,
                                                op0=ALU.mult, op1=ALU.add))
                        v.wait_ge(sct.h, b_sct + 2)
                        scb.inc(v.tensor_copy(b2bf[0:1, 0:2], psSelf[0:1, 100:102]))
                        v.wait_ge(sbc.h, b_sbc + 2)
                        v.tensor_copy(biascol[:, 0:2], psSelf[:, 64:66])
                        v.drain()

            with contextlib.ExitStack() as ps3:
                psS = env.psum(env.uname("psS"), [128, 4 * 512], F32, stack=ps3)
                psZ = env.psum(env.uname("psZ"), [128, 2 * 512], F32, stack=ps3)
                psT2 = env.psum(env.uname("psT2"), [128, 2048], BF16, stack=ps3)
                b_sS, b_sexp, b_smask = sS.n, sexp.n, smask.n
                b_sZ, b_sZn, b_sZt, b_sZe = sZ.n, sZn.n, sZt.n, sZe.n
                ntile = [4 * qc + 4 for qc in range(4)]
                cumt = [sum(ntile[:qc]) for qc in range(5)]
                with nc.Block() as blk:
                    @blk.tensor
                    def _(t):
                        for qc in range(4):
                            for ti in range(4 * qc + 4):
                                gidx = cumt[qc] + ti
                                buf = gidx % 2
                                if gidx >= 2:
                                    t.wait_ge(sexp.h, b_sexp + (gidx - 1) * 2)
                                t.matmul(psS[:, buf * 1024:buf * 1024 + 512],
                                         qkvp[0:64, 2048 + ti * 128:2048 + (ti + 1) * 128],
                                         qkvp[0:64, qc * 512:(qc + 1) * 512],
                                         start=True, stop=True, tile_position=(0, 0))
                                ins = t.matmul(
                                    psS[:, buf * 1024 + 512:buf * 1024 + 1024],
                                    qkvp[64:128, 2048 + ti * 128:2048 + (ti + 1) * 128],
                                    qkvp[64:128, qc * 512:(qc + 1) * 512],
                                    start=True, stop=True, tile_position=(64, 0))
                                sS.inc(ins)
                            t.wait_ge(smask.h, b_smask + qc * 8 + 8)
                            for h2 in range(2):
                                if qc > 0:
                                    t.wait_ge(sZn.h, b_sZn + (qc - 1) * 2 + h2 + 1)
                                for qsub in range(4):
                                    qt = 4 * qc + qsub
                                    for ti in range(qt + 1):
                                        ins = t.matmul(
                                            psZ[:, h2 * 512 + qsub * 65:
                                                h2 * 512 + qsub * 65 + 65],
                                            Pb[:, h2 * 8192 + ti * 512 + qsub * 128:
                                               h2 * 8192 + ti * 512 + (qsub + 1) * 128],
                                            vaug[:, h2 * 1040 + ti * 65:
                                                 h2 * 1040 + (ti + 1) * 65],
                                            start=(ti == 0), stop=(ti == qt))
                                    sZ.inc(ins)
                            for h2 in range(2):
                                zi = qc * 2 + h2
                                half = zi % 2
                                t.wait_ge(sZn.h, b_sZn + zi + 1)
                                if zi >= 2:
                                    t.wait_ge(sZe.h, b_sZe + zi - 1)
                                for qsub in range(4):
                                    ins = t.transpose(
                                        psT2[0:64, half * 1024 + qsub * 128:
                                             half * 1024 + (qsub + 1) * 128],
                                        ztmp[:, h2 * 256 + qsub * 64:
                                             h2 * 256 + (qsub + 1) * 64],
                                        env.IDENT[:])
                                    sZt.inc(ins, 1 if qsub == 3 else 0)

                    @blk.scalar
                    def _(sc):
                        for qc in range(4):
                            sc.wait_ge(sZ.h, b_sZ + qc * 8)
                            for ti in range(4 * qc + 4):
                                gidx = cumt[qc] + ti
                                buf = gidx % 2
                                sc.wait_ge(sS.h, b_sS + gidx + 1)
                                sexp.inc(sc.activation(
                                    Pb[:, 0 * 8192 + ti * 512:0 * 8192 + (ti + 1) * 512],
                                    psS[:, buf * 1024:buf * 1024 + 512],
                                    AF.Exp, bias=biascol[:, 0:1], scale=0.125))
                                sexp.inc(sc.activation(
                                    Pb[:, 1 * 8192 + ti * 512:1 * 8192 + (ti + 1) * 512],
                                    psS[:, buf * 1024 + 512:buf * 1024 + 1024],
                                    AF.Exp, bias=biascol[:, 1:2], scale=0.125))

                    @blk.vector
                    def _(v):
                        for qc in range(4):
                            for j in range(4):
                                ti = 4 * qc + j
                                gidx = cumt[qc] + ti
                                v.wait_ge(sexp.h, b_sexp + (gidx + 1) * 2)
                                for h2 in range(2):
                                    sl = Pb[:, h2 * 8192 + ti * 512 + j * 128:
                                            h2 * 8192 + ti * 512 + (j + 1) * 128]
                                    smask.inc(v.tensor_tensor(sl, sl, env.TRI[:],
                                                              op=ALU.mult))
                            for h2 in range(2):
                                v.wait_ge(sZ.h, b_sZ + qc * 8 + (h2 + 1) * 4)
                                v.reciprocal(
                                    linv4[:, h2 * 4:(h2 + 1) * 4],
                                    bass.AP(psZ[:, 0:1].tensor,
                                            psZ[:, 0:1].offset + h2 * 512 + 64,
                                            [list(psZ[:, 0:1].ap[0]), [65, 4]]))
                                v.drain()
                                sZn.inc(v.tensor_tensor(
                                    ap3(ztmp[:, h2 * 256:(h2 + 1) * 256], 4, 64),
                                    apcols3(psZ, h2 * 512, 65, 4, 64),
                                    apbcast(linv4, h2 * 4, 1, 4, 64),
                                    op=ALU.mult))
                            for h2 in range(2):
                                zi = qc * 2 + h2
                                half = zi % 2
                                v.wait_ge(sZt.h, b_sZt + zi + 1)
                                sZe.inc(v.tensor_copy(
                                    Zcat[h2 * 64:h2 * 64 + 64,
                                         p * 2048 + qc * 512:p * 2048 + (qc + 1) * 512],
                                    psT2[0:64, half * 1024:half * 1024 + 512]))

        with contextlib.ExitStack() as ls2:
            projw = env.sbuf(env.uname("projw"), [128, 8 * 1024], BF16, stack=ls2)
            po = env.psum(env.uname("po"), [128, 2 * 512], F32, stack=ls2)
            swp = env.sem("l0a_wp")
            sop = env.sem("l0a_op")
            sevp = env.sem("l0a_ev")
            b_swp, b_sop, b_sevp = swp.n, sop.n, sevp.n
            with nc.Block() as blk:
                @blk.gpsimd
                def _(g):
                    for jc in range(8):
                        swp.dma(g.dma_start(
                            projw[:, jc * 1024:(jc + 1) * 1024],
                            env.hbm["l0a_proj"][jc * 128:(jc + 1) * 128, :]))

                @blk.tensor
                def _(t):
                    for ti in range(NT):
                        for e in range(2):
                            oi = ti * 2 + e
                            buf = oi % 2
                            if oi >= 2:
                                t.wait_ge(sevp.h, b_sevp + oi - 1)
                            for jc in range(8):
                                t.wait_ge(swp.h, b_swp + (jc + 1) * 16)
                                ins = t.matmul(
                                    po[:, buf * 512:(buf + 1) * 512],
                                    Zcat[:, jc * 2048 + ti * 128:jc * 2048 + (ti + 1) * 128],
                                    projw[:, jc * 1024 + e * 512:jc * 1024 + (e + 1) * 512],
                                    start=(jc == 0), stop=(jc == 7))
                            sop.inc(ins)

                @blk.vector
                def _(v):
                    for ti in range(NT):
                        for e in range(2):
                            oi = ti * 2 + e
                            buf = oi % 2
                            v.wait_ge(sop.h, b_sop + oi + 1)
                            xs = tl(env.X, ti)[:, e * 512:(e + 1) * 512]
                            sevp.inc(v.scalar_tensor_tensor(
                                xs, po[:, buf * 512:(buf + 1) * 512], 1.0, xs,
                                op0=ALU.mult, op1=ALU.add))
    emit_ln_epilogue(env)



def emit_l0_ffn(env):
    nc = env.nc
    with contextlib.ExitStack() as lo:
        alpha = env.sbuf(env.uname("alpha"), [128, 1024], BF16, stack=lo)
        svwa = env.sbuf(env.uname("svwa"), [64, 1024], BF16, stack=lo)
        slotS = env.sbuf(env.uname("slotS"), [64, 1024], F32, stack=lo)
        slotb = env.sbuf(env.uname("slotb"), [64, 1024], BF16, stack=lo)

        def tr64(src, dst, nblk):
            with contextlib.ExitStack() as lb:
                pt = env.psum(env.uname("ptr"), [128, max(512, nblk * 64)], BF16,
                              stack=lb)
                with nc.Block() as blk:
                    @blk.tensor
                    def _(t):
                        for j in range(nblk):
                            t.transpose(pt[:, j * 64:(j + 1) * 64],
                                        src[0:64, j * 128:(j + 1) * 128],
                                        env.IDENT[0:64, 0:64])
                with nc.Block() as blk:
                    @blk.vector
                    def _(v):
                        v.tensor_copy(dst[:, 0:nblk * 64], pt[:, 0:nblk * 64])

        with contextlib.ExitStack() as la:
            XF = emit_shadow(env, la)
            w1m = env.sbuf(env.uname("w1m"), [128, 2 * 1024], BF16, stack=la)
            w1sm = env.sbuf(env.uname("w1sm"), [128, 4096], BF16, stack=la)
            hfm = env.sbuf(env.uname("hfm"), [128, 4 * 2048], BF16, stack=la)
            rl = env.sbuf(env.uname("rl"), [128, 1024], F32, stack=la)
            rcur = env.sbuf(env.uname("rcur"), [128, 1024], F32, stack=la)
            mk = env.sbuf(env.uname("mk"), [128, 1024], F32, stack=la)
            tkst = env.sbuf(env.uname("tkst"), [128, 64], F32, stack=la)
            w2b = env.sbuf(env.uname("w2b"), [128, 256], BF16, stack=la)
            smf = env.sbuf(env.uname("smf"), [128, 8], F32, stack=la)
            smb = env.sbuf(env.uname("smb"), [128, 8], BF16, stack=la)
            b1cc = env.sbuf(env.uname("b1cc"), [128, 4], F32, stack=la)
            b1e = env.sbuf(env.uname("b1e"), [128, 4], F32, stack=la)
            b2i = env.sbuf(env.uname("b2i"), [128, 64], F32, stack=la)
            b2rb = env.sbuf(env.uname("b2rb"), [1, 64], BF16, stack=la)
            taus = env.sbuf(env.uname("taus"), [1, 4], F32, stack=la)
            taub = env.sbuf(env.uname("taub"), [1, 1], BF16, stack=la)
            invt = env.sbuf(env.uname("invt"), [128, 1], F32, stack=la)

            sA = env.sem("l0f_A")
            with nc.Block() as blk:
                @blk.sync
                def _(e):
                    sA.dma(e.dma_start(slotS[:], env.hbm["l0f_slot"]))
                    sA.dma(e.dma_start(taus[0:1, 0:1],
                           env.hbm["l0f_tau"].rearrange("(a b) -> a b", a=1)))
                @blk.gpsimd
                def _(g):
                    sA.dma(g.dma_start(slotb[:], env.hbm["l0f_slot"]))
                    sA.dma(g.dma_start(b2rb[0:1, :],
                           env.hbm["l0f_b2"].rearrange("(a b) -> a b", a=1)))
                    for m in range(4):
                        sA.dma(g.dma_start(
                            ap3(w1sm[:, m * 1024:(m + 1) * 1024], 8, 128),
                            env.hbm["l0f_w1s"].rearrange("(kc p) m -> p kc m", p=128)
                            [:, :, m * 128:(m + 1) * 128]))
                    for m in range(4):
                        sA.dma(g.dma_start(w2b[:, m * 64:(m + 1) * 64],
                               env.hbm["l0f_w2"][m * 128:(m + 1) * 128, :]))
                    g.wait_ge(sA.h, sA.n)
            emit_colload(env, "l0f_b1", 4, b1cc)
            with nc.Block() as blk:
                @blk.scalar
                def _(sc):
                    sc.activation(taus[0:1, 1:2], taus[0:1, 0:1], AF.Abs,
                                  bias=0.0, scale=1.0)
            with nc.Block() as blk:
                @blk.vector
                def _(v):
                    v.tensor_scalar(taus[0:1, 1:2], taus[0:1, 1:2], 1.0, 0.1,
                                    op0=ALU.mult, op1=ALU.add)
                    v.drain()
                    v.reciprocal(taus[0:1, 2:3], taus[0:1, 1:2])
                    v.drain()
                    v.tensor_copy(taub[0:1, 0:1], taus[0:1, 2:3])
            with contextlib.ExitStack() as lb:
                psC = env.psum(env.uname("psC"), [128, 512], F32, stack=lb)
                with nc.Block() as blk:
                    @blk.tensor
                    def _(t):
                        for dc in range(8):
                            t.matmul(psC[:, dc:dc + 1],
                                     slotb[0:64, dc * 128:(dc + 1) * 128],
                                     env.ONES[0:64, 0:1], start=True, stop=True)
                        t.matmul(psC[:, 16:17], env.ONES[0:1, :], taub[0:1, 0:1],
                                 start=True, stop=True)
                        t.matmul(psC[:, 128:192], env.ONES[0:1, :], b2rb[0:1, :],
                                 start=True, stop=True)
                with nc.Block() as blk:
                    @blk.vector
                    def _(v):
                        v.tensor_scalar(smf[:], psC[:, 0:8], 1.0 / 64, None,
                                        op0=ALU.mult)
                        v.tensor_copy(invt[:, 0:1], psC[:, 16:17])
                        v.drain()
                        v.tensor_copy(smb[:], smf[:])
                        v.tensor_scalar(b2i[:], psC[:, 128:192], invt[:, 0:1],
                                        None, op0=ALU.mult)
            with contextlib.ExitStack() as lb:
                psB = env.psum(env.uname("psB"), [128, 512], F32, stack=lb)
                with nc.Block() as blk:
                    @blk.tensor
                    def _(t):
                        for m in range(4):
                            for k in range(8):
                                t.matmul(psB[:, m:m + 1],
                                         w1sm[:, m * 1024 + k * 128:
                                              m * 1024 + (k + 1) * 128],
                                         smb[:, k:k + 1],
                                         start=(k == 0), stop=(k == 7))
                with nc.Block() as blk:
                    @blk.vector
                    def _(v):
                        v.tensor_tensor(b1e[:], psB[:, 0:4], b1cc[:], op=ALU.add)

            sw1 = env.sem("l0f_w1")
            sh1 = env.sem("l0f_h1")
            she = env.sem("l0f_he")
            b_sw1, b_sh1, b_she = sw1.n, sh1.n, she.n
            w1r = env.hbm["l0f_w1x"].rearrange("(kc p) m -> p kc m", p=128)
            with contextlib.ExitStack() as lb:
                psH = env.psum(env.uname("psH"), [128, 2 * 512], F32, stack=lb)
                with nc.Block() as blk:
                    @blk.gpsimd
                    def _(g):
                        for m in range(4):
                            if m >= 2:
                                g.wait_ge(sh1.h, b_sh1 + (m - 1) * 4)
                            sw1.dma(g.dma_start(
                                ap3(w1m[:, (m % 2) * 1024:(m % 2 + 1) * 1024], 8, 128),
                                w1r[:, :, m * 128:(m + 1) * 128]))

                    @blk.tensor
                    def _(t):
                        for m in range(4):
                            t.wait_ge(sw1.h, b_sw1 + (m + 1) * 16)
                            for qc in range(4):
                                oi = m * 4 + qc
                                half = oi % 2
                                if oi >= 2:
                                    t.wait_ge(she.h, b_she + oi - 1)
                                for k in range(8):
                                    ins = t.matmul(
                                        psH[:, half * 512:(half + 1) * 512],
                                        w1m[:, (m % 2) * 1024 + k * 128:
                                            (m % 2) * 1024 + (k + 1) * 128],
                                        XF[:, k * 2048 + qc * 512:
                                           k * 2048 + (qc + 1) * 512],
                                        start=(k == 0), stop=(k == 7))
                                    if k == 7:
                                        sh1.inc(ins)

                    @blk.scalar
                    def _(sc):
                        for m in range(4):
                            for qc in range(4):
                                oi = m * 4 + qc
                                half = oi % 2
                                sc.wait_ge(sh1.h, b_sh1 + oi + 1)
                                she.inc(sc.activation(
                                    hfm[:, m * 2048 + qc * 512:
                                        m * 2048 + (qc + 1) * 512],
                                    psH[:, half * 512:(half + 1) * 512],
                                    AF.Gelu, bias=b1e[:, m:m + 1], scale=1.0))

            srl = env.sem("l0f_rl")
            sre = env.sem("l0f_re")
            b_srl, b_sre = srl.n, sre.n
            with contextlib.ExitStack() as lb:
                psRL = env.psum(env.uname("psRL"), [128, 2 * 512], F32, stack=lb)
                with nc.Block() as blk:
                    @blk.tensor
                    def _(t):
                        for g in range(4):
                            half = g % 2
                            if g >= 2:
                                t.wait_ge(sre.h, b_sre + g - 1)
                            for tj in range(4):
                                ti = g * 4 + tj
                                for m in range(4):
                                    ins = t.matmul(
                                        psRL[:, half * 512 + tj * 64:
                                             half * 512 + (tj + 1) * 64],
                                        hfm[:, m * 2048 + ti * 128:
                                            m * 2048 + (ti + 1) * 128],
                                        w2b[:, m * 64:(m + 1) * 64],
                                        start=(m == 0), stop=(m == 3))
                                    if m == 3 and tj == 3:
                                        srl.inc(ins)

                    @blk.vector
                    def _(v):
                        for g in range(4):
                            half = g % 2
                            v.wait_ge(srl.h, b_srl + g + 1)
                            sre.inc(v.scalar_tensor_tensor(
                                ap3(rl[:, g * 256:(g + 1) * 256], 4, 64),
                                ap3(psRL[:, half * 512:half * 512 + 256], 4, 64),
                                invt[:, 0:1],
                                bass.AP(b2i[:, 0:1].tensor, b2i[:, 0:1].offset,
                                        [list(b2i[:, 0:1].ap[0]), [0, 4], [1, 64]]),
                                op0=ALU.mult, op1=ALU.add))

            dbg(env, "rl", rl[:], [128, 1024])
            stk = env.sem("l0f_tk")
            b_stk = stk.n
            with nc.Block() as blk:
                @blk.vector
                def _(v):
                    v.tensor_copy(rcur[:], rl[:])
                    v.drain()
                    for i in range(8):
                        v.reduce_max(tkst[:, 0:16], ap3(rcur[:, :], 16, 64),
                                     axis=AX.X)
                        v.drain()
                        if i == 0:
                            v.tensor_scalar(tkst[:, 32:48], tkst[:, 0:16], -1.0,
                                            None, op0=ALU.mult)
                            v.drain()
                        if i < 7:
                            v.tensor_tensor(ap3(mk[:, :], 16, 64),
                                            ap3(rcur[:, :], 16, 64),
                                            apbcast(tkst, 0, 1, 16, 64),
                                            op=ALU.is_ge)
                            v.drain()
                            v.scalar_tensor_tensor(rcur[:], mk[:], -1e30, rcur[:],
                                                   op0=ALU.mult, op1=ALU.add)
                            v.drain()
                    stk.inc(v.nop())

                @blk.scalar
                def _(sc):
                    sc.wait_ge(stk.h, b_stk + 1)
                    for ti in range(16):
                        ins = sc.activation(rcur[:, ti * 64:(ti + 1) * 64],
                                            rl[:, ti * 64:(ti + 1) * 64],
                                            AF.Exp, bias=tkst[:, 32 + ti:33 + ti],
                                            scale=1.0)
                        stk.inc(ins, 1 if ti == 15 else 0)
            with nc.Block() as blk:
                @blk.vector
                def _(v):
                    v.tensor_tensor(ap3(mk[:, :], 16, 64), ap3(rl[:, :], 16, 64),
                                    apbcast(tkst, 0, 1, 16, 64), op=ALU.is_ge)
                    v.drain()
                    v.tensor_tensor(rcur[:], rcur[:], mk[:], op=ALU.mult)
                    v.drain()
                    v.reduce_sum(tkst[:, 16:32], ap3(rcur[:, :], 16, 64), axis=AX.X)
                    v.drain()
                    v.reciprocal(tkst[:, 48:64], tkst[:, 16:32])
                    v.drain()
                    v.tensor_tensor(ap3(alpha[:, :], 16, 64), ap3(rcur[:, :], 16, 64),
                                    apbcast(tkst, 48, 1, 16, 64), op=ALU.mult)

        dbg(env, "alpha", alpha[:], [128, 1024])
        with contextlib.ExitStack() as lb0:
            sib = env.sbuf(env.uname("sib"), [64, 1024], BF16, stack=lb0)
            sif = env.sbuf(env.uname("sif"), [128, 512], BF16, stack=lb0)
            snf = env.sbuf(env.uname("snf"), [128, 512], BF16, stack=lb0)
            gisc = env.sbuf(env.uname("gisc"), [64, 3072], F32, stack=lb0)
            ghsc = env.sbuf(env.uname("ghsc"), [64, 3072], F32, stack=lb0)
            bib = env.sbuf(env.uname("bib"), [64, 3072], BF16, stack=lb0)
            bhb = env.sbuf(env.uname("bhb"), [64, 3072], BF16, stack=lb0)
            rstg = env.sbuf(env.uname("rstg"), [1, 512], F32, stack=lb0)
            rstb = env.sbuf(env.uname("rstb"), [1, 512], BF16, stack=lb0)

            wstr = env.sbuf(env.uname("wstr"), [128, 2 * 512], BF16, stack=lb0)
            snew = env.sbuf(env.uname("snew"), [64, 1024], F32, stack=lb0)
            dnv = env.sbuf(env.uname("dnv"), [64, 4], F32, stack=lb0)
            qf = env.sbuf(env.uname("qf"), [128, 128], BF16, stack=lb0)
            kf = env.sbuf(env.uname("kf"), [128, 128], BF16, stack=lb0)
            vsb = env.sbuf(env.uname("vsb"), [64, 1024], BF16, stack=lb0)
            qsm = env.sbuf(env.uname("qsm"), [64, 512], BF16, stack=lb0)
            Pt = env.sbuf(env.uname("Pt"), [64, 64], BF16, stack=lb0)
            ysc = env.sbuf(env.uname("ysc"), [64, 1024], F32, stack=lb0)
            y2 = env.sbuf(env.uname("y2"), [64, 1024], F32, stack=lb0)
            ylb = env.sbuf(env.uname("ylb"), [64, 1024], BF16, stack=lb0)
            ylf = env.sbuf(env.uname("ylf"), [128, 512], BF16, stack=lb0)
            svb = env.sbuf(env.uname("svb"), [64, 1024], BF16, stack=lb0)
            svf = env.sbuf(env.uname("svf"), [128, 512], BF16, stack=lb0)
            lnst = env.sbuf(env.uname("lnst"), [64, 8], F32, stack=lb0)
            scs2 = env.sbuf(env.uname("scs2"), [64, 4], F32, stack=lb0)
            cb1 = env.sbuf(env.uname("cb1"), [1, 2], BF16, stack=lb0)
            bcol2 = env.sbuf(env.uname("bcol2"), [64, 1], F32, stack=lb0)
            bqb = env.sbuf(env.uname("bqb"), [64, 512], F32, stack=lb0)
            bvb = env.sbuf(env.uname("bvb"), [64, 1024], F32, stack=lb0)
            lgb = env.sbuf(env.uname("lgb"), [64, 1024], F32, stack=lb0)
            lbb = env.sbuf(env.uname("lbb"), [64, 1024], F32, stack=lb0)
            cvb = env.sbuf(env.uname("cvb"), [64, 1024], BF16, stack=lb0)
            rstg2 = env.sbuf(env.uname("rstg2"), [1, 512], F32, stack=lb0)
            bvpb = env.sbuf(env.uname("bvpb"), [128, 8], BF16, stack=lb0)

            sB = env.sem("l0f_B")
            emit_colload(env, "l0f_bvp", 8, bvpb)
            with contextlib.ExitStack() as lb:
                pb = env.psum(env.uname("pbc2"), [64, 512], F32, stack=lb)
                bcjobs = []
                for name, dst, w in (("bih", bib, 3072), ("bhh", bhb, 3072),
                                     ("bq", bqb, 256), ("bk", bqb, 256),
                                     ("bv", bvb, 1024), ("lng", lgb, 1024),
                                     ("lnb", lbb, 1024)):
                    for c0 in range(0, w, 512):
                        cw = min(512, w - c0)
                        do = (256 + c0 if name == "bk" else c0)
                        bcjobs.append((name, c0, dst, do, cw))
                sBC = env.sem("l0f_bc")
                for name, c0, dst, dst_off, cw in bcjobs:
                    b_sBC = sBC.n
                    with nc.Block() as blk:
                        @blk.sync
                        def _(e, name=name, c0=c0, cw=cw):
                            sBC.dma(e.dma_start(rstg[0:1, 0:cw],
                                    env.hbm["l0f_" + name]
                                    .rearrange("(a b) -> a b", a=1)[0:1, c0:c0 + cw]))
                        @blk.vector
                        def _(v, b_sBC=b_sBC, cw=cw):
                            v.wait_ge(sBC.h, b_sBC + 16)
                            v.tensor_copy(rstb[0:1, 0:cw], rstg[0:1, 0:cw])
                    with nc.Block() as blk:
                        @blk.tensor
                        def _(t, cw=cw):
                            t.matmul(pb[0:64, 0:cw], env.ONES[0:1, 0:64],
                                     rstb[0:1, 0:cw], start=True, stop=True)
                    with nc.Block() as blk:
                        @blk.vector
                        def _(v, dst=dst, dst_off=dst_off, cw=cw):
                            v.tensor_copy(dst[0:64, dst_off:dst_off + cw],
                                          pb[0:64, 0:cw])

            with contextlib.ExitStack() as lb:
                psSI = env.psum(env.uname("psSI"), [128, 1024], F32, stack=lb)
                psD = env.psum(env.uname("psD"), [128, 512], F32, stack=lb)
                with nc.Block() as blk:
                    @blk.tensor
                    def _(t):
                        for half in range(2):
                            for ti in range(16):
                                t.matmul(psSI[0:64, half * 512:(half + 1) * 512],
                                         alpha[:, ti * 64:(ti + 1) * 64],
                                         env.XT[:, ti * 1024 + half * 512:
                                                ti * 1024 + (half + 1) * 512],
                                         start=(ti == 0), stop=(ti == 15))
                        for ti in range(16):
                            t.matmul(psD[0:64, 0:1], alpha[:, ti * 64:(ti + 1) * 64],
                                     env.ONES[:, 0:1],
                                     start=(ti == 0), stop=(ti == 15))
                with nc.Block() as blk:
                    @blk.vector
                    def _(v):
                        v.tensor_scalar(dnv[0:64, 0:1], psD[0:64, 0:1], 1.0, 1e-8,
                                        op0=ALU.mult, op1=ALU.add)
                        v.drain()
                        v.reciprocal(dnv[0:64, 1:2], dnv[0:64, 0:1])
                        v.drain()
                        v.tensor_scalar(sib[:], psSI[0:64, :], dnv[0:64, 1:2],
                                        None, op0=ALU.mult)
            dbg(env, "sib", sib[:], [64, 1024])
            tr64(sib, sif, 8)
            tr64(slotb, snf, 8)

            for wname, lhs, dstg, bcb in (("l0f_whhT", snf, ghsc, bhb),
                                          ("l0f_wihT", sif, gisc, bib)):
                sg = env.sem("l0f_g")
                sge = env.sem("l0f_ge")
                swg = env.sem("l0f_wg")
                b_sg, b_sge, b_swg = sg.n, sge.n, swg.n
                with contextlib.ExitStack() as lb:
                    psG = env.psum(env.uname("psG"), [128, 1024], F32, stack=lb)
                    with nc.Block() as blk:
                        @blk.gpsimd
                        def _(g, wname=wname, b_sg=b_sg):
                            for n6 in range(6):
                                for k in range(8):
                                    idx = n6 * 8 + k
                                    if idx >= 2:
                                        g.wait_ge(sg.h, b_sg + idx - 1)
                                    swg.dma(g.dma_start(
                                        wstr[:, (idx % 2) * 512:(idx % 2 + 1) * 512],
                                        env.hbm[wname][k * 128:(k + 1) * 128,
                                                       n6 * 512:(n6 + 1) * 512]))

                        @blk.tensor
                        def _(t, lhs=lhs, b_swg=b_swg, b_sge=b_sge):
                            for n6 in range(6):
                                half = n6 % 2
                                if n6 >= 2:
                                    t.wait_ge(sge.h, b_sge + n6 - 1)
                                for k in range(8):
                                    idx = n6 * 8 + k
                                    t.wait_ge(swg.h, b_swg + (idx + 1) * 16)
                                    ins = t.matmul(
                                        psG[0:64, half * 512:(half + 1) * 512],
                                        lhs[:, k * 64:(k + 1) * 64],
                                        wstr[:, (idx % 2) * 512:(idx % 2 + 1) * 512],
                                        start=(k == 0), stop=(k == 7))
                                    sg.inc(ins)

                        @blk.vector
                        def _(v, dstg=dstg, bcb=bcb, b_sg=b_sg):
                            for n6 in range(6):
                                half = n6 % 2
                                v.wait_ge(sg.h, b_sg + (n6 + 1) * 8)
                                sge.inc(v.tensor_tensor(
                                    dstg[0:64, n6 * 512:(n6 + 1) * 512],
                                    psG[0:64, half * 512:(half + 1) * 512],
                                    bcb[0:64, n6 * 512:(n6 + 1) * 512], op=ALU.add))

            dbg(env, "gisc", gisc[:], [64, 3072])
            dbg(env, "ghsc", ghsc[:], [64, 3072])
            sg1 = env.sem("l0f_s1")
            sg2 = env.sem("l0f_s2")
            sg3 = env.sem("l0f_s3")
            sg4 = env.sem("l0f_s4")
            b1_, b2_, b3_, b4_ = sg1.n, sg2.n, sg3.n, sg4.n
            with nc.Block() as blk:
                @blk.vector
                def _(v):
                    sg1.inc(v.tensor_tensor(ghsc[0:64, 0:1024], gisc[0:64, 0:1024],
                                            ghsc[0:64, 0:1024], op=ALU.add))
                    sg1.inc(v.tensor_tensor(ghsc[0:64, 1024:2048],
                                            gisc[0:64, 1024:2048],
                                            ghsc[0:64, 1024:2048], op=ALU.add))
                    v.wait_ge(sg2.h, b2_ + 1)
                    v.tensor_tensor(ghsc[0:64, 2048:3072], ghsc[0:64, 2048:3072],
                                    ghsc[0:64, 0:1024], op=ALU.mult)
                    v.drain()
                    sg3.inc(v.tensor_tensor(ghsc[0:64, 2048:3072],
                                            ghsc[0:64, 2048:3072],
                                            gisc[0:64, 2048:3072], op=ALU.add))
                    v.wait_ge(sg4.h, b4_ + 2)
                    v.tensor_tensor(snew[:], slotS[:], ghsc[0:64, 2048:3072],
                                    op=ALU.subtract)
                    v.drain()
                    v.tensor_tensor(snew[:], snew[:], ghsc[0:64, 1024:2048],
                                    op=ALU.mult)
                    v.drain()
                    v.tensor_tensor(snew[:], snew[:], ghsc[0:64, 2048:3072],
                                    op=ALU.add)
                    v.drain()
                    v.tensor_copy(sib[:], snew[:])

                @blk.scalar
                def _(sc):
                    sc.wait_ge(sg1.h, b1_ + 1)
                    sg2.inc(sc.activation(ghsc[0:64, 0:1024], ghsc[0:64, 0:1024],
                                          AF.Sigmoid, bias=0.0, scale=1.0))
                    sc.wait_ge(sg1.h, b1_ + 2)
                    sg4.inc(sc.activation(ghsc[0:64, 1024:2048],
                                          ghsc[0:64, 1024:2048],
                                          AF.Sigmoid, bias=0.0, scale=1.0))
                    sc.wait_ge(sg3.h, b3_ + 1)
                    sg4.inc(sc.activation(ghsc[0:64, 2048:3072],
                                          ghsc[0:64, 2048:3072],
                                          AF.Tanh, bias=0.0, scale=1.0))
            dbg(env, "snew", snew[:], [64, 1024])
            tr64(sib, snf, 8)

            swq = env.sem("l0f_swq")
            sqv = env.sem("l0f_qv")
            b_swq, b_sqv = swq.n, sqv.n
            with contextlib.ExitStack() as lb:
                psA = env.psum(env.uname("psA"), [128, 1024], F32, stack=lb)
                psV = env.psum(env.uname("psV"), [128, 1024], F32, stack=lb)
                with nc.Block() as blk:
                    @blk.gpsimd
                    def _(g):
                        for k in range(8):
                            if k >= 2:
                                g.wait_ge(sqv.h, b_sqv + k - 1)
                            swq.dma(g.dma_start(
                                wstr[:, (k % 2) * 512:(k % 2) * 512 + 256],
                                env.hbm["l0f_wq"][k * 128:(k + 1) * 128, :]))
                            swq.dma(g.dma_start(
                                wstr[:, (k % 2) * 512 + 256:(k % 2) * 512 + 512],
                                env.hbm["l0f_wk"][k * 128:(k + 1) * 128, :]))
                    @blk.tensor
                    def _(t):
                        for k in range(8):
                            t.wait_ge(swq.h, b_swq + (k + 1) * 32)
                            t.matmul(psA[0:64, 0:256], snf[:, k * 64:(k + 1) * 64],
                                     wstr[:, (k % 2) * 512:(k % 2) * 512 + 256],
                                     start=(k == 0), stop=(k == 7))
                            ins = t.matmul(
                                psA[0:64, 512:768], snf[:, k * 64:(k + 1) * 64],
                                wstr[:, (k % 2) * 512 + 256:(k % 2) * 512 + 512],
                                start=(k == 0), stop=(k == 7))
                            sqv.inc(ins)
                with nc.Block() as blk:
                    @blk.vector
                    def _(v):
                        v.tensor_tensor(qsm[0:64, 0:256], psA[0:64, 0:256],
                                        bqb[0:64, 0:256], op=ALU.add)
                        v.tensor_tensor(qsm[0:64, 256:512], psA[0:64, 512:768],
                                        bqb[0:64, 256:512], op=ALU.add)
                for vh in range(2):
                    svp = env.sem("l0f_vp")
                    svc = env.sem("l0f_vc")
                    b_svp, b_svc = svp.n, svc.n
                    with nc.Block() as blk:
                        @blk.gpsimd
                        def _(g, vh=vh, b_svp=b_svp, b_svc=b_svc, svc=svc):
                            for k in range(8):
                                if k >= 2:
                                    g.wait_ge(svc.h, b_svc + k - 1)
                                svp.dma(g.dma_start(
                                    wstr[:, (k % 2) * 512:(k % 2 + 1) * 512],
                                    env.hbm["l0f_wv"][k * 128:(k + 1) * 128,
                                                      vh * 512:(vh + 1) * 512]))
                        @blk.tensor
                        def _(t, vh=vh, b_svp=b_svp, svc=svc):
                            for k in range(8):
                                t.wait_ge(svp.h, b_svp + (k + 1) * 16)
                                svc.inc(t.matmul(
                                    psV[0:64, vh * 512:(vh + 1) * 512],
                                    snf[:, k * 64:(k + 1) * 64],
                                    wstr[:, (k % 2) * 512:(k % 2 + 1) * 512],
                                    start=(k == 0), stop=(k == 7)))
                with nc.Block() as blk:
                    @blk.vector
                    def _(v):
                        v.tensor_tensor(ysc[:], psV[0:64, :], bvb[:], op=ALU.add)
                        v.drain()
                        v.tensor_copy(vsb[:], ysc[:])
            tr64(qsm[0:64, 0:256], qf, 2)
            tr64(qsm[0:64, 256:512], kf, 2)
            with contextlib.ExitStack() as lb:
                psSc = env.psum(env.uname("psSc"), [128, 512], F32, stack=lb)
                psC2 = env.psum(env.uname("psC2"), [128, 1024], F32, stack=lb)
                with nc.Block() as blk:
                    @blk.tensor
                    def _(t):
                        for c in range(2):
                            t.matmul(psSc[0:64, 0:64], kf[:, c * 64:(c + 1) * 64],
                                     qf[:, c * 64:(c + 1) * 64],
                                     start=(c == 0), stop=(c == 1))
                with nc.Block() as blk:
                    @blk.vector
                    def _(v):
                        v.reduce_max(scs2[0:64, 0:1], psSc[0:64, 0:64], axis=AX.X)
                with nc.Block() as blk:
                    @blk.tensor
                    def _(t):
                        t.transpose(psSc[0:1, 128:192], scs2[0:64, 0:1],
                                    env.IDF32[0:64, 0:64])
                with nc.Block() as blk:
                    @blk.vector
                    def _(v):
                        v.reduce_max(scs2[0:1, 1:2], psSc[0:1, 128:192], axis=AX.X)
                        v.drain()
                        v.tensor_scalar(scs2[0:1, 2:3], scs2[0:1, 1:2], -0.0625,
                                        None, op0=ALU.mult)
                        v.drain()
                        v.tensor_copy(cb1[0:1, 0:1], scs2[0:1, 2:3])
                with nc.Block() as blk:
                    @blk.tensor
                    def _(t):
                        t.matmul(psSc[0:64, 192:193], env.ONES[0:1, 0:64],
                                 cb1[0:1, 0:1], start=True, stop=True)
                with nc.Block() as blk:
                    @blk.vector
                    def _(v):
                        v.tensor_copy(bcol2[0:64, 0:1], psSc[0:64, 192:193])
                with nc.Block() as blk:
                    @blk.scalar
                    def _(sc):
                        sc.activation(Pt[0:64, 0:64], psSc[0:64, 0:64], AF.Exp,
                                      bias=bcol2[0:64, 0:1], scale=0.0625)
                with nc.Block() as blk:
                    @blk.tensor
                    def _(t):
                        t.matmul(psC2[0:64, 0:512], Pt[0:64, 0:64],
                                 vsb[0:64, 0:512], start=True, stop=True)
                        t.matmul(psC2[0:64, 512:1024], Pt[0:64, 0:64],
                                 vsb[0:64, 512:1024], start=True, stop=True)
                        t.matmul(psSc[0:64, 256:257], Pt[0:64, 0:64],
                                 env.ONES[0:64, 0:1], start=True, stop=True)
                with nc.Block() as blk:
                    @blk.vector
                    def _(v):
                        v.reciprocal(dnv[0:64, 2:3], psSc[0:64, 256:257])
                        v.drain()
                        v.tensor_scalar(y2[:], psC2[0:64, :], dnv[0:64, 2:3],
                                        None, op0=ALU.mult)
                        v.drain()
                        v.tensor_tensor(ysc[:], y2[:], snew[:], op=ALU.add)

            dbg(env, "vsb", vsb[:], [64, 1024])
            dbg(env, "ysc", ysc[:], [64, 1024])
            sg5 = env.sem("l0f_s5")
            b5_ = sg5.n
            with nc.Block() as blk:
                @blk.vector
                def _(v):
                    v.tensor_scalar(y2[:], ysc[:], 1.0, 0.0, op0=ALU.mult,
                                    op1=ALU.add, accum_out=lnst[0:64, 0:1])
                    v.scalar_tensor_tensor(y2[:], ysc[:], 1.0, ysc[:],
                                           op0=ALU.mult, op1=ALU.mult,
                                           accum_out=lnst[0:64, 1:2])
                    v.drain()
                    v.tensor_scalar(lnst[0:64, 2:3], lnst[0:64, 0:1], 1.0 / 1024,
                                    None, op0=ALU.mult)
                    v.tensor_scalar(lnst[0:64, 3:4], lnst[0:64, 1:2], 1.0 / 1024,
                                    1e-5, op0=ALU.mult, op1=ALU.add)
                    v.drain()
                    v.scalar_tensor_tensor(lnst[0:64, 4:5], lnst[0:64, 2:3], 1.0,
                                           lnst[0:64, 2:3], op0=ALU.mult,
                                           op1=ALU.mult)
                    v.drain()
                    sg5.inc(v.tensor_tensor(lnst[0:64, 5:6], lnst[0:64, 3:4],
                                            lnst[0:64, 4:5], op=ALU.subtract))
                @blk.scalar
                def _(sc):
                    sc.wait_ge(sg5.h, b5_ + 1)
                    sc.activation(lnst[0:64, 6:7], lnst[0:64, 5:6], AF.Sqrt,
                                  bias=0.0, scale=1.0)
            with nc.Block() as blk:
                @blk.vector
                def _(v):
                    v.reciprocal(lnst[0:64, 7:8], lnst[0:64, 6:7])
                    v.drain()
                    v.tensor_scalar(y2[:], ysc[:], lnst[0:64, 2:3],
                                    lnst[0:64, 7:8], op0=ALU.subtract,
                                    op1=ALU.mult)
                    v.drain()
                    v.tensor_tensor(y2[:], y2[:], lgb[0:64, :], op=ALU.mult)
                    v.drain()
                    v.tensor_tensor(y2[:], y2[:], lbb[0:64, :], op=ALU.add)
                    v.drain()
                    v.tensor_copy(ylb[:], y2[:])
            dbg(env, "y2", y2[:], [64, 1024])
            tr64(ylb, ylf, 8)

            with contextlib.ExitStack() as lb:
                psSV = env.psum(env.uname("psSV"), [128, 1024], F32, stack=lb)
                psCV = env.psum(env.uname("psCV"), [128, 1024], F32, stack=lb)
                for vh in range(2):
                    svp = env.sem("l0f_sv")
                    svc = env.sem("l0f_svc")
                    b_svp, b_svc = svp.n, svc.n
                    with nc.Block() as blk:
                        @blk.gpsimd
                        def _(g, vh=vh, b_svp=b_svp, b_svc=b_svc, svc=svc):
                            for k in range(8):
                                if k >= 2:
                                    g.wait_ge(svc.h, b_svc + k - 1)
                                svp.dma(g.dma_start(
                                    wstr[:, (k % 2) * 512:(k % 2 + 1) * 512],
                                    env.hbm["l0f_wvp"][k * 128:(k + 1) * 128,
                                                       vh * 512:(vh + 1) * 512]))
                        @blk.tensor
                        def _(t, vh=vh, b_svp=b_svp, svc=svc):
                            for k in range(8):
                                t.wait_ge(svp.h, b_svp + (k + 1) * 16)
                                svc.inc(t.matmul(
                                    psSV[0:64, vh * 512:(vh + 1) * 512],
                                    ylf[:, k * 64:(k + 1) * 64],
                                    wstr[:, (k % 2) * 512:(k % 2 + 1) * 512],
                                    start=(k == 0), stop=(k == 7)))
                with nc.Block() as blk:
                    @blk.vector
                    def _(v):
                        v.tensor_copy(svb[:], psSV[0:64, :])
                tr64(svb, svf, 8)
                for vh in range(2):
                    svp = env.sem("l0f_wo")
                    svc = env.sem("l0f_woc")
                    b_svp, b_svc = svp.n, svc.n
                    with nc.Block() as blk:
                        @blk.gpsimd
                        def _(g, vh=vh, b_svp=b_svp, b_svc=b_svc, svc=svc):
                            for k in range(8):
                                if k >= 2:
                                    g.wait_ge(svc.h, b_svc + k - 1)
                                svp.dma(g.dma_start(
                                    wstr[:, (k % 2) * 512:(k % 2 + 1) * 512],
                                    env.hbm["l0f_wo"][k * 128:(k + 1) * 128,
                                                      vh * 512:(vh + 1) * 512]))
                        @blk.tensor
                        def _(t, vh=vh, b_svp=b_svp, svc=svc):
                            for k in range(8):
                                t.wait_ge(svp.h, b_svp + (k + 1) * 16)
                                t.matmul(psSV[0:64, vh * 512:(vh + 1) * 512],
                                         svf[:, k * 64:(k + 1) * 64],
                                         wstr[:, (k % 2) * 512:(k % 2 + 1) * 512],
                                         start=(k == 0), stop=(k == 7))
                                svc.inc(t.matmul(
                                    psCV[0:1, vh * 512:(vh + 1) * 512],
                                    bvpb[:, k:k + 1],
                                    wstr[:, (k % 2) * 512:(k % 2 + 1) * 512],
                                    start=(k == 0), stop=(k == 7)))
                with contextlib.ExitStack() as lb2:
                    pb3 = env.psum(env.uname("pbc3"), [64, 512], F32, stack=lb2)
                    scv = env.sem("l0f_cv")
                    for c0 in (0, 512):
                        b_scv = scv.n
                        with nc.Block() as blk:
                            @blk.sync
                            def _(e, c0=c0, b_scv=b_scv):
                                scv.dma(e.dma_start(rstg2[0:1, 0:512],
                                        env.hbm["l0f_bo"]
                                        .rearrange("(a b) -> a b", a=1)
                                        [0:1, c0:c0 + 512]))
                            @blk.vector
                            def _(v, c0=c0, b_scv=b_scv):
                                v.wait_ge(scv.h, b_scv + 16)
                                v.tensor_tensor(rstg[0:1, 0:512],
                                                psCV[0:1, c0:c0 + 512],
                                                rstg2[0:1, 0:512], op=ALU.add)
                                v.drain()
                                v.tensor_copy(rstb[0:1, 0:512], rstg[0:1, 0:512])
                        with nc.Block() as blk:
                            @blk.tensor
                            def _(t):
                                t.matmul(pb3[0:64, 0:512], env.ONES[0:1, 0:64],
                                         rstb[0:1, 0:512], start=True, stop=True)
                        with nc.Block() as blk:
                            @blk.vector
                            def _(v, c0=c0):
                                v.tensor_copy(cvb[0:64, c0:c0 + 512],
                                              pb3[0:64, 0:512])
                with nc.Block() as blk:
                    @blk.vector
                    def _(v):
                        v.tensor_tensor(svwa[:], psSV[0:64, :], cvb[0:64, :],
                                        op=ALU.add)

            dbg(env, "svwa", svwa[:], [64, 1024])
        with contextlib.ExitStack() as lc:
            af = env.sbuf(env.uname("af"), [64, 2048], BF16, stack=lc)
            satr = env.sem("l0f_at")
            sate = env.sem("l0f_ae")
            b_satr, b_sate = satr.n, sate.n
            with contextlib.ExitStack() as lb:
                pt2 = env.psum(env.uname("ptr2"), [128, 2048], BF16, stack=lb)
                with nc.Block() as blk:
                    @blk.tensor
                    def _(t):
                        for g in range(4):
                            half = g % 2
                            if g >= 2:
                                t.wait_ge(sate.h, b_sate + g - 1)
                            for j in range(4):
                                ti = g * 4 + j
                                ins = t.transpose(
                                    pt2[0:64, half * 1024 + j * 128:
                                        half * 1024 + (j + 1) * 128],
                                    alpha[:, ti * 64:(ti + 1) * 64],
                                    env.IDENT[:])
                                satr.inc(ins, 1 if j == 3 else 0)
                    @blk.vector
                    def _(v):
                        for g in range(4):
                            half = g % 2
                            v.wait_ge(satr.h, b_satr + g + 1)
                            sate.inc(v.tensor_copy(
                                af[0:64, g * 512:(g + 1) * 512],
                                pt2[0:64, half * 1024:half * 1024 + 512]))
            so = env.sem("l0f_o")
            sev = env.sem("l0f_evo")
            b_so, b_sev = so.n, sev.n
            with contextlib.ExitStack() as lb:
                psO = env.psum(env.uname("psO"), [128, 2 * 512], F32, stack=lb)
                with nc.Block() as blk:
                    @blk.tensor
                    def _(t):
                        for ti in range(16):
                            for e2 in range(2):
                                oi = ti * 2 + e2
                                buf = oi % 2
                                if oi >= 2:
                                    t.wait_ge(sev.h, b_sev + oi - 1)
                                so.inc(t.matmul(
                                    psO[:, buf * 512:(buf + 1) * 512],
                                    af[0:64, ti * 128:(ti + 1) * 128],
                                    svwa[0:64, e2 * 512:(e2 + 1) * 512],
                                    start=True, stop=True))
                    @blk.vector
                    def _(v):
                        for ti in range(16):
                            for e2 in range(2):
                                oi = ti * 2 + e2
                                buf = oi % 2
                                v.wait_ge(so.h, b_so + oi + 1)
                                xs = tl(env.X, ti)[:, e2 * 512:(e2 + 1) * 512]
                                sev.inc(v.scalar_tensor_tensor(
                                    xs, psO[:, buf * 512:(buf + 1) * 512], 1.0,
                                    xs, op0=ALU.mult, op1=ALU.add))
    emit_ln_epilogue(env)


def emit_l1_attn(env):
    nc = env.nc
    Pb = env.BIGPOOL
    zsc = nc.dram_tensor(env.uname("zsc"), [2048, 2048], BF16,
                         kind="Internal").ap()
    szd = env.sem("l1a_zd")
    with contextlib.ExitStack() as ls:
        XF = emit_shadow(env, ls)
        wks = env.sbuf(env.uname("wks"), [128, 2 * 384], BF16, stack=ls)
        qkvh = env.sbuf(env.uname("qkvh"), [128, 3 * 2048], BF16, stack=ls)
        qrot = env.sbuf(env.uname("qrot"), [128, 2048], BF16, stack=ls)
        krot = env.sbuf(env.uname("krot"), [128, 2048], BF16, stack=ls)
        rscr = env.sbuf(env.uname("rscr"), [128, 2048], BF16, stack=ls)
        qsw = env.sbuf(env.uname("qsw"), [128, 2048], BF16, stack=ls)
        ksw = env.sbuf(env.uname("ksw"), [128, 2048], BF16, stack=ls)
        cosb = env.sbuf(env.uname("cosb"), [128, 2048], BF16, stack=ls)
        sinb = env.sbuf(env.uname("sinb"), [128, 2048], BF16, stack=ls)
        vaug1 = env.sbuf(env.uname("vaug1"), [128, 16 * 129], BF16, stack=ls)
        vaug2 = env.sbuf(env.uname("vaug2"), [128, 16 * 129], BF16, stack=ls)
        cselfS = env.sbuf(env.uname("cselfS"), [128, 2], F32, stack=ls)
        cmax2 = env.sbuf(env.uname("cmax2"), [2, 4], F32, stack=ls)
        b2bf = env.sbuf(env.uname("b2bf"), [1, 2], BF16, stack=ls)
        biascol = env.sbuf(env.uname("biascol"), [128, 2], F32, stack=ls)
        lam4 = env.sbuf(env.uname("lam4"), [64, 4], F32, stack=ls)
        lprodb = env.sbuf(env.uname("lprodb"), [64, 2], BF16, stack=ls)
        lrow = env.sbuf(env.uname("lrow"), [1, 4], F32, stack=ls)
        nlbf = env.sbuf(env.uname("nlbf"), [1, 1], BF16, stack=ls)
        nlcol = env.sbuf(env.uname("nlcol"), [128, 1], F32, stack=ls)
        linv = env.sbuf(env.uname("linv"), [128, 4], F32, stack=ls)
        ascr = env.sbuf(env.uname("ascr"), [128, 256], F32, stack=ls)
        bscr = env.sbuf(env.uname("bscr"), [128, 256], F32, stack=ls)
        zcomb = env.sbuf(env.uname("zcomb"), [128, 256], BF16, stack=ls)
        zfm = env.sbuf(env.uname("zfm"), [128, 512], BF16, stack=ls)

        sc0 = env.sem("l1a_c")
        spre = env.sem("l1a_p")
        b_sc0 = sc0.n
        with contextlib.ExitStack() as ps0:
            psL = env.psum(env.uname("psL"), [128, 128], F32, stack=ps0)
            with nc.Block() as blk:
                @blk.gpsimd
                def _(g):
                    sc0.dma(g.dma_start(cosb[:], env.hbm["c_cos"]))
                    sc0.dma(g.dma_start(sinb[:], env.hbm["c_sin"]))
                    g.wait_ge(sc0.h, sc0.n)
                @blk.sync
                def _(e):
                    sc0.dma(e.dma_start(lam4[:], env.hbm["l1a_lam"]))
                @blk.vector
                def _(v):
                    v.wait_ge(sc0.h, b_sc0 + 48)
                    v.memset(apcols(vaug1, 128, 129, 16), 1.0)
                    v.memset(apcols(vaug2, 128, 129, 16), 1.0)
                    ins = v.tensor_tensor(
                        lprodb[:, 0:2],
                        bass.AP(lam4[:, 0:1].tensor, lam4[:, 0:1].offset,
                                [list(lam4[:, 0:1].ap[0]), [2, 2]]),
                        bass.AP(lam4[:, 0:1].tensor, lam4[:, 0:1].offset + 1,
                                [list(lam4[:, 0:1].ap[0]), [2, 2]]),
                        op=ALU.mult)
                    spre.inc(ins)
            with nc.Block() as blk:
                @blk.tensor
                def _(t):
                    t.matmul(psL[0:2, 0:1], lprodb[0:64, 0:2],
                             env.ONES[0:64, 0:1], start=True, stop=True)
            with nc.Block() as blk:
                @blk.scalar
                def _(sc):
                    sc.activation(cmax2[0:2, 2:3], psL[0:2, 0:1], AF.Exp,
                                  bias=0.0, scale=1.0)
            with nc.Block() as blk:
                @blk.tensor
                def _(t):
                    t.transpose(psL[0:1, 4:6], cmax2[0:2, 2:3], env.IDF32[0:2, 0:2])
            with nc.Block() as blk:
                @blk.vector
                def _(v):
                    v.tensor_copy(lrow[0:1, 0:2], psL[0:1, 4:6])
                    v.drain()
                    v.tensor_tensor(lrow[0:1, 2:3], lrow[0:1, 0:1], lrow[0:1, 1:2],
                                    op=ALU.subtract)
                    v.drain()
                    v.tensor_scalar(lrow[0:1, 3:4], lrow[0:1, 2:3], -1.0, -0.2,
                                    op0=ALU.mult, op1=ALU.add)
                    v.drain()
                    v.tensor_copy(nlbf[0:1, 0:1], lrow[0:1, 3:4])
            with nc.Block() as blk:
                @blk.tensor
                def _(t):
                    t.matmul(psL[:, 8:9], env.ONES[0:1, :], nlbf[0:1, 0:1],
                             start=True, stop=True)
            with nc.Block() as blk:
                @blk.vector
                def _(v):
                    v.tensor_copy(nlcol[:, 0:1], psL[:, 8:9])
        dbg(env, "nlcol", nlcol[:], [128, 1])

        wr = env.hbm["l1a_wqkv"].rearrange("(kc p) m -> p kc m", p=128)
        swk = env.sem("l1a_wk")
        sk = env.sem("l1a_kc")
        sq = env.sem("l1a_q")
        sqe = env.sem("l1a_qe")
        srot = env.sem("l1a_r")
        stv = env.sem("l1a_tv")
        sve = env.sem("l1a_ve")
        skq = env.sem("l1a_kq")
        ssf = env.sem("l1a_sf")
        scs = env.sem("l1a_cs")
        sct = env.sem("l1a_ct")
        scb = env.sem("l1a_cb")
        sbc = env.sem("l1a_bc")
        sS = env.sem("l1a_S")
        sexp = env.sem("l1a_e")
        sZ = env.sem("l1a_Z")
        sZn = env.sem("l1a_Zn")
        sZt = env.sem("l1a_Zt")
        sZe = env.sem("l1a_Ze")

        for h in range(16):
            with contextlib.ExitStack() as ps1:
                psQ = env.psum(env.uname("psQ"), [128, 6 * 512], F32, stack=ps1)
                b_swk, b_sk, b_sq, b_sqe = swk.n, sk.n, sq.n, sqe.n
                with nc.Block() as blk:
                    @blk.gpsimd
                    def _(g):
                        for half in range(2):
                            for k in range(8):
                                idx = half * 8 + k
                                if idx >= 2:
                                    g.wait_ge(sk.h, b_sk + idx - 1)
                                swk.dma(g.dma_start(
                                    ap3(wks[:, (idx % 2) * 384:(idx % 2 + 1) * 384],
                                        1, 384),
                                    wr[:, k:k + 1, 384 * h:384 * (h + 1)]))

                    @blk.tensor
                    def _(t):
                        for half in range(2):
                            for k in range(8):
                                idx = half * 8 + k
                                t.wait_ge(swk.h, b_swk + (idx + 1) * 16)
                                if half == 1 and k == 0:
                                    t.wait_ge(sqe.h, b_sqe + 6)
                                for m in range(3):
                                    for q2 in range(2):
                                        gi = m * 2 + q2
                                        ins = t.matmul(
                                            psQ[:, gi * 512:(gi + 1) * 512],
                                            wks[:, (idx % 2) * 384 + m * 128:
                                                (idx % 2) * 384 + (m + 1) * 128],
                                            XF[:, k * 2048 + half * 1024 +
                                                   q2 * 512:k * 2048 + half * 1024 +
                                                   (q2 + 1) * 512],
                                            start=(k == 0), stop=(k == 7))
                                        if k == 7:
                                            sq.inc(ins)
                                            if m == 2 and q2 == 1:
                                                sk.inc(t.nop())
                                        elif m == 2 and q2 == 1:
                                            sk.inc(ins)

                    @blk.vector
                    def _(v):
                        for half in range(2):
                            for m in range(3):
                                for q2 in range(2):
                                    gi = m * 2 + q2
                                    v.wait_ge(sq.h, b_sq + half * 6 + gi + 1)
                                    sqe.inc(v.tensor_copy(
                                        qkvh[:, m * 2048 + half * 1024 + q2 * 512:
                                             m * 2048 + half * 1024 + (q2 + 1) * 512],
                                        psQ[:, gi * 512:(gi + 1) * 512]))

            if h == 0:
                dbg(env, "xf0", XF[:, 0:2048], [128, 2048])
                dbg(env, "qkvh0", qkvh[:, 0:2048], [128, 2048])
            with contextlib.ExitStack() as ps2:
                psT = env.psum(env.uname("psT"), [128, 2048], BF16, stack=ps2)
                psSelf = env.psum(env.uname("psSelf"), [128, 128], F32, stack=ps2)
                b_stv, b_sve, b_skq, b_ssf = stv.n, sve.n, skq.n, ssf.n
                b_scs, b_sct, b_scb, b_sbc = scs.n, sct.n, scb.n, sbc.n
                b_srot = srot.n
                ssw = env.sem("l1a_sw")
                b_ssw = ssw.n
                with nc.Block() as blk:
                    @blk.sync
                    def _(e):
                        for dst, src0 in ((qsw, 0), (ksw, 2048)):
                            ssw.dma(e.dma_start(dst[0:64, :],
                                                qkvh[64:128, src0:src0 + 2048]))
                            ssw.dma(e.dma_start(dst[64:128, :],
                                                qkvh[0:64, src0:src0 + 2048]))

                    @blk.vector
                    def _(v):
                        v.wait_ge(ssw.h, b_ssw + 64)
                        for dst, sw, src0 in ((qrot, qsw, 0), (krot, ksw, 2048)):
                            v.tensor_tensor(rscr[:, :], sw[:, :], sinb[:, :],
                                            op=ALU.mult)
                            v.tensor_tensor(dst[:, :], qkvh[:, src0:src0 + 2048],
                                            cosb[:, :], op=ALU.mult)
                            v.drain()
                            srot.inc(v.tensor_tensor(dst[:, :], dst[:, :], rscr[:, :],
                                                     op=ALU.add))
                        v.drain()
                        skq.inc(v.tensor_tensor(rscr[0:64, :], krot[0:64, :],
                                                qrot[0:64, :], op=ALU.mult))
                        skq.inc(v.tensor_tensor(rscr[64:128, :], krot[64:128, :],
                                                qrot[64:128, :], op=ALU.mult))
                        for g in range(4):
                            half = g % 2
                            v.wait_ge(stv.h, b_stv + g + 1)
                            sve.inc(v.tensor_copy(
                                apcols3(vaug1, g * 4 * 129, 129, 4, 128),
                                ap3(psT[:, half * 1024:half * 1024 + 512], 4, 128)))
                            sve.inc(v.tensor_scalar(
                                apcols3(vaug2, g * 4 * 129, 129, 4, 128),
                                ap3(psT[:, half * 1024:half * 1024 + 512], 4, 128),
                                nlcol[:, 0:1], None, op0=ALU.mult))
                        v.wait_ge(ssf.h, b_ssf + 2)
                        for br in range(2):
                            scs.inc(v.reduce_max(cselfS[:, br:br + 1],
                                                 psSelf[:, br * 16:(br + 1) * 16],
                                                 axis=AX.X))
                        v.wait_ge(sct.h, b_sct + 1)
                        v.reduce_max(cmax2[0:2, 0:1], psSelf[0:2, 0:128], axis=AX.X)
                        v.drain()
                        scb.inc(v.tensor_scalar(cmax2[0:2, 1:2], cmax2[0:2, 0:1],
                                                -0.125, -MARGIN,
                                                op0=ALU.mult, op1=ALU.add))
                        v.wait_ge(sct.h, b_sct + 2)
                        scb.inc(v.tensor_copy(b2bf[0:1, 0:2], psSelf[0:1, 100:102]))
                        v.wait_ge(sbc.h, b_sbc + 2)
                        v.tensor_copy(biascol[:, 0:2], psSelf[:, 64:66])
                        v.drain()

                    @blk.tensor
                    def _(t):
                        for g in range(4):
                            half = g % 2
                            if g >= 2:
                                t.wait_ge(sve.h, b_sve + 2 * (g - 1))
                            for j in range(4):
                                ti = g * 4 + j
                                ins = t.transpose(
                                    psT[:, half * 1024 + j * 128:
                                        half * 1024 + (j + 1) * 128],
                                    qkvh[:, 4096 + ti * 128:4096 + (ti + 1) * 128],
                                    env.IDENT[:])
                                stv.inc(ins, 1 if j == 3 else 0)
                        t.wait_ge(skq.h, b_skq + 2)
                        for br in range(2):
                            for ti in range(16):
                                ins = t.matmul(
                                    psSelf[:, br * 16 + ti:br * 16 + ti + 1],
                                    rscr[64 * br:64 * br + 64, ti * 128:(ti + 1) * 128],
                                    env.ONES[64 * br:64 * br + 64, 0:1],
                                    start=True, stop=True)
                                ssf.inc(ins, 1 if ti == 15 else 0)
                        t.wait_ge(scs.h, b_scs + 2)
                        sct.inc(t.transpose(psSelf[0:2, 0:128], cselfS[:, 0:2],
                                            env.IDF32[:]))
                        t.wait_ge(scb.h, b_scb + 1)
                        sct.inc(t.transpose(psSelf[0:1, 100:102], cmax2[0:2, 1:2],
                                            env.IDF32[0:2, 0:2]))
                        t.wait_ge(scb.h, b_scb + 2)
                        for br in range(2):
                            sbc.inc(t.matmul(psSelf[:, 64 + br:65 + br],
                                             env.ONES[0:1, :],
                                             b2bf[0:1, br:br + 1],
                                             start=True, stop=True))

            if h == 0:
                dbg(env, "qrot0", qrot[:], [128, 2048])
                dbg(env, "biasc0", biascol[:], [128, 2])
                dbg(env, "vaug10", vaug1[:, 0:1024], [128, 1024])
            with contextlib.ExitStack() as ps3:
                psS = env.psum(env.uname("psS"), [128, 4 * 512], F32, stack=ps3)
                psZ1 = env.psum(env.uname("psZ1"), [128, 512], F32, stack=ps3)
                psZ2 = env.psum(env.uname("psZ2"), [128, 512], F32, stack=ps3)
                psT2 = env.psum(env.uname("psT2"), [128, 2048], BF16, stack=ps3)
                b_sS, b_sexp, b_sZ = sS.n, sexp.n, sZ.n
                b_sZn, b_sZt, b_sZe, b_szd = sZn.n, sZt.n, sZe.n, szd.n
                with nc.Block() as blk:
                    @blk.tensor
                    def _(t):
                        for qc in range(4):
                            for ti in range(16):
                                gidx = qc * 16 + ti
                                buf = gidx % 2
                                if gidx >= 2:
                                    t.wait_ge(sexp.h, b_sexp + (gidx - 1) * 2)
                                t.matmul(psS[:, buf * 1024:buf * 1024 + 512],
                                         krot[0:64, ti * 128:(ti + 1) * 128],
                                         qrot[0:64, qc * 512:(qc + 1) * 512],
                                         start=True, stop=True, tile_position=(0, 0))
                                ins = t.matmul(
                                    psS[:, buf * 1024 + 512:buf * 1024 + 1024],
                                    krot[64:128, ti * 128:(ti + 1) * 128],
                                    qrot[64:128, qc * 512:(qc + 1) * 512],
                                    start=True, stop=True, tile_position=(64, 0))
                                sS.inc(ins)
                            for h2 in range(2):
                                zi = qc * 2 + h2
                                if zi >= 1:
                                    t.wait_ge(sZn.h, b_sZn + zi * 3)
                                t.wait_ge(sexp.h, b_sexp + (qc * 16 + 16) * 2)
                                for qs in range(2):
                                    qsub = h2 * 2 + qs
                                    for br, (pbase, va) in enumerate(
                                            ((0, vaug1), (8192, vaug2))):
                                        for ti in range(16):
                                            ins = t.matmul(
                                                (psZ1 if br == 0 else psZ2)
                                                [:, qs * 129:qs * 129 + 129],
                                                Pb[:, pbase + ti * 512 + qsub * 128:
                                                   pbase + ti * 512 + (qsub + 1) * 128],
                                                va[:, ti * 129:(ti + 1) * 129],
                                                start=(ti == 0), stop=(ti == 15))
                                        sZ.inc(ins)
                            for h2 in range(2):
                                zi = qc * 2 + h2
                                half = zi % 2
                                t.wait_ge(sZn.h, b_sZn + zi * 3 + 3)
                                if zi >= 2:
                                    t.wait_ge(sZe.h, b_sZe + zi - 1)
                                for qs in range(2):
                                    ins = t.transpose(
                                        psT2[:, half * 1024 + qs * 128:
                                             half * 1024 + (qs + 1) * 128],
                                        zcomb[:, qs * 128:(qs + 1) * 128],
                                        env.IDENT[:])
                                    sZt.inc(ins, 1 if qs == 1 else 0)

                    @blk.scalar
                    def _(sc):
                        for qc in range(4):
                            if qc > 0:
                                sc.wait_ge(sZ.h, b_sZ + qc * 8)
                            for ti in range(16):
                                gidx = qc * 16 + ti
                                buf = gidx % 2
                                sc.wait_ge(sS.h, b_sS + gidx + 1)
                                sexp.inc(sc.activation(
                                    Pb[:, ti * 512:(ti + 1) * 512],
                                    psS[:, buf * 1024:buf * 1024 + 512],
                                    AF.Exp, bias=biascol[:, 0:1], scale=0.125))
                                sexp.inc(sc.activation(
                                    Pb[:, 8192 + ti * 512:8192 + (ti + 1) * 512],
                                    psS[:, buf * 1024 + 512:buf * 1024 + 1024],
                                    AF.Exp, bias=biascol[:, 1:2], scale=0.125))

                    @blk.vector
                    def _(v):
                        for qc in range(4):
                            if qc > 0:
                                v.wait_ge(szd.h, b_szd + qc * 16)
                            for h2 in range(2):
                                zi = qc * 2 + h2
                                v.wait_ge(sZ.h, b_sZ + zi * 4 + 4)
                                v.reciprocal(linv[:, 0:2],
                                             apcols(psZ1, 128, 129, 2))
                                v.reciprocal(linv[:, 2:4],
                                             apcols(psZ2, 128, 129, 2))
                                v.drain()
                                v.tensor_tensor(ap3(ascr[:, 0:256], 2, 128),
                                                apcols3(psZ1, 0, 129, 2, 128),
                                                apbcast(linv, 0, 1, 2, 128),
                                                op=ALU.mult)
                                sZn.inc(v.tensor_tensor(
                                    ap3(bscr[:, 0:256], 2, 128),
                                    apcols3(psZ2, 0, 129, 2, 128),
                                    apbcast(linv, 2, 1, 2, 128),
                                    op=ALU.mult), 2)
                                v.drain()
                                sZn.inc(v.tensor_tensor(zcomb[:, 0:256],
                                                        ascr[:, 0:256],
                                                        bscr[:, 0:256], op=ALU.add))
                                zi2 = zi
                                half = zi2 % 2
                                v.wait_ge(sZt.h, b_sZt + zi2 + 1)
                                sZe.inc(v.tensor_copy(
                                    zfm[:, h2 * 256:(h2 + 1) * 256],
                                    psT2[:, half * 1024:half * 1024 + 256]))

                    @blk.sync
                    def _(e):
                        for qc in range(4):
                            e.wait_ge(sZe.h, b_sZe + qc * 2 + 2)
                            szd.dma(e.dma_start(
                                zsc[h * 128:(h + 1) * 128, qc * 512:(qc + 1) * 512],
                                zfm[:]))

    with contextlib.ExitStack() as ls2:
        pjw = env.sbuf(env.uname("pjw"), [128, 16 * 1024], BF16, stack=ls2)
        zw = env.sbuf(env.uname("zw"), [128, 2 * 2048], BF16, stack=ls2)
        po = env.psum(env.uname("po"), [128, 2 * 512], F32, stack=ls2)
        swp = env.sem("l1a_wp")
        szw = env.sem("l1a_zw")
        sop = env.sem("l1a_op")
        sevp = env.sem("l1a_ev")
        b_swp, b_szw, b_sop, b_sevp = swp.n, szw.n, sop.n, sevp.n
        zr = zsc.rearrange("(jc p) c -> p jc c", p=128)
        with nc.Block() as blk:
            @blk.gpsimd
            def _(g):
                for jc in range(16):
                    swp.dma(g.dma_start(
                        pjw[:, jc * 1024:(jc + 1) * 1024],
                        env.hbm["l1a_projT"][jc * 128:(jc + 1) * 128, :]))

            @blk.sync
            def _(e):
                e.wait_ge(szd.h, szd.n)
                for ti in range(NT):
                    if ti >= 2:
                        e.wait_ge(sop.h, b_sop + (ti - 1) * 2)
                    szw.dma(e.dma_start(
                        ap3(zw[:, (ti % 2) * 2048:(ti % 2 + 1) * 2048], 16, 128),
                        zr[:, :, ti * 128:(ti + 1) * 128]))

            @blk.tensor
            def _(t):
                t.wait_ge(swp.h, b_swp + 256)
                for ti in range(NT):
                    t.wait_ge(szw.h, b_szw + (ti + 1) * 16)
                    for e2 in range(2):
                        oi = ti * 2 + e2
                        buf = oi % 2
                        if oi >= 2:
                            t.wait_ge(sevp.h, b_sevp + oi - 1)
                        for jc in range(16):
                            ins = t.matmul(
                                po[:, buf * 512:(buf + 1) * 512],
                                zw[:, (ti % 2) * 2048 + jc * 128:
                                   (ti % 2) * 2048 + (jc + 1) * 128],
                                pjw[:, jc * 1024 + e2 * 512:jc * 1024 + (e2 + 1) * 512],
                                start=(jc == 0), stop=(jc == 15))
                        sop.inc(ins)

            @blk.vector
            def _(v):
                for ti in range(NT):
                    for e2 in range(2):
                        oi = ti * 2 + e2
                        buf = oi % 2
                        v.wait_ge(sop.h, b_sop + oi + 1)
                        xs = tl(env.X, ti)[:, e2 * 512:(e2 + 1) * 512]
                        sevp.inc(v.scalar_tensor_tensor(
                            xs, po[:, buf * 512:(buf + 1) * 512], 1.0, xs,
                            op0=ALU.mult, op1=ALU.add))
    emit_ln_epilogue(env)



BLOCK_EMITTERS = {}


def build_program(block_names, nseq=2):
    nc = bass.Bass("TRN2", target_bir_lowering=False)
    env = Env(nc)
    with env.es:
        declare_hbm(env, x_batches=nseq)
        alloc_core(env)
        emit_constants(env)
        for seq in range(nseq):
            emit_load_x(env, seq)
            for name in block_names:
                BLOCK_EMITTERS[name](env)
            emit_store_x(env, seq)
    return nc


BLOCK_EMITTERS["l1f"] = emit_l1_ffn
BLOCK_EMITTERS["l0a"] = emit_l0_attn
BLOCK_EMITTERS["l1a"] = emit_l1_attn
BLOCK_EMITTERS["l0f"] = emit_l0_ffn


def kernel(**inputs):
    w = prep_weights(inputs)
    x = inputs["L0_attn_x"]
    nc = build_program(["l0a", "l0f", "l1a", "l1f"])
    in_maps = []
    for c in range(NCORES):
        m = {"x": np.ascontiguousarray(x[2 * c:2 * c + 2])}
        m.update(w)
        in_maps.append(m)
    res = run_bass_kernel_spmd(nc, in_maps, core_ids=list(range(NCORES)))
    out = np.concatenate([r["out"] for r in res.results], axis=0)
    return np.ascontiguousarray(out.astype(np.float32))


# version 06
# speedup vs baseline: 1.1085x; 1.1085x over previous
import numpy as np
import contextlib
import concourse.bass as bass
import concourse.mybir as mybir
from concourse.bass_utils import run_bass_kernel_spmd

F32 = mybir.dt.float32
BF16 = mybir.dt.bfloat16
AF = mybir.ActivationFunctionType
ALU = mybir.AluOpType
AX = mybir.AxisListType

NCORES = 8
T = 2048
D = 1024
NT = T // 128
ND = D // 128
MARGIN = 20.0



def prep_weights(inp):
    w = {}
    H, Dh = 16, 64
    Wq = inp["L0_attn_Wq"].reshape(H, Dh, D)
    Wk = inp["L0_attn_Wk"].reshape(H, Dh, D)
    Wv = inp["L0_attn_Wv"].reshape(H, Dh, D)
    blocks, bvec = [], []
    for p in range(8):
        h1, h2 = 2 * p, 2 * p + 1
        blocks += [Wq[h1].T, Wq[h2].T, Wk[h1].T, Wk[h2].T, Wv[h1].T, Wv[h2].T]
        bvec += [inp["L0_attn_bq"][h1], inp["L0_attn_bq"][h2],
                 inp["L0_attn_bk"][h1], inp["L0_attn_bk"][h2],
                 inp["L0_attn_bv"][h1], inp["L0_attn_bv"][h2]]
    w["l0a_wqkv"] = np.ascontiguousarray(np.concatenate(blocks, axis=1))
    w["l0a_bqkv"] = np.ascontiguousarray(np.concatenate(bvec))
    w["l0a_proj"] = np.ascontiguousarray(inp["L0_attn_proj"])
    w["l0f_w1x"] = np.ascontiguousarray(inp["L0_ffn_w1"][:D])
    w["l0f_w1s"] = np.ascontiguousarray(inp["L0_ffn_w1"][D:])
    w["l0f_b1"] = np.ascontiguousarray(inp["L0_ffn_b1"])
    w["l0f_w2"] = np.ascontiguousarray(inp["L0_ffn_w2"])
    w["l0f_b2"] = np.ascontiguousarray(inp["L0_ffn_b2"])
    w["l0f_tau"] = np.ascontiguousarray(inp["L0_ffn_tau"])
    w["l0f_wihT"] = np.ascontiguousarray(inp["L0_ffn_Wih"].T)
    w["l0f_whhT"] = np.ascontiguousarray(inp["L0_ffn_Whh"].T)
    w["l0f_bih"] = np.ascontiguousarray(inp["L0_ffn_bih"])
    w["l0f_bhh"] = np.ascontiguousarray(inp["L0_ffn_bhh"])
    w["l0f_wq"] = np.ascontiguousarray(inp["L0_ffn_wq"])
    w["l0f_wk"] = np.ascontiguousarray(inp["L0_ffn_wk"])
    w["l0f_wv"] = np.ascontiguousarray(inp["L0_ffn_wv"])
    w["l0f_bq"] = np.ascontiguousarray(inp["L0_ffn_bq"])
    w["l0f_bk"] = np.ascontiguousarray(inp["L0_ffn_bk"])
    w["l0f_bv"] = np.ascontiguousarray(inp["L0_ffn_bv"])
    w["l0f_lng"] = np.ascontiguousarray(inp["L0_ffn_ln_g"])
    w["l0f_lnb"] = np.ascontiguousarray(inp["L0_ffn_ln_b"])
    w["l0f_wvp"] = np.ascontiguousarray(inp["L0_ffn_wvp"])
    w["l0f_bvp"] = np.ascontiguousarray(inp["L0_ffn_bvp"])
    w["l0f_wo"] = np.ascontiguousarray(inp["L0_ffn_wo"])
    w["l0f_bo"] = np.ascontiguousarray(inp["L0_ffn_bo"])
    w["l0f_slot"] = np.ascontiguousarray(inp["L0_ffn_slot_init"])
    Wqkv = inp["L1_attn_W_qkv"].reshape(6, H, Dh, D)
    blocks = []
    for h in range(H):
        for d6 in range(6):
            blocks.append(Wqkv[d6, h].T)
    w["l1a_wqkv"] = np.ascontiguousarray(np.concatenate(blocks, axis=1))
    w["l1a_projT"] = np.ascontiguousarray(inp["L1_attn_W_proj"].T)
    w["l1a_lam"] = np.ascontiguousarray(np.stack([
        inp["L1_attn_lambda_q1"], inp["L1_attn_lambda_k1"],
        inp["L1_attn_lambda_q2"], inp["L1_attn_lambda_k2"]], axis=1))
    w["l1f_w1"] = np.ascontiguousarray(inp["L1_ffn_W1"])
    w["l1f_b1"] = np.ascontiguousarray(inp["L1_ffn_b1"])
    w["l1f_w2"] = np.ascontiguousarray(inp["L1_ffn_W2"])
    w["l1f_b2"] = np.ascontiguousarray(inp["L1_ffn_b2"])
    w["c_ident"] = np.eye(128, dtype=np.float32)
    tt = np.arange(128)[:, None]
    qq = np.arange(512)[None, :]
    w["c_mask"] = np.ascontiguousarray(np.concatenate(
        [(tt + o <= qq).astype(np.float32) for o in (0, 128, 256, 384)], axis=1))
    w["c_tri"] = (np.arange(128)[:, None] <= np.arange(128)[None, :]).astype(np.float32)
    hd = 64
    inv_freq = 1.0 / 10000.0 ** (np.arange(0, 2 * hd, 2, dtype=np.float32) / (2 * hd))
    freqs = np.arange(T, dtype=np.float32)[:, None] * inv_freq[None, :]
    emb = np.concatenate((freqs, freqs), axis=-1)
    cos = np.cos(emb).astype(np.float32)
    sin = np.sin(emb).astype(np.float32)
    sin_signed = sin.copy()
    sin_signed[:, :hd] *= -1.0
    w["c_cos"] = np.ascontiguousarray(cos.T)
    w["c_sin"] = np.ascontiguousarray(sin_signed.T)
    return w


WEIGHT_SPECS = [
    ("l0a_wqkv", [1024, 3072]), ("l0a_bqkv", [3072]), ("l0a_proj", [1024, 1024]),
    ("l0f_w1x", [1024, 512]), ("l0f_w1s", [1024, 512]), ("l0f_b1", [512]),
    ("l0f_w2", [512, 64]), ("l0f_b2", [64]), ("l0f_tau", [1]),
    ("l0f_wihT", [1024, 3072]), ("l0f_whhT", [1024, 3072]),
    ("l0f_bih", [3072]), ("l0f_bhh", [3072]),
    ("l0f_wq", [1024, 256]), ("l0f_wk", [1024, 256]), ("l0f_wv", [1024, 1024]),
    ("l0f_bq", [256]), ("l0f_bk", [256]), ("l0f_bv", [1024]),
    ("l0f_lng", [1024]), ("l0f_lnb", [1024]),
    ("l0f_wvp", [1024, 1024]), ("l0f_bvp", [1024]),
    ("l0f_wo", [1024, 1024]), ("l0f_bo", [1024]), ("l0f_slot", [64, 1024]),
    ("l1a_wqkv", [1024, 6144]), ("l1a_projT", [2048, 1024]), ("l1a_lam", [64, 4]),
    ("l1f_w1", [1024, 4096]), ("l1f_b1", [4096]),
    ("l1f_w2", [4096, 1024]), ("l1f_b2", [1024]),
    ("c_ident", [128, 128]), ("c_mask", [128, 2048]), ("c_tri", [128, 128]),
    ("c_cos", [128, 2048]), ("c_sin", [128, 2048]),
]



class Sem:
    def __init__(self, env, name):
        self.h = env.es.enter_context(env.nc.semaphore(name))
        self.n = 0

    def inc(self, instr, k=1):
        if k:
            instr.then_inc(self.h, k)
            self.n += k
        return instr

    def dma(self, instr):
        return self.inc(instr, 16)


class Env:
    def __init__(self, nc):
        self.nc = nc
        self.es = contextlib.ExitStack()
        self.hbm = {}
        self._semi = 0

    def sbuf(self, name, shape, dtype, stack=None):
        return (stack or self.es).enter_context(
            self.nc.sbuf_tensor(name, shape, dtype))

    def psum(self, name, shape, dtype=F32, stack=None):
        return (stack or self.es).enter_context(
            self.nc.psum_tensor(name, shape, dtype))

    def sem(self, name=None):
        if name is None:
            self._semi += 1
            name = f"sem_{self._semi}"
        if not hasattr(self, "_semcache"):
            self._semcache = {}
        if name not in self._semcache:
            self._semcache[name] = Sem(self, name)
        return self._semcache[name]

    def uname(self, base):
        self._semi += 1
        return f"{base}_{self._semi}"


def dbg(env, name, ap, shape):
    if not getattr(env, "debug", False):
        return
    nc = env.nc
    name = name + getattr(env, "dbg_tag", "")
    d = nc.dram_tensor("dbg_" + name, shape, ap.tensor.dtype,
                       kind="ExternalOutput").ap()
    s = env.sem("dbg")
    with nc.Block() as blk:
        @blk.sync
        def _(e):
            s.dma(e.dma_start(d, ap))
            e.wait_ge(s.h, s.n)


def declare_hbm(env, x_batches=2):
    nc = env.nc
    env.x_in = nc.dram_tensor("x", [x_batches, T, D], F32, kind="ExternalInput").ap()
    env.x_out = nc.dram_tensor("out", [x_batches, T, D], F32,
                               kind="ExternalOutput").ap()
    for name, shape in WEIGHT_SPECS:
        env.hbm[name] = nc.dram_tensor(name, shape, F32, kind="ExternalInput").ap()


def alloc_core(env):
    env.X = env.sbuf("Xres", [128, NT * 1024], F32)
    env.BIGPOOL = env.sbuf("BIGPOOL", [128, NT * 1024], BF16)
    env.XT = env.BIGPOOL
    env.IDENT = env.sbuf("IDENT", [128, 128], BF16)
    env.IDF32 = env.sbuf("IDF32", [128, 128], F32)
    env.ONES = env.sbuf("ONESBF", [128, 128], BF16)
    env.MASK = env.sbuf("MASKBF", [128, 4 * 512], BF16)
    env.TRI = env.sbuf("TRIBF", [128, 128], BF16)
    env.STAT = env.sbuf("STAT", [128, 64], F32)
    env.SHIFTC = env.sbuf("SHIFTC", [128, 2], F32)
    env.SCR = env.sbuf("SCR", [128, 1024], F32)


def tl(buf, ti, w=1024):
    return buf[:, ti * w:(ti + 1) * w]



def emit_load_x(env, seq):
    nc = env.nc
    s = env.sem("ldx")
    with nc.Block() as blk:
        @blk.sync
        def _(e):
            for ti in range(NT):
                s.dma(e.dma_start(tl(env.X, ti),
                                  env.x_in[seq, ti * 128:(ti + 1) * 128, :]))
            e.wait_ge(s.h, s.n)


def emit_store_x(env, seq):
    nc = env.nc
    s = env.sem("stx")
    with nc.Block() as blk:
        @blk.sync
        def _(e):
            for ti in range(NT):
                s.dma(e.dma_start(env.x_out[seq, ti * 128:(ti + 1) * 128, :],
                                  tl(env.X, ti)))
            e.wait_ge(s.h, s.n)


def emit_constants(env):
    nc = env.nc
    s = env.sem("ldc")
    with nc.Block() as blk:
        @blk.gpsimd
        def _(g):
            s.dma(g.dma_start(env.IDENT[:], env.hbm["c_ident"]))
            s.dma(g.dma_start(env.IDF32[:], env.hbm["c_ident"]))
            s.dma(g.dma_start(env.MASK[:], env.hbm["c_mask"]))
            s.dma(g.dma_start(env.TRI[:], env.hbm["c_tri"]))
            g.wait_ge(s.h, s.n)
        @blk.vector
        def _(v):
            v.memset(env.ONES[:], 1.0)
            v.memset(env.SHIFTC[:, 0:1], -16.0)
            v.memset(env.SHIFTC[:, 1:2], -4.0)


def emit_colload(env, hbm_name, n, dst):
    nc = env.nc
    s = env.sem("clod")
    b_s = s.n
    with contextlib.ExitStack() as ls:
        stg = env.sbuf(env.uname("clstg"), [max(1, n), 128], F32, stack=ls)
        pt = env.psum(env.uname("clpt"), [128, max(n, 1)], F32, stack=ls)
        with nc.Block() as blk:
            @blk.sync
            def _(e):
                s.dma(e.dma_start(stg[0:n, :],
                      env.hbm[hbm_name].rearrange("(p a) -> p a", p=n)))
            @blk.tensor
            def _(t):
                t.wait_ge(s.h, b_s + 16)
                t.transpose(pt[:, 0:n], stg[0:n, :], env.IDF32[0:n, 0:n])
                t.drain()
        with nc.Block() as blk:
            @blk.vector
            def _(v):
                v.tensor_copy(dst[:, 0:n], pt[:, 0:n])


def emit_shadow(env, stack, do_xt=True):
    nc = env.nc
    xf = env.sbuf(env.uname("XF"), [128, ND * 2048], BF16, stack=stack)
    if do_xt:
        with nc.Block() as blk:
            @blk.vector
            def _(v):
                for ti in range(NT):
                    v.tensor_copy(tl(env.XT, ti), tl(env.X, ti))
    st = env.sem("sh_t")
    se = env.sem("sh_e")
    with contextlib.ExitStack() as ls:
        ptr = env.psum(env.uname("ps_shtr"), [128, 2048], BF16, stack=ls)
        b_st, b_se = st.n, se.n
        with nc.Block() as blk:
            @blk.tensor
            def _(t):
                grp = 0
                for dc in range(ND):
                    for g in range(4):
                        half = grp % 2
                        if grp >= 2:
                            t.wait_ge(se.h, b_se + grp - 1)
                        for j in range(4):
                            ti = g * 4 + j
                            t.transpose(
                                ptr[:, half * 1024 + j * 128:half * 1024 + (j + 1) * 128],
                                tl(env.XT, ti)[:, dc * 128:(dc + 1) * 128],
                                env.IDENT[:])
                            if j == 3:
                                t.drain()
                                st.inc(t.nop())
                        grp += 1

            @blk.vector
            def _(v):
                grp = 0
                for dc in range(ND):
                    for g in range(4):
                        half = grp % 2
                        v.wait_ge(st.h, b_st + grp + 1)
                        dst = xf[:, dc * 2048 + g * 512:dc * 2048 + (g + 1) * 512]
                        se.inc(v.tensor_copy(dst, ptr[:, half * 1024:half * 1024 + 512]))
                        grp += 1
    return xf


def emit_ln_epilogue(env):
    nc = env.nc
    s1 = env.sem("ln_a")
    b1 = s1.n
    with nc.Block() as blk:
        @blk.vector
        def _(v):
            for ti in range(NT):
                v.tensor_scalar(env.SCR[:, 0:1024], tl(env.X, ti), 1.0, 0.0,
                                op0=ALU.mult, op1=ALU.add,
                                accum_out=env.STAT[:, ti:ti + 1])
            for ti in range(NT):
                v.scalar_tensor_tensor(env.SCR[:, 0:1024], tl(env.X, ti), 1.0,
                                       tl(env.X, ti), op0=ALU.mult, op1=ALU.mult,
                                       accum_out=env.STAT[:, 16 + ti:17 + ti])
            v.drain()
            v.tensor_scalar(env.STAT[:, 32:48], env.STAT[:, 0:16], 1.0 / 1024, None,
                            op0=ALU.mult)
            v.tensor_scalar(env.STAT[:, 16:32], env.STAT[:, 16:32], 1.0 / 1024,
                            1e-5, op0=ALU.mult, op1=ALU.add)
            v.drain()
            v.scalar_tensor_tensor(env.STAT[:, 48:64], env.STAT[:, 32:48], 1.0,
                                   env.STAT[:, 32:48], op0=ALU.mult, op1=ALU.mult)
            v.drain()
            s1.inc(v.tensor_tensor(env.STAT[:, 16:32], env.STAT[:, 16:32],
                                   env.STAT[:, 48:64], op=ALU.subtract))

        @blk.scalar
        def _(sc):
            sc.wait_ge(s1.h, b1 + 1)
            sc.activation(env.STAT[:, 48:64], env.STAT[:, 16:32], AF.Sqrt,
                          bias=0.0, scale=1.0)
    with nc.Block() as blk:
        @blk.vector
        def _(v):
            v.reciprocal(env.STAT[:, 16:32], env.STAT[:, 48:64])
            v.drain()
            for ti in range(NT):
                v.tensor_scalar(tl(env.X, ti), tl(env.X, ti),
                                env.STAT[:, 32 + ti:33 + ti],
                                env.STAT[:, 16 + ti:17 + ti],
                                op0=ALU.subtract, op1=ALU.mult)


def emit_bcast_row(env, blk_scope, dst_sb, src_row_bf, width):
    nc = env.nc
    with contextlib.ExitStack() as ls:
        pb = env.psum(env.uname("ps_bc"), [128, ((width + 511) // 512) * 512], F32, stack=ls)
        with nc.Block() as blk:
            @blk.tensor
            def _(t):
                for c0 in range(0, width, 512):
                    cw = min(512, width - c0)
                    t.matmul(pb[:, c0:c0 + cw], env.ONES[0:1, :],
                             src_row_bf[0:1, c0:c0 + cw], start=True, stop=True)
        with nc.Block() as blk:
            @blk.vector
            def _(v):
                v.tensor_copy(dst_sb[:, 0:width], pb[:, 0:width])



def emit_l1_ffn(env):
    nc = env.nc
    with contextlib.ExitStack() as ls:
        XF = emit_shadow(env, ls)
        w1m = env.sbuf(env.uname("w1m"), [128, 2 * 1024], BF16, stack=ls)
        w2s = env.sbuf(env.uname("w2s"), [128, 2 * 512], BF16, stack=ls)
        hbuf = env.sbuf(env.uname("hbuf"), [128, 32 * 512], BF16, stack=ls)
        b1c = env.sbuf(env.uname("b1c"), [128, 32], F32, stack=ls)
        b2bc = env.sbuf(env.uname("b2bc"), [128, 1024], F32, stack=ls)
        b2row = env.sbuf(env.uname("b2row"), [1, 1024], BF16, stack=ls)
        ph = env.psum(env.uname("l1f_ph"), [128, 2 * 512], F32, stack=ls)
        po = env.psum(env.uname("l1f_po"), [128, 4 * 512], F32, stack=ls)

        sw1 = env.sem("l1f_w1")
        sw2 = env.sem("l1f_w2")
        sh = env.sem("l1f_h")
        she = env.sem("l1f_he")
        s2 = env.sem("l1f_o")
        sev = env.sem("l1f_ev")

        sc0 = env.sem("l1f_c")
        emit_colload(env, "l1f_b1", 32, b1c)
        with nc.Block() as blk:
            @blk.gpsimd
            def _(g):
                sc0.dma(g.dma_start(b2row[0:1, :],
                        env.hbm["l1f_b2"].rearrange("(a b) -> a b", a=1)))
                g.wait_ge(sc0.h, sc0.n)
        emit_bcast_row(env, None, b2bc, b2row, 1024)

        w1r = env.hbm["l1f_w1"].rearrange("(kc p) m -> p kc m", p=128)
        for tc in range(4):
            b_sw1, b_sw2 = sw1.n, sw2.n
            b_sh, b_she, b_s2, b_sev = sh.n, she.n, s2.n, sev.n
            with nc.Block() as blk:
                @blk.gpsimd
                def _(g):
                    for m in range(32):
                        if m >= 2:
                            g.wait_ge(sh.h, b_sh + m - 1)
                        half = m % 2
                        sw1.dma(g.dma_start(
                            w1m[:, half * 1024:(half + 1) * 1024]
                                .rearrange("p (a b) -> p a b", b=128),
                            w1r[:, :, m * 128:(m + 1) * 128]))
                    for e in range(2):
                        for m in range(32):
                            g2 = e * 32 + m
                            if g2 >= 2:
                                g.wait_ge(s2.h, b_s2 + g2 - 1)
                            half = g2 % 2
                            sw2.dma(g.dma_start(
                                w2s[:, half * 512:(half + 1) * 512],
                                env.hbm["l1f_w2"][m * 128:(m + 1) * 128,
                                                  e * 512:(e + 1) * 512]))

                @blk.tensor
                def _(t):
                    for m in range(32):
                        half = m % 2
                        t.wait_ge(sw1.h, b_sw1 + (m + 1) * 16)
                        if m >= 2:
                            t.wait_ge(she.h, b_she + m - 1)
                        for k in range(8):
                            ins = t.matmul(
                                ph[:, half * 512:(half + 1) * 512],
                                w1m[:, half * 1024 + k * 128:half * 1024 + (k + 1) * 128],
                                XF[:, k * 2048 + tc * 512:k * 2048 + (tc + 1) * 512],
                                start=(k == 0), stop=(k == 7))
                            sh.inc(ins, 1 if k == 7 else 0)
                    t.wait_ge(she.h, b_she + 32)
                    for e in range(2):
                        if e == 1:
                            t.wait_ge(sev.h, b_sev + 4)
                        for m in range(32):
                            g2 = e * 32 + m
                            half = g2 % 2
                            t.wait_ge(sw2.h, b_sw2 + (g2 + 1) * 16)
                            for ti4 in range(4):
                                ins = t.matmul(
                                    po[:, ti4 * 512:(ti4 + 1) * 512],
                                    hbuf[:, m * 512 + ti4 * 128:m * 512 + (ti4 + 1) * 128],
                                    w2s[:, half * 512:(half + 1) * 512],
                                    start=(m == 0), stop=(m == 31))
                                s2.inc(ins, 1 if ti4 == 3 else 0)

                @blk.scalar
                def _(sc):
                    for m in range(32):
                        half = m % 2
                        sc.wait_ge(sh.h, b_sh + m + 1)
                        she.inc(sc.activation(hbuf[:, m * 512:(m + 1) * 512],
                                              ph[:, half * 512:(half + 1) * 512],
                                              AF.Relu, bias=b1c[:, m:m + 1],
                                              scale=1.0))

                @blk.vector
                def _(v):
                    for e in range(2):
                        v.wait_ge(s2.h, b_s2 + (e + 1) * 32)
                        for ti4 in range(4):
                            ti = tc * 4 + ti4
                            xs = tl(env.X, ti)[:, e * 512:(e + 1) * 512]
                            v.tensor_tensor(env.SCR[:, 0:512],
                                            po[:, ti4 * 512:(ti4 + 1) * 512],
                                            b2bc[:, e * 512:(e + 1) * 512],
                                            op=ALU.add)
                            sev.inc(v.tensor_tensor(xs, xs, env.SCR[:, 0:512],
                                                    op=ALU.add))
    emit_ln_epilogue(env)



def ap3(ap2, group, inner):
    return ap2.rearrange("p (a b) -> p a b", b=inner)


def apcols(t, offset, step, count):
    a = t[:, 0:1]
    return bass.AP(a.tensor, a.offset + offset, [list(a.ap[0]), [step, count]])


def apcols3(t, offset, step, count, inner):
    a = t[:, 0:1]
    return bass.AP(a.tensor, a.offset + offset,
                   [list(a.ap[0]), [step, count], [1, inner]])


def apbcast(t, offset, step, count, inner):
    a = t[:, 0:1]
    return bass.AP(a.tensor, a.offset + offset,
                   [list(a.ap[0]), [step, count], [0, inner]])


def emit_l0_attn(env):
    nc = env.nc
    Pb = env.BIGPOOL
    with contextlib.ExitStack() as ls:
        XF = emit_shadow(env, ls)
        wks = env.sbuf(env.uname("wks"), [128, 2 * 384], BF16, stack=ls)
        qkvp = env.sbuf(env.uname("qkvp"), [128, 3 * 2048], BF16, stack=ls)
        vaug = env.sbuf(env.uname("vaug"), [128, 2 * 1040], BF16, stack=ls)
        Zcat = env.sbuf(env.uname("Zcat"), [128, 8 * 2048], BF16, stack=ls)
        bqkv = env.sbuf(env.uname("bqkv"), [128, 24], F32, stack=ls)
        cselfS = env.sbuf(env.uname("cselfS"), [128, 2], F32, stack=ls)
        cmax2 = env.sbuf(env.uname("cmax2"), [2, 4], F32, stack=ls)
        b2bf = env.sbuf(env.uname("b2bf"), [1, 2], BF16, stack=ls)
        biascol = env.sbuf(env.uname("biascol"), [128, 2], F32, stack=ls)
        linv4 = env.sbuf(env.uname("linv4"), [128, 8], F32, stack=ls)
        ztmp = env.sbuf(env.uname("ztmp"), [128, 512], BF16, stack=ls)

        emit_colload(env, "l0a_bqkv", 24, bqkv)
        with nc.Block() as blk:
            @blk.vector
            def _(v):
                v.memset(apcols(vaug, 64, 65, 32), 1.0)

        wr = env.hbm["l0a_wqkv"].rearrange("(kc p) m -> p kc m", p=128)
        swk = env.sem("l0a_wk")
        sk = env.sem("l0a_kc")
        sq = env.sem("l0a_q")
        sqe = env.sem("l0a_qe")
        stv = env.sem("l0a_tv")
        sve = env.sem("l0a_ve")
        skq = env.sem("l0a_kq")
        ssf = env.sem("l0a_sf")
        scs = env.sem("l0a_cs")
        sct = env.sem("l0a_ct")
        scb = env.sem("l0a_cb")
        sbc = env.sem("l0a_bc")
        sS = env.sem("l0a_S")
        sexp = env.sem("l0a_e")
        smask = env.sem("l0a_m")
        sZ = env.sem("l0a_Z")
        sZn = env.sem("l0a_Zn")
        sZt = env.sem("l0a_Zt")
        sZe = env.sem("l0a_Ze")

        for p in range(8):
            with contextlib.ExitStack() as ps1:
                psQ = env.psum(env.uname("psQ"), [128, 6 * 512], F32, stack=ps1)
                b_swk, b_sk, b_sq, b_sqe = swk.n, sk.n, sq.n, sqe.n
                with nc.Block() as blk:
                    @blk.gpsimd
                    def _(g):
                        for half in range(2):
                            for k in range(8):
                                idx = half * 8 + k
                                if idx >= 2:
                                    g.wait_ge(sk.h, b_sk + idx - 1)
                                swk.dma(g.dma_start(
                                    ap3(wks[:, (idx % 2) * 384:(idx % 2 + 1) * 384],
                                        1, 384),
                                    wr[:, k:k + 1, 384 * p:384 * (p + 1)]))

                    @blk.tensor
                    def _(t):
                        for half in range(2):
                            for k in range(8):
                                idx = half * 8 + k
                                t.wait_ge(swk.h, b_swk + (idx + 1) * 16)
                                if half == 1 and k == 0:
                                    t.wait_ge(sqe.h, b_sqe + 6)
                                for m in range(3):
                                    for q2 in range(2):
                                        gi = m * 2 + q2
                                        ins = t.matmul(
                                            psQ[:, gi * 512:(gi + 1) * 512],
                                            wks[:, (idx % 2) * 384 + m * 128:
                                                (idx % 2) * 384 + (m + 1) * 128],
                                            XF[:, k * 2048 + half * 1024 +
                                                   q2 * 512:k * 2048 + half * 1024 +
                                                   (q2 + 1) * 512],
                                            start=(k == 0), stop=(k == 7))
                                        if k == 7:
                                            sq.inc(ins)
                                            if m == 2 and q2 == 1:
                                                sk.inc(t.nop())
                                        elif m == 2 and q2 == 1:
                                            sk.inc(ins)

                    @blk.scalar
                    def _(sc):
                        for half in range(2):
                            for m in range(3):
                                for q2 in range(2):
                                    gi = m * 2 + q2
                                    sc.wait_ge(sq.h, b_sq + half * 6 + gi + 1)
                                    sqe.inc(sc.activation(
                                        qkvp[:, m * 2048 + half * 1024 + q2 * 512:
                                             m * 2048 + half * 1024 + (q2 + 1) * 512],
                                        psQ[:, gi * 512:(gi + 1) * 512],
                                        AF.Identity, bias=bqkv[:, p * 3 + m:p * 3 + m + 1],
                                        scale=1.0))

            with contextlib.ExitStack() as ps2:
                psT = env.psum(env.uname("psT"), [128, 2048], BF16, stack=ps2)
                b_stv, b_sve = stv.n, sve.n
                with nc.Block() as blk:
                    @blk.tensor
                    def _(t):
                        for h2 in range(2):
                            for g in range(4):
                                grp = h2 * 4 + g
                                half = grp % 2
                                if grp >= 2:
                                    t.wait_ge(sve.h, b_sve + grp - 1)
                                for j in range(4):
                                    ti = g * 4 + j
                                    t.transpose(
                                        psT[:, half * 1024 + j * 64:
                                            half * 1024 + (j + 1) * 64],
                                        qkvp[64 * h2:64 * h2 + 64,
                                             2 * 2048 + ti * 128:2 * 2048 + (ti + 1) * 128],
                                        env.IDENT[64 * h2:64 * h2 + 64,
                                                  64 * h2:64 * h2 + 64])
                                    if j == 3:
                                        t.drain()
                                        stv.inc(t.nop())

                    @blk.vector
                    def _(v):
                        for h2 in range(2):
                            for g in range(4):
                                grp = h2 * 4 + g
                                half = grp % 2
                                v.wait_ge(stv.h, b_stv + grp + 1)
                                sve.inc(v.tensor_scalar(
                                    apcols3(vaug, h2 * 1040 + g * 260, 65, 4, 64),
                                    ap3(psT[:, half * 1024:half * 1024 + 256], 4, 64),
                                    0.125, None, op0=ALU.mult))

            with contextlib.ExitStack() as ps3:
                psS = env.psum(env.uname("psS"), [128, 4 * 512], F32, stack=ps3)
                psZ = env.psum(env.uname("psZ"), [128, 2 * 512], F32, stack=ps3)
                psT2 = env.psum(env.uname("psT2"), [128, 2048], BF16, stack=ps3)
                b_sS, b_sexp, b_smask = sS.n, sexp.n, smask.n
                b_sZ, b_sZn, b_sZt, b_sZe = sZ.n, sZn.n, sZt.n, sZe.n
                ntile = [4 * qc + 4 for qc in range(4)]
                cumt = [sum(ntile[:qc]) for qc in range(5)]
                with nc.Block() as blk:
                    @blk.tensor
                    def _(t):
                        for qc in range(4):
                            for ti in range(4 * qc + 4):
                                gidx = cumt[qc] + ti
                                buf = gidx % 2
                                if gidx >= 2:
                                    t.wait_ge(sexp.h, b_sexp + (gidx - 1) * 2)
                                t.matmul(psS[:, buf * 1024:buf * 1024 + 512],
                                         qkvp[0:64, 2048 + ti * 128:2048 + (ti + 1) * 128],
                                         qkvp[0:64, qc * 512:(qc + 1) * 512],
                                         start=True, stop=True, tile_position=(0, 0))
                                ins = t.matmul(
                                    psS[:, buf * 1024 + 512:buf * 1024 + 1024],
                                    qkvp[64:128, 2048 + ti * 128:2048 + (ti + 1) * 128],
                                    qkvp[64:128, qc * 512:(qc + 1) * 512],
                                    start=True, stop=True, tile_position=(64, 0))
                                sS.inc(ins)
                            t.wait_ge(smask.h, b_smask + qc * 8 + 8)
                            for h2 in range(2):
                                if qc > 0:
                                    t.wait_ge(sZn.h, b_sZn + (qc - 1) * 2 + h2 + 1)
                                for qsub in range(4):
                                    qt = 4 * qc + qsub
                                    for ti in range(qt + 1):
                                        t.matmul(
                                            psZ[:, h2 * 512 + qsub * 65:
                                                h2 * 512 + qsub * 65 + 65],
                                            Pb[:, h2 * 8192 + ti * 512 + qsub * 128:
                                               h2 * 8192 + ti * 512 + (qsub + 1) * 128],
                                            vaug[:, h2 * 1040 + ti * 65:
                                                 h2 * 1040 + (ti + 1) * 65],
                                            start=(ti == 0), stop=(ti == qt))
                                    t.drain()
                                    sZ.inc(t.nop())
                            for h2 in range(2):
                                zi = qc * 2 + h2
                                half = zi % 2
                                t.wait_ge(sZn.h, b_sZn + zi + 1)
                                if zi >= 2:
                                    t.wait_ge(sZe.h, b_sZe + zi - 1)
                                for qsub in range(4):
                                    t.transpose(
                                        psT2[0:64, half * 1024 + qsub * 128:
                                             half * 1024 + (qsub + 1) * 128],
                                        ztmp[:, h2 * 256 + qsub * 64:
                                             h2 * 256 + (qsub + 1) * 64],
                                        env.IDENT[:])
                                t.drain()
                                sZt.inc(t.nop())

                    @blk.scalar
                    def _(sc):
                        for qc in range(4):
                            sc.wait_ge(sZ.h, b_sZ + qc * 8)
                            for ti in range(4 * qc + 4):
                                gidx = cumt[qc] + ti
                                buf = gidx % 2
                                sc.wait_ge(sS.h, b_sS + gidx + 1)
                                sexp.inc(sc.activation(
                                    Pb[:, 0 * 8192 + ti * 512:0 * 8192 + (ti + 1) * 512],
                                    psS[:, buf * 1024:buf * 1024 + 512],
                                    AF.Exp, bias=env.SHIFTC[:, 0:1], scale=0.125))
                                sexp.inc(sc.activation(
                                    Pb[:, 1 * 8192 + ti * 512:1 * 8192 + (ti + 1) * 512],
                                    psS[:, buf * 1024 + 512:buf * 1024 + 1024],
                                    AF.Exp, bias=env.SHIFTC[:, 0:1], scale=0.125))

                    @blk.vector
                    def _(v):
                        for qc in range(4):
                            for j in range(4):
                                ti = 4 * qc + j
                                gidx = cumt[qc] + ti
                                v.wait_ge(sexp.h, b_sexp + (gidx + 1) * 2)
                                for h2 in range(2):
                                    sl = Pb[:, h2 * 8192 + ti * 512 + j * 128:
                                            h2 * 8192 + ti * 512 + (j + 1) * 128]
                                    v.tensor_tensor(sl, sl, env.TRI[:], op=ALU.mult)
                                    v.drain()
                                    smask.inc(v.nop())
                            for h2 in range(2):
                                v.wait_ge(sZ.h, b_sZ + qc * 8 + (h2 + 1) * 4)
                                v.reciprocal(
                                    linv4[:, h2 * 4:(h2 + 1) * 4],
                                    bass.AP(psZ[:, 0:1].tensor,
                                            psZ[:, 0:1].offset + h2 * 512 + 64,
                                            [list(psZ[:, 0:1].ap[0]), [65, 4]]))
                                v.drain()
                                v.tensor_tensor(
                                    ap3(ztmp[:, h2 * 256:(h2 + 1) * 256], 4, 64),
                                    apcols3(psZ, h2 * 512, 65, 4, 64),
                                    apbcast(linv4, h2 * 4, 1, 4, 64),
                                    op=ALU.mult)
                                v.drain()
                                sZn.inc(v.nop())
                            for h2 in range(2):
                                zi = qc * 2 + h2
                                half = zi % 2
                                v.wait_ge(sZt.h, b_sZt + zi + 1)
                                sZe.inc(v.tensor_copy(
                                    Zcat[h2 * 64:h2 * 64 + 64,
                                         p * 2048 + qc * 512:p * 2048 + (qc + 1) * 512],
                                    psT2[0:64, half * 1024:half * 1024 + 512]))

        with contextlib.ExitStack() as ls2:
            projw = env.sbuf(env.uname("projw"), [128, 8 * 1024], BF16, stack=ls2)
            po = env.psum(env.uname("po"), [128, 2 * 512], F32, stack=ls2)
            swp = env.sem("l0a_wp")
            sop = env.sem("l0a_op")
            sevp = env.sem("l0a_ev")
            b_swp, b_sop, b_sevp = swp.n, sop.n, sevp.n
            with nc.Block() as blk:
                @blk.gpsimd
                def _(g):
                    for jc in range(8):
                        swp.dma(g.dma_start(
                            projw[:, jc * 1024:(jc + 1) * 1024],
                            env.hbm["l0a_proj"][jc * 128:(jc + 1) * 128, :]))

                @blk.tensor
                def _(t):
                    for ti in range(NT):
                        for e in range(2):
                            oi = ti * 2 + e
                            buf = oi % 2
                            if oi >= 2:
                                t.wait_ge(sevp.h, b_sevp + oi - 1)
                            for jc in range(8):
                                t.wait_ge(swp.h, b_swp + (jc + 1) * 16)
                                ins = t.matmul(
                                    po[:, buf * 512:(buf + 1) * 512],
                                    Zcat[:, jc * 2048 + ti * 128:jc * 2048 + (ti + 1) * 128],
                                    projw[:, jc * 1024 + e * 512:jc * 1024 + (e + 1) * 512],
                                    start=(jc == 0), stop=(jc == 7))
                            sop.inc(ins)

                @blk.vector
                def _(v):
                    for ti in range(NT):
                        for e in range(2):
                            oi = ti * 2 + e
                            buf = oi % 2
                            v.wait_ge(sop.h, b_sop + oi + 1)
                            xs = tl(env.X, ti)[:, e * 512:(e + 1) * 512]
                            sevp.inc(v.scalar_tensor_tensor(
                                xs, po[:, buf * 512:(buf + 1) * 512], 1.0, xs,
                                op0=ALU.mult, op1=ALU.add))
    emit_ln_epilogue(env)



def emit_l0_ffn(env):
    nc = env.nc
    with contextlib.ExitStack() as lo:
        alpha = env.sbuf(env.uname("alpha"), [128, 1024], BF16, stack=lo)
        svwa = env.sbuf(env.uname("svwa"), [64, 1024], BF16, stack=lo)
        slotS = env.sbuf(env.uname("slotS"), [64, 1024], F32, stack=lo)
        slotb = env.sbuf(env.uname("slotb"), [64, 1024], BF16, stack=lo)

        def tr64(src, dst, nblk):
            with contextlib.ExitStack() as lb:
                pt = env.psum(env.uname("ptr"), [128, max(512, nblk * 64)], BF16,
                              stack=lb)
                with nc.Block() as blk:
                    @blk.tensor
                    def _(t):
                        for j in range(nblk):
                            t.transpose(pt[:, j * 64:(j + 1) * 64],
                                        src[0:64, j * 128:(j + 1) * 128],
                                        env.IDENT[0:64, 0:64])
                        t.drain()
                with nc.Block() as blk:
                    @blk.vector
                    def _(v):
                        v.tensor_copy(dst[:, 0:nblk * 64], pt[:, 0:nblk * 64])

        with contextlib.ExitStack() as la:
            XF = emit_shadow(env, la)
            w1m = env.sbuf(env.uname("w1m"), [128, 2 * 1024], BF16, stack=la)
            w1sm = env.sbuf(env.uname("w1sm"), [128, 4096], BF16, stack=la)
            hfm = env.sbuf(env.uname("hfm"), [128, 4 * 2048], BF16, stack=la)
            rl = env.sbuf(env.uname("rl"), [128, 1024], F32, stack=la)
            rcur = env.sbuf(env.uname("rcur"), [128, 1024], F32, stack=la)
            mk = env.sbuf(env.uname("mk"), [128, 1024], F32, stack=la)
            tkst = env.sbuf(env.uname("tkst"), [128, 64], F32, stack=la)
            w2b = env.sbuf(env.uname("w2b"), [128, 256], BF16, stack=la)
            smf = env.sbuf(env.uname("smf"), [128, 8], F32, stack=la)
            smb = env.sbuf(env.uname("smb"), [128, 8], BF16, stack=la)
            b1cc = env.sbuf(env.uname("b1cc"), [128, 4], F32, stack=la)
            b1e = env.sbuf(env.uname("b1e"), [128, 4], F32, stack=la)
            b2i = env.sbuf(env.uname("b2i"), [128, 64], F32, stack=la)
            b2rb = env.sbuf(env.uname("b2rb"), [1, 64], BF16, stack=la)
            taus = env.sbuf(env.uname("taus"), [1, 4], F32, stack=la)
            taub = env.sbuf(env.uname("taub"), [1, 1], BF16, stack=la)
            invt = env.sbuf(env.uname("invt"), [128, 1], F32, stack=la)

            sA = env.sem("l0f_A")
            with nc.Block() as blk:
                @blk.sync
                def _(e):
                    sA.dma(e.dma_start(slotS[:], env.hbm["l0f_slot"]))
                    sA.dma(e.dma_start(taus[0:1, 0:1],
                           env.hbm["l0f_tau"].rearrange("(a b) -> a b", a=1)))
                @blk.gpsimd
                def _(g):
                    sA.dma(g.dma_start(slotb[:], env.hbm["l0f_slot"]))
                    sA.dma(g.dma_start(b2rb[0:1, :],
                           env.hbm["l0f_b2"].rearrange("(a b) -> a b", a=1)))
                    for m in range(4):
                        sA.dma(g.dma_start(
                            ap3(w1sm[:, m * 1024:(m + 1) * 1024], 8, 128),
                            env.hbm["l0f_w1s"].rearrange("(kc p) m -> p kc m", p=128)
                            [:, :, m * 128:(m + 1) * 128]))
                    for m in range(4):
                        sA.dma(g.dma_start(w2b[:, m * 64:(m + 1) * 64],
                               env.hbm["l0f_w2"][m * 128:(m + 1) * 128, :]))
                    g.wait_ge(sA.h, sA.n)
            emit_colload(env, "l0f_b1", 4, b1cc)
            with nc.Block() as blk:
                @blk.scalar
                def _(sc):
                    sc.activation(taus[0:1, 1:2], taus[0:1, 0:1], AF.Abs,
                                  bias=0.0, scale=1.0)
            with nc.Block() as blk:
                @blk.vector
                def _(v):
                    v.tensor_scalar(taus[0:1, 1:2], taus[0:1, 1:2], 1.0, 0.1,
                                    op0=ALU.mult, op1=ALU.add)
                    v.drain()
                    v.reciprocal(taus[0:1, 2:3], taus[0:1, 1:2])
                    v.drain()
                    v.tensor_copy(taub[0:1, 0:1], taus[0:1, 2:3])
            with contextlib.ExitStack() as lb:
                psC = env.psum(env.uname("psC"), [128, 512], F32, stack=lb)
                with nc.Block() as blk:
                    @blk.tensor
                    def _(t):
                        for dc in range(8):
                            t.matmul(psC[:, dc:dc + 1],
                                     slotb[0:64, dc * 128:(dc + 1) * 128],
                                     env.ONES[0:64, 0:1], start=True, stop=True)
                        t.matmul(psC[:, 16:17], env.ONES[0:1, :], taub[0:1, 0:1],
                                 start=True, stop=True)
                        t.matmul(psC[:, 128:192], env.ONES[0:1, :], b2rb[0:1, :],
                                 start=True, stop=True)
                        t.drain()
                with nc.Block() as blk:
                    @blk.vector
                    def _(v):
                        v.tensor_scalar(smf[:], psC[:, 0:8], 1.0 / 64, None,
                                        op0=ALU.mult)
                        v.tensor_copy(invt[:, 0:1], psC[:, 16:17])
                        v.drain()
                        v.tensor_copy(smb[:], smf[:])
                        v.tensor_scalar(b2i[:], psC[:, 128:192], invt[:, 0:1],
                                        None, op0=ALU.mult)
            with contextlib.ExitStack() as lb:
                psB = env.psum(env.uname("psB"), [128, 512], F32, stack=lb)
                with nc.Block() as blk:
                    @blk.tensor
                    def _(t):
                        for m in range(4):
                            for k in range(8):
                                t.matmul(psB[:, m:m + 1],
                                         w1sm[:, m * 1024 + k * 128:
                                              m * 1024 + (k + 1) * 128],
                                         smb[:, k:k + 1],
                                         start=(k == 0), stop=(k == 7))
                        t.drain()
                with nc.Block() as blk:
                    @blk.vector
                    def _(v):
                        v.tensor_tensor(b1e[:], psB[:, 0:4], b1cc[:], op=ALU.add)

            sw1 = env.sem("l0f_w1")
            sh1 = env.sem("l0f_h1")
            she = env.sem("l0f_he")
            b_sw1, b_sh1, b_she = sw1.n, sh1.n, she.n
            w1r = env.hbm["l0f_w1x"].rearrange("(kc p) m -> p kc m", p=128)
            with contextlib.ExitStack() as lb:
                psH = env.psum(env.uname("psH"), [128, 2 * 512], F32, stack=lb)
                with nc.Block() as blk:
                    @blk.gpsimd
                    def _(g):
                        for m in range(4):
                            if m >= 2:
                                g.wait_ge(sh1.h, b_sh1 + (m - 1) * 4)
                            sw1.dma(g.dma_start(
                                ap3(w1m[:, (m % 2) * 1024:(m % 2 + 1) * 1024], 8, 128),
                                w1r[:, :, m * 128:(m + 1) * 128]))

                    @blk.tensor
                    def _(t):
                        for m in range(4):
                            t.wait_ge(sw1.h, b_sw1 + (m + 1) * 16)
                            for qc in range(4):
                                oi = m * 4 + qc
                                half = oi % 2
                                if oi >= 2:
                                    t.wait_ge(she.h, b_she + oi - 1)
                                for k in range(8):
                                    ins = t.matmul(
                                        psH[:, half * 512:(half + 1) * 512],
                                        w1m[:, (m % 2) * 1024 + k * 128:
                                            (m % 2) * 1024 + (k + 1) * 128],
                                        XF[:, k * 2048 + qc * 512:
                                           k * 2048 + (qc + 1) * 512],
                                        start=(k == 0), stop=(k == 7))
                                    if k == 7:
                                        sh1.inc(ins)

                    @blk.scalar
                    def _(sc):
                        for m in range(4):
                            for qc in range(4):
                                oi = m * 4 + qc
                                half = oi % 2
                                sc.wait_ge(sh1.h, b_sh1 + oi + 1)
                                she.inc(sc.activation(
                                    hfm[:, m * 2048 + qc * 512:
                                        m * 2048 + (qc + 1) * 512],
                                    psH[:, half * 512:(half + 1) * 512],
                                    AF.Gelu, bias=b1e[:, m:m + 1], scale=1.0))

            srl = env.sem("l0f_rl")
            sre = env.sem("l0f_re")
            b_srl, b_sre = srl.n, sre.n
            with contextlib.ExitStack() as lb:
                psRL = env.psum(env.uname("psRL"), [128, 2 * 512], F32, stack=lb)
                with nc.Block() as blk:
                    @blk.tensor
                    def _(t):
                        for g in range(4):
                            half = g % 2
                            if g >= 2:
                                t.wait_ge(sre.h, b_sre + g - 1)
                            for tj in range(4):
                                ti = g * 4 + tj
                                for m in range(4):
                                    ins = t.matmul(
                                        psRL[:, half * 512 + tj * 64:
                                             half * 512 + (tj + 1) * 64],
                                        hfm[:, m * 2048 + ti * 128:
                                            m * 2048 + (ti + 1) * 128],
                                        w2b[:, m * 64:(m + 1) * 64],
                                        start=(m == 0), stop=(m == 3))
                                    if m == 3 and tj == 3:
                                        t.drain()
                                        srl.inc(t.nop())

                    @blk.vector
                    def _(v):
                        for g in range(4):
                            half = g % 2
                            v.wait_ge(srl.h, b_srl + g + 1)
                            sre.inc(v.scalar_tensor_tensor(
                                ap3(rl[:, g * 256:(g + 1) * 256], 4, 64),
                                ap3(psRL[:, half * 512:half * 512 + 256], 4, 64),
                                invt[:, 0:1],
                                bass.AP(b2i[:, 0:1].tensor, b2i[:, 0:1].offset,
                                        [list(b2i[:, 0:1].ap[0]), [0, 4], [1, 64]]),
                                op0=ALU.mult, op1=ALU.add))

            dbg(env, "rl", rl[:], [128, 1024])
            stk = env.sem("l0f_tk")
            b_stk = stk.n
            with nc.Block() as blk:
                @blk.vector
                def _(v):
                    v.tensor_copy(rcur[:], rl[:])
                    v.drain()
                    for i in range(8):
                        v.reduce_max(tkst[:, 0:16], ap3(rcur[:, :], 16, 64),
                                     axis=AX.X)
                        v.drain()
                        if i == 0:
                            v.tensor_scalar(tkst[:, 32:48], tkst[:, 0:16], -1.0,
                                            None, op0=ALU.mult)
                            v.drain()
                        if i < 7:
                            v.tensor_tensor(ap3(mk[:, :], 16, 64),
                                            ap3(rcur[:, :], 16, 64),
                                            apbcast(tkst, 0, 1, 16, 64),
                                            op=ALU.is_ge)
                            v.drain()
                            v.scalar_tensor_tensor(rcur[:], mk[:], -1e30, rcur[:],
                                                   op0=ALU.mult, op1=ALU.add)
                            v.drain()
                    stk.inc(v.nop())

                @blk.scalar
                def _(sc):
                    sc.wait_ge(stk.h, b_stk + 1)
                    for ti in range(16):
                        ins = sc.activation(rcur[:, ti * 64:(ti + 1) * 64],
                                            rl[:, ti * 64:(ti + 1) * 64],
                                            AF.Exp, bias=tkst[:, 32 + ti:33 + ti],
                                            scale=1.0)
                        stk.inc(ins, 1 if ti == 15 else 0)
            with nc.Block() as blk:
                @blk.vector
                def _(v):
                    v.tensor_tensor(ap3(mk[:, :], 16, 64), ap3(rl[:, :], 16, 64),
                                    apbcast(tkst, 0, 1, 16, 64), op=ALU.is_ge)
                    v.drain()
                    v.tensor_tensor(rcur[:], rcur[:], mk[:], op=ALU.mult)
                    v.drain()
                    v.reduce_sum(tkst[:, 16:32], ap3(rcur[:, :], 16, 64), axis=AX.X)
                    v.drain()
                    v.reciprocal(tkst[:, 48:64], tkst[:, 16:32])
                    v.drain()
                    v.tensor_tensor(ap3(alpha[:, :], 16, 64), ap3(rcur[:, :], 16, 64),
                                    apbcast(tkst, 48, 1, 16, 64), op=ALU.mult)

        dbg(env, "alpha", alpha[:], [128, 1024])
        with contextlib.ExitStack() as lb0:
            sib = env.sbuf(env.uname("sib"), [64, 1024], BF16, stack=lb0)
            sif = env.sbuf(env.uname("sif"), [128, 512], BF16, stack=lb0)
            snf = env.sbuf(env.uname("snf"), [128, 512], BF16, stack=lb0)
            gisc = env.sbuf(env.uname("gisc"), [64, 3072], F32, stack=lb0)
            ghsc = env.sbuf(env.uname("ghsc"), [64, 3072], F32, stack=lb0)
            bib = env.sbuf(env.uname("bib"), [64, 3072], BF16, stack=lb0)
            bhb = env.sbuf(env.uname("bhb"), [64, 3072], BF16, stack=lb0)
            rstg = env.sbuf(env.uname("rstg"), [1, 512], F32, stack=lb0)
            rstb = env.sbuf(env.uname("rstb"), [1, 512], BF16, stack=lb0)

            wstr = env.sbuf(env.uname("wstr"), [128, 2 * 512], BF16, stack=lb0)
            snew = env.sbuf(env.uname("snew"), [64, 1024], F32, stack=lb0)
            dnv = env.sbuf(env.uname("dnv"), [64, 4], F32, stack=lb0)
            qf = env.sbuf(env.uname("qf"), [128, 128], BF16, stack=lb0)
            kf = env.sbuf(env.uname("kf"), [128, 128], BF16, stack=lb0)
            vsb = env.sbuf(env.uname("vsb"), [64, 1024], BF16, stack=lb0)
            qsm = env.sbuf(env.uname("qsm"), [64, 512], BF16, stack=lb0)
            Pt = env.sbuf(env.uname("Pt"), [64, 64], BF16, stack=lb0)
            ysc = env.sbuf(env.uname("ysc"), [64, 1024], F32, stack=lb0)
            y2 = env.sbuf(env.uname("y2"), [64, 1024], F32, stack=lb0)
            ylb = env.sbuf(env.uname("ylb"), [64, 1024], BF16, stack=lb0)
            ylf = env.sbuf(env.uname("ylf"), [128, 512], BF16, stack=lb0)
            svb = env.sbuf(env.uname("svb"), [64, 1024], BF16, stack=lb0)
            svf = env.sbuf(env.uname("svf"), [128, 512], BF16, stack=lb0)
            lnst = env.sbuf(env.uname("lnst"), [64, 8], F32, stack=lb0)
            scs2 = env.sbuf(env.uname("scs2"), [64, 4], F32, stack=lb0)
            cb1 = env.sbuf(env.uname("cb1"), [1, 2], BF16, stack=lb0)
            bcol2 = env.sbuf(env.uname("bcol2"), [64, 1], F32, stack=lb0)
            bqb = env.sbuf(env.uname("bqb"), [64, 512], F32, stack=lb0)
            bvb = env.sbuf(env.uname("bvb"), [64, 1024], F32, stack=lb0)
            lgb = env.sbuf(env.uname("lgb"), [64, 1024], F32, stack=lb0)
            lbb = env.sbuf(env.uname("lbb"), [64, 1024], F32, stack=lb0)
            cvb = env.sbuf(env.uname("cvb"), [64, 1024], BF16, stack=lb0)
            rstg2 = env.sbuf(env.uname("rstg2"), [1, 512], F32, stack=lb0)
            bvpb = env.sbuf(env.uname("bvpb"), [128, 8], BF16, stack=lb0)

            sB = env.sem("l0f_B")
            emit_colload(env, "l0f_bvp", 8, bvpb)
            with contextlib.ExitStack() as lb:
                pb = env.psum(env.uname("pbc2"), [64, 512], F32, stack=lb)
                bcjobs = []
                for name, dst, w in (("bih", bib, 3072), ("bhh", bhb, 3072),
                                     ("bq", bqb, 256), ("bk", bqb, 256),
                                     ("bv", bvb, 1024), ("lng", lgb, 1024),
                                     ("lnb", lbb, 1024)):
                    for c0 in range(0, w, 512):
                        cw = min(512, w - c0)
                        do = (256 + c0 if name == "bk" else c0)
                        bcjobs.append((name, c0, dst, do, cw))
                sBC = env.sem("l0f_bc")
                for name, c0, dst, dst_off, cw in bcjobs:
                    b_sBC = sBC.n
                    with nc.Block() as blk:
                        @blk.sync
                        def _(e, name=name, c0=c0, cw=cw):
                            sBC.dma(e.dma_start(rstg[0:1, 0:cw],
                                    env.hbm["l0f_" + name]
                                    .rearrange("(a b) -> a b", a=1)[0:1, c0:c0 + cw]))
                        @blk.vector
                        def _(v, b_sBC=b_sBC, cw=cw):
                            v.wait_ge(sBC.h, b_sBC + 16)
                            v.tensor_copy(rstb[0:1, 0:cw], rstg[0:1, 0:cw])
                    with nc.Block() as blk:
                        @blk.tensor
                        def _(t, cw=cw):
                            t.matmul(pb[0:64, 0:cw], env.ONES[0:1, 0:64],
                                     rstb[0:1, 0:cw], start=True, stop=True)
                            t.drain()
                    with nc.Block() as blk:
                        @blk.vector
                        def _(v, dst=dst, dst_off=dst_off, cw=cw):
                            v.tensor_copy(dst[0:64, dst_off:dst_off + cw],
                                          pb[0:64, 0:cw])

            with contextlib.ExitStack() as lb:
                psSI = env.psum(env.uname("psSI"), [128, 1024], F32, stack=lb)
                psD = env.psum(env.uname("psD"), [128, 512], F32, stack=lb)
                with nc.Block() as blk:
                    @blk.tensor
                    def _(t):
                        for half in range(2):
                            for ti in range(16):
                                t.matmul(psSI[0:64, half * 512:(half + 1) * 512],
                                         alpha[:, ti * 64:(ti + 1) * 64],
                                         env.XT[:, ti * 1024 + half * 512:
                                                ti * 1024 + (half + 1) * 512],
                                         start=(ti == 0), stop=(ti == 15))
                        for ti in range(16):
                            t.matmul(psD[0:64, 0:1], alpha[:, ti * 64:(ti + 1) * 64],
                                     env.ONES[:, 0:1],
                                     start=(ti == 0), stop=(ti == 15))
                        t.drain()
                with nc.Block() as blk:
                    @blk.vector
                    def _(v):
                        v.tensor_scalar(dnv[0:64, 0:1], psD[0:64, 0:1], 1.0, 1e-8,
                                        op0=ALU.mult, op1=ALU.add)
                        v.drain()
                        v.reciprocal(dnv[0:64, 1:2], dnv[0:64, 0:1])
                        v.drain()
                        v.tensor_scalar(sib[:], psSI[0:64, :], dnv[0:64, 1:2],
                                        None, op0=ALU.mult)
            dbg(env, "sib", sib[:], [64, 1024])
            tr64(sib, sif, 8)
            tr64(slotb, snf, 8)

            for wname, lhs, dstg, bcb in (("l0f_whhT", snf, ghsc, bhb),
                                          ("l0f_wihT", sif, gisc, bib)):
                sg = env.sem("l0f_g")
                sge = env.sem("l0f_ge")
                swg = env.sem("l0f_wg")
                b_sg, b_sge, b_swg = sg.n, sge.n, swg.n
                with contextlib.ExitStack() as lb:
                    psG = env.psum(env.uname("psG"), [128, 1024], F32, stack=lb)
                    with nc.Block() as blk:
                        @blk.gpsimd
                        def _(g, wname=wname, b_sg=b_sg):
                            for n6 in range(6):
                                for k in range(8):
                                    idx = n6 * 8 + k
                                    if idx >= 2:
                                        g.wait_ge(sg.h, b_sg + idx - 1)
                                    swg.dma(g.dma_start(
                                        wstr[:, (idx % 2) * 512:(idx % 2 + 1) * 512],
                                        env.hbm[wname][k * 128:(k + 1) * 128,
                                                       n6 * 512:(n6 + 1) * 512]))

                        @blk.tensor
                        def _(t, lhs=lhs, b_swg=b_swg, b_sge=b_sge):
                            for n6 in range(6):
                                half = n6 % 2
                                if n6 >= 2:
                                    t.wait_ge(sge.h, b_sge + n6 - 1)
                                for k in range(8):
                                    idx = n6 * 8 + k
                                    t.wait_ge(swg.h, b_swg + (idx + 1) * 16)
                                    ins = t.matmul(
                                        psG[0:64, half * 512:(half + 1) * 512],
                                        lhs[:, k * 64:(k + 1) * 64],
                                        wstr[:, (idx % 2) * 512:(idx % 2 + 1) * 512],
                                        start=(k == 0), stop=(k == 7))
                                    if k == 7:
                                        t.drain()
                                    sg.inc(ins if k < 7 else t.nop())

                        @blk.vector
                        def _(v, dstg=dstg, bcb=bcb, b_sg=b_sg):
                            for n6 in range(6):
                                half = n6 % 2
                                v.wait_ge(sg.h, b_sg + (n6 + 1) * 8)
                                sge.inc(v.tensor_tensor(
                                    dstg[0:64, n6 * 512:(n6 + 1) * 512],
                                    psG[0:64, half * 512:(half + 1) * 512],
                                    bcb[0:64, n6 * 512:(n6 + 1) * 512], op=ALU.add))

            dbg(env, "gisc", gisc[:], [64, 3072])
            dbg(env, "ghsc", ghsc[:], [64, 3072])
            sg1 = env.sem("l0f_s1")
            sg2 = env.sem("l0f_s2")
            sg3 = env.sem("l0f_s3")
            sg4 = env.sem("l0f_s4")
            b1_, b2_, b3_, b4_ = sg1.n, sg2.n, sg3.n, sg4.n
            with nc.Block() as blk:
                @blk.vector
                def _(v):
                    sg1.inc(v.tensor_tensor(ghsc[0:64, 0:1024], gisc[0:64, 0:1024],
                                            ghsc[0:64, 0:1024], op=ALU.add))
                    sg1.inc(v.tensor_tensor(ghsc[0:64, 1024:2048],
                                            gisc[0:64, 1024:2048],
                                            ghsc[0:64, 1024:2048], op=ALU.add))
                    v.wait_ge(sg2.h, b2_ + 1)
                    v.tensor_tensor(ghsc[0:64, 2048:3072], ghsc[0:64, 2048:3072],
                                    ghsc[0:64, 0:1024], op=ALU.mult)
                    v.drain()
                    sg3.inc(v.tensor_tensor(ghsc[0:64, 2048:3072],
                                            ghsc[0:64, 2048:3072],
                                            gisc[0:64, 2048:3072], op=ALU.add))
                    v.wait_ge(sg4.h, b4_ + 2)
                    v.tensor_tensor(snew[:], slotS[:], ghsc[0:64, 2048:3072],
                                    op=ALU.subtract)
                    v.drain()
                    v.tensor_tensor(snew[:], snew[:], ghsc[0:64, 1024:2048],
                                    op=ALU.mult)
                    v.drain()
                    v.tensor_tensor(snew[:], snew[:], ghsc[0:64, 2048:3072],
                                    op=ALU.add)
                    v.drain()
                    v.tensor_copy(sib[:], snew[:])

                @blk.scalar
                def _(sc):
                    sc.wait_ge(sg1.h, b1_ + 1)
                    sg2.inc(sc.activation(ghsc[0:64, 0:1024], ghsc[0:64, 0:1024],
                                          AF.Sigmoid, bias=0.0, scale=1.0))
                    sc.wait_ge(sg1.h, b1_ + 2)
                    sg4.inc(sc.activation(ghsc[0:64, 1024:2048],
                                          ghsc[0:64, 1024:2048],
                                          AF.Sigmoid, bias=0.0, scale=1.0))
                    sc.wait_ge(sg3.h, b3_ + 1)
                    sg4.inc(sc.activation(ghsc[0:64, 2048:3072],
                                          ghsc[0:64, 2048:3072],
                                          AF.Tanh, bias=0.0, scale=1.0))
            dbg(env, "snew", snew[:], [64, 1024])
            tr64(sib, snf, 8)

            swq = env.sem("l0f_swq")
            sqv = env.sem("l0f_qv")
            b_swq, b_sqv = swq.n, sqv.n
            with contextlib.ExitStack() as lb:
                psA = env.psum(env.uname("psA"), [128, 1024], F32, stack=lb)
                psV = env.psum(env.uname("psV"), [128, 1024], F32, stack=lb)
                with nc.Block() as blk:
                    @blk.gpsimd
                    def _(g):
                        for k in range(8):
                            if k >= 2:
                                g.wait_ge(sqv.h, b_sqv + k - 1)
                            swq.dma(g.dma_start(
                                wstr[:, (k % 2) * 512:(k % 2) * 512 + 256],
                                env.hbm["l0f_wq"][k * 128:(k + 1) * 128, :]))
                            swq.dma(g.dma_start(
                                wstr[:, (k % 2) * 512 + 256:(k % 2) * 512 + 512],
                                env.hbm["l0f_wk"][k * 128:(k + 1) * 128, :]))
                    @blk.tensor
                    def _(t):
                        for k in range(8):
                            t.wait_ge(swq.h, b_swq + (k + 1) * 32)
                            t.matmul(psA[0:64, 0:256], snf[:, k * 64:(k + 1) * 64],
                                     wstr[:, (k % 2) * 512:(k % 2) * 512 + 256],
                                     start=(k == 0), stop=(k == 7))
                            ins = t.matmul(
                                psA[0:64, 512:768], snf[:, k * 64:(k + 1) * 64],
                                wstr[:, (k % 2) * 512 + 256:(k % 2) * 512 + 512],
                                start=(k == 0), stop=(k == 7))
                            if k == 7:
                                t.drain()
                            sqv.inc(ins if k < 7 else t.nop())
                with nc.Block() as blk:
                    @blk.vector
                    def _(v):
                        v.tensor_tensor(qsm[0:64, 0:256], psA[0:64, 0:256],
                                        bqb[0:64, 0:256], op=ALU.add)
                        v.tensor_tensor(qsm[0:64, 256:512], psA[0:64, 512:768],
                                        bqb[0:64, 256:512], op=ALU.add)
                for vh in range(2):
                    svp = env.sem("l0f_vp")
                    svc = env.sem("l0f_vc")
                    b_svp, b_svc = svp.n, svc.n
                    with nc.Block() as blk:
                        @blk.gpsimd
                        def _(g, vh=vh, b_svp=b_svp, b_svc=b_svc, svc=svc):
                            for k in range(8):
                                if k >= 2:
                                    g.wait_ge(svc.h, b_svc + k - 1)
                                svp.dma(g.dma_start(
                                    wstr[:, (k % 2) * 512:(k % 2 + 1) * 512],
                                    env.hbm["l0f_wv"][k * 128:(k + 1) * 128,
                                                      vh * 512:(vh + 1) * 512]))
                        @blk.tensor
                        def _(t, vh=vh, b_svp=b_svp, svc=svc):
                            for k in range(8):
                                t.wait_ge(svp.h, b_svp + (k + 1) * 16)
                                ins = t.matmul(
                                    psV[0:64, vh * 512:(vh + 1) * 512],
                                    snf[:, k * 64:(k + 1) * 64],
                                    wstr[:, (k % 2) * 512:(k % 2 + 1) * 512],
                                    start=(k == 0), stop=(k == 7))
                                if k == 7:
                                    t.drain()
                                svc.inc(ins if k < 7 else t.nop())
                with nc.Block() as blk:
                    @blk.vector
                    def _(v):
                        v.tensor_tensor(ysc[:], psV[0:64, :], bvb[:], op=ALU.add)
                        v.drain()
                        v.tensor_copy(vsb[:], ysc[:])
            tr64(qsm[0:64, 0:256], qf, 2)
            tr64(qsm[0:64, 256:512], kf, 2)
            with contextlib.ExitStack() as lb:
                psSc = env.psum(env.uname("psSc"), [128, 512], F32, stack=lb)
                psC2 = env.psum(env.uname("psC2"), [128, 1024], F32, stack=lb)
                with nc.Block() as blk:
                    @blk.tensor
                    def _(t):
                        for c in range(2):
                            t.matmul(psSc[0:64, 0:64], kf[:, c * 64:(c + 1) * 64],
                                     qf[:, c * 64:(c + 1) * 64],
                                     start=(c == 0), stop=(c == 1))
                        t.drain()
                with nc.Block() as blk:
                    @blk.scalar
                    def _(sc):
                        sc.activation(Pt[0:64, 0:64], psSc[0:64, 0:64], AF.Exp,
                                      bias=env.SHIFTC[0:64, 1:2], scale=0.0625)
                with nc.Block() as blk:
                    @blk.tensor
                    def _(t):
                        t.matmul(psC2[0:64, 0:512], Pt[0:64, 0:64],
                                 vsb[0:64, 0:512], start=True, stop=True)
                        t.matmul(psC2[0:64, 512:1024], Pt[0:64, 0:64],
                                 vsb[0:64, 512:1024], start=True, stop=True)
                        t.matmul(psSc[0:64, 256:257], Pt[0:64, 0:64],
                                 env.ONES[0:64, 0:1], start=True, stop=True)
                        t.drain()
                with nc.Block() as blk:
                    @blk.vector
                    def _(v):
                        v.reciprocal(dnv[0:64, 2:3], psSc[0:64, 256:257])
                        v.drain()
                        v.tensor_scalar(y2[:], psC2[0:64, :], dnv[0:64, 2:3],
                                        None, op0=ALU.mult)
                        v.drain()
                        v.tensor_tensor(ysc[:], y2[:], snew[:], op=ALU.add)

            dbg(env, "vsb", vsb[:], [64, 1024])
            dbg(env, "ysc", ysc[:], [64, 1024])
            sg5 = env.sem("l0f_s5")
            b5_ = sg5.n
            with nc.Block() as blk:
                @blk.vector
                def _(v):
                    v.tensor_scalar(y2[:], ysc[:], 1.0, 0.0, op0=ALU.mult,
                                    op1=ALU.add, accum_out=lnst[0:64, 0:1])
                    v.scalar_tensor_tensor(y2[:], ysc[:], 1.0, ysc[:],
                                           op0=ALU.mult, op1=ALU.mult,
                                           accum_out=lnst[0:64, 1:2])
                    v.drain()
                    v.tensor_scalar(lnst[0:64, 2:3], lnst[0:64, 0:1], 1.0 / 1024,
                                    None, op0=ALU.mult)
                    v.tensor_scalar(lnst[0:64, 3:4], lnst[0:64, 1:2], 1.0 / 1024,
                                    1e-5, op0=ALU.mult, op1=ALU.add)
                    v.drain()
                    v.scalar_tensor_tensor(lnst[0:64, 4:5], lnst[0:64, 2:3], 1.0,
                                           lnst[0:64, 2:3], op0=ALU.mult,
                                           op1=ALU.mult)
                    v.drain()
                    sg5.inc(v.tensor_tensor(lnst[0:64, 5:6], lnst[0:64, 3:4],
                                            lnst[0:64, 4:5], op=ALU.subtract))
                @blk.scalar
                def _(sc):
                    sc.wait_ge(sg5.h, b5_ + 1)
                    sc.activation(lnst[0:64, 6:7], lnst[0:64, 5:6], AF.Sqrt,
                                  bias=0.0, scale=1.0)
            with nc.Block() as blk:
                @blk.vector
                def _(v):
                    v.reciprocal(lnst[0:64, 7:8], lnst[0:64, 6:7])
                    v.drain()
                    v.tensor_scalar(y2[:], ysc[:], lnst[0:64, 2:3],
                                    lnst[0:64, 7:8], op0=ALU.subtract,
                                    op1=ALU.mult)
                    v.drain()
                    v.tensor_tensor(y2[:], y2[:], lgb[0:64, :], op=ALU.mult)
                    v.drain()
                    v.tensor_tensor(y2[:], y2[:], lbb[0:64, :], op=ALU.add)
                    v.drain()
                    v.tensor_copy(ylb[:], y2[:])
            dbg(env, "y2", y2[:], [64, 1024])
            tr64(ylb, ylf, 8)

            with contextlib.ExitStack() as lb:
                psSV = env.psum(env.uname("psSV"), [128, 1024], F32, stack=lb)
                psCV = env.psum(env.uname("psCV"), [128, 1024], F32, stack=lb)
                for vh in range(2):
                    svp = env.sem("l0f_sv")
                    svc = env.sem("l0f_svc")
                    b_svp, b_svc = svp.n, svc.n
                    with nc.Block() as blk:
                        @blk.gpsimd
                        def _(g, vh=vh, b_svp=b_svp, b_svc=b_svc, svc=svc):
                            for k in range(8):
                                if k >= 2:
                                    g.wait_ge(svc.h, b_svc + k - 1)
                                svp.dma(g.dma_start(
                                    wstr[:, (k % 2) * 512:(k % 2 + 1) * 512],
                                    env.hbm["l0f_wvp"][k * 128:(k + 1) * 128,
                                                       vh * 512:(vh + 1) * 512]))
                        @blk.tensor
                        def _(t, vh=vh, b_svp=b_svp, svc=svc):
                            for k in range(8):
                                t.wait_ge(svp.h, b_svp + (k + 1) * 16)
                                ins = t.matmul(
                                    psSV[0:64, vh * 512:(vh + 1) * 512],
                                    ylf[:, k * 64:(k + 1) * 64],
                                    wstr[:, (k % 2) * 512:(k % 2 + 1) * 512],
                                    start=(k == 0), stop=(k == 7))
                                if k == 7:
                                    t.drain()
                                svc.inc(ins if k < 7 else t.nop())
                with nc.Block() as blk:
                    @blk.vector
                    def _(v):
                        v.tensor_copy(svb[:], psSV[0:64, :])
                tr64(svb, svf, 8)
                for vh in range(2):
                    svp = env.sem("l0f_wo")
                    svc = env.sem("l0f_woc")
                    b_svp, b_svc = svp.n, svc.n
                    with nc.Block() as blk:
                        @blk.gpsimd
                        def _(g, vh=vh, b_svp=b_svp, b_svc=b_svc, svc=svc):
                            for k in range(8):
                                if k >= 2:
                                    g.wait_ge(svc.h, b_svc + k - 1)
                                svp.dma(g.dma_start(
                                    wstr[:, (k % 2) * 512:(k % 2 + 1) * 512],
                                    env.hbm["l0f_wo"][k * 128:(k + 1) * 128,
                                                      vh * 512:(vh + 1) * 512]))
                        @blk.tensor
                        def _(t, vh=vh, b_svp=b_svp, svc=svc):
                            for k in range(8):
                                t.wait_ge(svp.h, b_svp + (k + 1) * 16)
                                t.matmul(psSV[0:64, vh * 512:(vh + 1) * 512],
                                         svf[:, k * 64:(k + 1) * 64],
                                         wstr[:, (k % 2) * 512:(k % 2 + 1) * 512],
                                         start=(k == 0), stop=(k == 7))
                                ins = t.matmul(
                                    psCV[0:1, vh * 512:(vh + 1) * 512],
                                    bvpb[:, k:k + 1],
                                    wstr[:, (k % 2) * 512:(k % 2 + 1) * 512],
                                    start=(k == 0), stop=(k == 7))
                                if k == 7:
                                    t.drain()
                                svc.inc(ins if k < 7 else t.nop())
                with contextlib.ExitStack() as lb2:
                    pb3 = env.psum(env.uname("pbc3"), [64, 512], F32, stack=lb2)
                    scv = env.sem("l0f_cv")
                    for c0 in (0, 512):
                        b_scv = scv.n
                        with nc.Block() as blk:
                            @blk.sync
                            def _(e, c0=c0, b_scv=b_scv):
                                scv.dma(e.dma_start(rstg2[0:1, 0:512],
                                        env.hbm["l0f_bo"]
                                        .rearrange("(a b) -> a b", a=1)
                                        [0:1, c0:c0 + 512]))
                            @blk.vector
                            def _(v, c0=c0, b_scv=b_scv):
                                v.wait_ge(scv.h, b_scv + 16)
                                v.tensor_tensor(rstg[0:1, 0:512],
                                                psCV[0:1, c0:c0 + 512],
                                                rstg2[0:1, 0:512], op=ALU.add)
                                v.drain()
                                v.tensor_copy(rstb[0:1, 0:512], rstg[0:1, 0:512])
                        with nc.Block() as blk:
                            @blk.tensor
                            def _(t):
                                t.matmul(pb3[0:64, 0:512], env.ONES[0:1, 0:64],
                                         rstb[0:1, 0:512], start=True, stop=True)
                                t.drain()
                        with nc.Block() as blk:
                            @blk.vector
                            def _(v, c0=c0):
                                v.tensor_copy(cvb[0:64, c0:c0 + 512],
                                              pb3[0:64, 0:512])
                with nc.Block() as blk:
                    @blk.vector
                    def _(v):
                        v.tensor_tensor(svwa[:], psSV[0:64, :], cvb[0:64, :],
                                        op=ALU.add)

            dbg(env, "svwa", svwa[:], [64, 1024])
        with contextlib.ExitStack() as lc:
            af = env.sbuf(env.uname("af"), [64, 2048], BF16, stack=lc)
            satr = env.sem("l0f_at")
            sate = env.sem("l0f_ae")
            b_satr, b_sate = satr.n, sate.n
            with contextlib.ExitStack() as lb:
                pt2 = env.psum(env.uname("ptr2"), [128, 2048], BF16, stack=lb)
                with nc.Block() as blk:
                    @blk.tensor
                    def _(t):
                        for g in range(4):
                            half = g % 2
                            if g >= 2:
                                t.wait_ge(sate.h, b_sate + g - 1)
                            for j in range(4):
                                ti = g * 4 + j
                                t.transpose(
                                    pt2[0:64, half * 1024 + j * 128:
                                        half * 1024 + (j + 1) * 128],
                                    alpha[:, ti * 64:(ti + 1) * 64],
                                    env.IDENT[:])
                                if j == 3:
                                    t.drain()
                                    satr.inc(t.nop())
                    @blk.vector
                    def _(v):
                        for g in range(4):
                            half = g % 2
                            v.wait_ge(satr.h, b_satr + g + 1)
                            sate.inc(v.tensor_copy(
                                af[0:64, g * 512:(g + 1) * 512],
                                pt2[0:64, half * 1024:half * 1024 + 512]))
            so = env.sem("l0f_o")
            sev = env.sem("l0f_evo")
            b_so, b_sev = so.n, sev.n
            with contextlib.ExitStack() as lb:
                psO = env.psum(env.uname("psO"), [128, 2 * 512], F32, stack=lb)
                with nc.Block() as blk:
                    @blk.tensor
                    def _(t):
                        for ti in range(16):
                            for e2 in range(2):
                                oi = ti * 2 + e2
                                buf = oi % 2
                                if oi >= 2:
                                    t.wait_ge(sev.h, b_sev + oi - 1)
                                t.matmul(
                                    psO[:, buf * 512:(buf + 1) * 512],
                                    af[0:64, ti * 128:(ti + 1) * 128],
                                    svwa[0:64, e2 * 512:(e2 + 1) * 512],
                                    start=True, stop=True)
                                t.drain()
                                so.inc(t.nop())
                    @blk.vector
                    def _(v):
                        for ti in range(16):
                            for e2 in range(2):
                                oi = ti * 2 + e2
                                buf = oi % 2
                                v.wait_ge(so.h, b_so + oi + 1)
                                xs = tl(env.X, ti)[:, e2 * 512:(e2 + 1) * 512]
                                sev.inc(v.scalar_tensor_tensor(
                                    xs, psO[:, buf * 512:(buf + 1) * 512], 1.0,
                                    xs, op0=ALU.mult, op1=ALU.add))
    emit_ln_epilogue(env)


def emit_l1_attn(env):
    nc = env.nc
    Pb = env.BIGPOOL
    zsc = nc.dram_tensor(env.uname("zsc"), [2048, 2048], BF16,
                         kind="Internal").ap()
    szd = env.sem("l1a_zd")
    with contextlib.ExitStack() as ls:
        XF = emit_shadow(env, ls)
        wks = env.sbuf(env.uname("wks"), [128, 2 * 384], BF16, stack=ls)
        qkvh = env.sbuf(env.uname("qkvh"), [128, 3 * 2048], BF16, stack=ls)
        qrot = env.sbuf(env.uname("qrot"), [128, 2048], BF16, stack=ls)
        krot = env.sbuf(env.uname("krot"), [128, 2048], BF16, stack=ls)
        rscr = env.sbuf(env.uname("rscr"), [128, 2048], BF16, stack=ls)
        qsw = env.sbuf(env.uname("qsw"), [128, 2048], BF16, stack=ls)
        ksw = env.sbuf(env.uname("ksw"), [128, 2048], BF16, stack=ls)
        cosb = env.sbuf(env.uname("cosb"), [128, 2048], BF16, stack=ls)
        sinb = env.sbuf(env.uname("sinb"), [128, 2048], BF16, stack=ls)
        vaug1 = env.sbuf(env.uname("vaug1"), [128, 16 * 129], BF16, stack=ls)
        vaug2 = env.sbuf(env.uname("vaug2"), [128, 16 * 129], BF16, stack=ls)
        cselfS = env.sbuf(env.uname("cselfS"), [128, 2], F32, stack=ls)
        cmax2 = env.sbuf(env.uname("cmax2"), [2, 4], F32, stack=ls)
        b2bf = env.sbuf(env.uname("b2bf"), [1, 2], BF16, stack=ls)
        biascol = env.sbuf(env.uname("biascol"), [128, 2], F32, stack=ls)
        lam4 = env.sbuf(env.uname("lam4"), [64, 4], F32, stack=ls)
        lprodb = env.sbuf(env.uname("lprodb"), [64, 2], BF16, stack=ls)
        lrow = env.sbuf(env.uname("lrow"), [1, 4], F32, stack=ls)
        nlbf = env.sbuf(env.uname("nlbf"), [1, 1], BF16, stack=ls)
        nlcol = env.sbuf(env.uname("nlcol"), [128, 1], F32, stack=ls)
        linv = env.sbuf(env.uname("linv"), [128, 4], F32, stack=ls)
        ascr = env.sbuf(env.uname("ascr"), [128, 256], F32, stack=ls)
        bscr = env.sbuf(env.uname("bscr"), [128, 256], F32, stack=ls)
        zcomb = env.sbuf(env.uname("zcomb"), [128, 256], BF16, stack=ls)
        zfm = env.sbuf(env.uname("zfm"), [128, 512], BF16, stack=ls)

        sc0 = env.sem("l1a_c")
        spre = env.sem("l1a_p")
        b_sc0 = sc0.n
        with contextlib.ExitStack() as ps0:
            psL = env.psum(env.uname("psL"), [128, 128], F32, stack=ps0)
            with nc.Block() as blk:
                @blk.gpsimd
                def _(g):
                    sc0.dma(g.dma_start(cosb[:], env.hbm["c_cos"]))
                    sc0.dma(g.dma_start(sinb[:], env.hbm["c_sin"]))
                    g.wait_ge(sc0.h, sc0.n)
                @blk.sync
                def _(e):
                    sc0.dma(e.dma_start(lam4[:], env.hbm["l1a_lam"]))
                @blk.vector
                def _(v):
                    v.wait_ge(sc0.h, b_sc0 + 48)
                    v.memset(apcols(vaug1, 128, 129, 16), 1.0)
                    v.memset(apcols(vaug2, 128, 129, 16), 1.0)
                    ins = v.tensor_tensor(
                        lprodb[:, 0:2],
                        bass.AP(lam4[:, 0:1].tensor, lam4[:, 0:1].offset,
                                [list(lam4[:, 0:1].ap[0]), [2, 2]]),
                        bass.AP(lam4[:, 0:1].tensor, lam4[:, 0:1].offset + 1,
                                [list(lam4[:, 0:1].ap[0]), [2, 2]]),
                        op=ALU.mult)
                    spre.inc(ins)
            with nc.Block() as blk:
                @blk.tensor
                def _(t):
                    t.matmul(psL[0:2, 0:1], lprodb[0:64, 0:2],
                             env.ONES[0:64, 0:1], start=True, stop=True)
                    t.drain()
            with nc.Block() as blk:
                @blk.scalar
                def _(sc):
                    sc.activation(cmax2[0:2, 2:3], psL[0:2, 0:1], AF.Exp,
                                  bias=0.0, scale=1.0)
            with nc.Block() as blk:
                @blk.tensor
                def _(t):
                    t.transpose(psL[0:1, 4:6], cmax2[0:2, 2:3], env.IDF32[0:2, 0:2])
                    t.drain()
            with nc.Block() as blk:
                @blk.vector
                def _(v):
                    v.tensor_copy(lrow[0:1, 0:2], psL[0:1, 4:6])
                    v.drain()
                    v.tensor_tensor(lrow[0:1, 2:3], lrow[0:1, 0:1], lrow[0:1, 1:2],
                                    op=ALU.subtract)
                    v.drain()
                    v.tensor_scalar(lrow[0:1, 3:4], lrow[0:1, 2:3], -1.0, -0.2,
                                    op0=ALU.mult, op1=ALU.add)
                    v.drain()
                    v.tensor_copy(nlbf[0:1, 0:1], lrow[0:1, 3:4])
            with nc.Block() as blk:
                @blk.tensor
                def _(t):
                    t.matmul(psL[:, 8:9], env.ONES[0:1, :], nlbf[0:1, 0:1],
                             start=True, stop=True)
                    t.drain()
            with nc.Block() as blk:
                @blk.vector
                def _(v):
                    v.tensor_copy(nlcol[:, 0:1], psL[:, 8:9])
        dbg(env, "nlcol", nlcol[:], [128, 1])

        wr = env.hbm["l1a_wqkv"].rearrange("(kc p) m -> p kc m", p=128)
        swk = env.sem("l1a_wk")
        sk = env.sem("l1a_kc")
        sq = env.sem("l1a_q")
        sqe = env.sem("l1a_qe")
        srot = env.sem("l1a_r")
        stv = env.sem("l1a_tv")
        sve = env.sem("l1a_ve")
        skq = env.sem("l1a_kq")
        ssf = env.sem("l1a_sf")
        scs = env.sem("l1a_cs")
        sct = env.sem("l1a_ct")
        scb = env.sem("l1a_cb")
        sbc = env.sem("l1a_bc")
        sS = env.sem("l1a_S")
        sexp = env.sem("l1a_e")
        sZ = env.sem("l1a_Z")
        sZn = env.sem("l1a_Zn")
        sZt = env.sem("l1a_Zt")
        sZe = env.sem("l1a_Ze")

        for h in range(16):
            with contextlib.ExitStack() as ps1:
                psQ = env.psum(env.uname("psQ"), [128, 6 * 512], F32, stack=ps1)
                b_swk, b_sk, b_sq, b_sqe = swk.n, sk.n, sq.n, sqe.n
                with nc.Block() as blk:
                    @blk.gpsimd
                    def _(g):
                        for half in range(2):
                            for k in range(8):
                                idx = half * 8 + k
                                if idx >= 2:
                                    g.wait_ge(sk.h, b_sk + idx - 1)
                                swk.dma(g.dma_start(
                                    ap3(wks[:, (idx % 2) * 384:(idx % 2 + 1) * 384],
                                        1, 384),
                                    wr[:, k:k + 1, 384 * h:384 * (h + 1)]))

                    @blk.tensor
                    def _(t):
                        for half in range(2):
                            for k in range(8):
                                idx = half * 8 + k
                                t.wait_ge(swk.h, b_swk + (idx + 1) * 16)
                                if half == 1 and k == 0:
                                    t.wait_ge(sqe.h, b_sqe + 6)
                                for m in range(3):
                                    for q2 in range(2):
                                        gi = m * 2 + q2
                                        ins = t.matmul(
                                            psQ[:, gi * 512:(gi + 1) * 512],
                                            wks[:, (idx % 2) * 384 + m * 128:
                                                (idx % 2) * 384 + (m + 1) * 128],
                                            XF[:, k * 2048 + half * 1024 +
                                                   q2 * 512:k * 2048 + half * 1024 +
                                                   (q2 + 1) * 512],
                                            start=(k == 0), stop=(k == 7))
                                        if k == 7:
                                            sq.inc(ins)
                                            if m == 2 and q2 == 1:
                                                sk.inc(t.nop())
                                        elif m == 2 and q2 == 1:
                                            sk.inc(ins)

                    @blk.vector
                    def _(v):
                        for half in range(2):
                            for m in range(3):
                                for q2 in range(2):
                                    gi = m * 2 + q2
                                    v.wait_ge(sq.h, b_sq + half * 6 + gi + 1)
                                    sqe.inc(v.tensor_copy(
                                        qkvh[:, m * 2048 + half * 1024 + q2 * 512:
                                             m * 2048 + half * 1024 + (q2 + 1) * 512],
                                        psQ[:, gi * 512:(gi + 1) * 512]))

            if h == 0:
                dbg(env, "xf0", XF[:, 0:2048], [128, 2048])
                dbg(env, "qkvh0", qkvh[:, 0:2048], [128, 2048])
            with contextlib.ExitStack() as ps2:
                psT = env.psum(env.uname("psT"), [128, 2048], BF16, stack=ps2)
                b_stv, b_sve = stv.n, sve.n
                b_srot = srot.n
                ssw = env.sem("l1a_sw")
                b_ssw = ssw.n
                with nc.Block() as blk:
                    @blk.sync
                    def _(e):
                        for dst, src0 in ((qsw, 0), (ksw, 2048)):
                            ssw.dma(e.dma_start(dst[0:64, :],
                                                qkvh[64:128, src0:src0 + 2048]))
                            ssw.dma(e.dma_start(dst[64:128, :],
                                                qkvh[0:64, src0:src0 + 2048]))

                    @blk.vector
                    def _(v):
                        v.wait_ge(ssw.h, b_ssw + 64)
                        for dst, sw2, src0 in ((qrot, qsw, 0), (krot, ksw, 2048)):
                            v.tensor_tensor(rscr[:, :], sw2[:, :], sinb[:, :],
                                            op=ALU.mult)
                            v.tensor_tensor(dst[:, :], qkvh[:, src0:src0 + 2048],
                                            cosb[:, :], op=ALU.mult)
                            v.drain()
                            srot.inc(v.tensor_tensor(dst[:, :], dst[:, :], rscr[:, :],
                                                     op=ALU.add))
                        for g in range(4):
                            half = g % 2
                            v.wait_ge(stv.h, b_stv + g + 1)
                            sve.inc(v.tensor_copy(
                                apcols3(vaug1, g * 4 * 129, 129, 4, 128),
                                ap3(psT[:, half * 1024:half * 1024 + 512], 4, 128)))
                            sve.inc(v.tensor_scalar(
                                apcols3(vaug2, g * 4 * 129, 129, 4, 128),
                                ap3(psT[:, half * 1024:half * 1024 + 512], 4, 128),
                                nlcol[:, 0:1], None, op0=ALU.mult))

                    @blk.tensor
                    def _(t):
                        for g in range(4):
                            half = g % 2
                            if g >= 2:
                                t.wait_ge(sve.h, b_sve + 2 * (g - 1))
                            for j in range(4):
                                ti = g * 4 + j
                                t.transpose(
                                    psT[:, half * 1024 + j * 128:
                                        half * 1024 + (j + 1) * 128],
                                    qkvh[:, 4096 + ti * 128:4096 + (ti + 1) * 128],
                                    env.IDENT[:])
                                if j == 3:
                                    t.drain()
                                    stv.inc(t.nop())

            with contextlib.ExitStack() as ps3:
                psS = env.psum(env.uname("psS"), [128, 4 * 512], F32, stack=ps3)
                psZ1 = env.psum(env.uname("psZ1"), [128, 512], F32, stack=ps3)
                psZ2 = env.psum(env.uname("psZ2"), [128, 512], F32, stack=ps3)
                psT2 = env.psum(env.uname("psT2"), [128, 2048], BF16, stack=ps3)
                b_sS, b_sexp, b_sZ = sS.n, sexp.n, sZ.n
                b_sZn, b_sZt, b_sZe, b_szd = sZn.n, sZt.n, sZe.n, szd.n
                with nc.Block() as blk:
                    @blk.tensor
                    def _(t):
                        for qc in range(4):
                            for ti in range(16):
                                gidx = qc * 16 + ti
                                buf = gidx % 2
                                if gidx >= 2:
                                    t.wait_ge(sexp.h, b_sexp + (gidx - 1) * 2)
                                t.matmul(psS[:, buf * 1024:buf * 1024 + 512],
                                         krot[0:64, ti * 128:(ti + 1) * 128],
                                         qrot[0:64, qc * 512:(qc + 1) * 512],
                                         start=True, stop=True, tile_position=(0, 0))
                                ins = t.matmul(
                                    psS[:, buf * 1024 + 512:buf * 1024 + 1024],
                                    krot[64:128, ti * 128:(ti + 1) * 128],
                                    qrot[64:128, qc * 512:(qc + 1) * 512],
                                    start=True, stop=True, tile_position=(64, 0))
                                sS.inc(ins)
                            for h2 in range(2):
                                zi = qc * 2 + h2
                                if zi >= 1:
                                    t.wait_ge(sZn.h, b_sZn + zi * 3)
                                t.wait_ge(sexp.h, b_sexp + (qc * 16 + 16) * 2)
                                for qs in range(2):
                                    qsub = h2 * 2 + qs
                                    for br, (pbase, va) in enumerate(
                                            ((0, vaug1), (8192, vaug2))):
                                        for ti in range(16):
                                            t.matmul(
                                                (psZ1 if br == 0 else psZ2)
                                                [:, qs * 129:qs * 129 + 129],
                                                Pb[:, pbase + ti * 512 + qsub * 128:
                                                   pbase + ti * 512 + (qsub + 1) * 128],
                                                va[:, ti * 129:(ti + 1) * 129],
                                                start=(ti == 0), stop=(ti == 15))
                                        t.drain()
                                        sZ.inc(t.nop())
                            for h2 in range(2):
                                zi = qc * 2 + h2
                                half = zi % 2
                                t.wait_ge(sZn.h, b_sZn + zi * 3 + 3)
                                if zi >= 2:
                                    t.wait_ge(sZe.h, b_sZe + zi - 1)
                                for qs in range(2):
                                    t.transpose(
                                        psT2[:, half * 1024 + qs * 128:
                                             half * 1024 + (qs + 1) * 128],
                                        zcomb[:, qs * 128:(qs + 1) * 128],
                                        env.IDENT[:])
                                t.drain()
                                sZt.inc(t.nop())

                    @blk.scalar
                    def _(sc):
                        for qc in range(4):
                            if qc > 0:
                                sc.wait_ge(sZ.h, b_sZ + qc * 8)
                            for ti in range(16):
                                gidx = qc * 16 + ti
                                buf = gidx % 2
                                sc.wait_ge(sS.h, b_sS + gidx + 1)
                                sexp.inc(sc.activation(
                                    Pb[:, ti * 512:(ti + 1) * 512],
                                    psS[:, buf * 1024:buf * 1024 + 512],
                                    AF.Exp, bias=env.SHIFTC[:, 0:1], scale=0.125))
                                sexp.inc(sc.activation(
                                    Pb[:, 8192 + ti * 512:8192 + (ti + 1) * 512],
                                    psS[:, buf * 1024 + 512:buf * 1024 + 1024],
                                    AF.Exp, bias=env.SHIFTC[:, 0:1], scale=0.125))

                    @blk.vector
                    def _(v):
                        for qc in range(4):
                            if qc > 0:
                                v.wait_ge(szd.h, b_szd + qc * 16)
                            for h2 in range(2):
                                zi = qc * 2 + h2
                                v.wait_ge(sZ.h, b_sZ + zi * 4 + 4)
                                v.reciprocal(linv[:, 0:2],
                                             apcols(psZ1, 128, 129, 2))
                                v.reciprocal(linv[:, 2:4],
                                             apcols(psZ2, 128, 129, 2))
                                v.drain()
                                v.tensor_tensor(ap3(ascr[:, 0:256], 2, 128),
                                                apcols3(psZ1, 0, 129, 2, 128),
                                                apbcast(linv, 0, 1, 2, 128),
                                                op=ALU.mult)
                                sZn.inc(v.tensor_tensor(
                                    ap3(bscr[:, 0:256], 2, 128),
                                    apcols3(psZ2, 0, 129, 2, 128),
                                    apbcast(linv, 2, 1, 2, 128),
                                    op=ALU.mult), 2)
                                v.drain()
                                v.tensor_tensor(zcomb[:, 0:256], ascr[:, 0:256],
                                                bscr[:, 0:256], op=ALU.add)
                                v.drain()
                                sZn.inc(v.nop())
                                zi2 = zi
                                half = zi2 % 2
                                v.wait_ge(sZt.h, b_sZt + zi2 + 1)
                                sZe.inc(v.tensor_copy(
                                    zfm[:, h2 * 256:(h2 + 1) * 256],
                                    psT2[:, half * 1024:half * 1024 + 256]))

                    @blk.sync
                    def _(e):
                        for qc in range(4):
                            e.wait_ge(sZe.h, b_sZe + qc * 2 + 2)
                            szd.dma(e.dma_start(
                                zsc[h * 128:(h + 1) * 128, qc * 512:(qc + 1) * 512],
                                zfm[:]))

    with contextlib.ExitStack() as ls2:
        pjw = env.sbuf(env.uname("pjw"), [128, 16 * 1024], BF16, stack=ls2)
        zw = env.sbuf(env.uname("zw"), [128, 2 * 2048], BF16, stack=ls2)
        po = env.psum(env.uname("po"), [128, 2 * 512], F32, stack=ls2)
        swp = env.sem("l1a_wp")
        szw = env.sem("l1a_zw")
        sop = env.sem("l1a_op")
        sevp = env.sem("l1a_ev")
        b_swp, b_szw, b_sop, b_sevp = swp.n, szw.n, sop.n, sevp.n
        zr = zsc.rearrange("(jc p) c -> p jc c", p=128)
        with nc.Block() as blk:
            @blk.gpsimd
            def _(g):
                for jc in range(16):
                    swp.dma(g.dma_start(
                        pjw[:, jc * 1024:(jc + 1) * 1024],
                        env.hbm["l1a_projT"][jc * 128:(jc + 1) * 128, :]))

            @blk.sync
            def _(e):
                e.wait_ge(szd.h, szd.n)
                for ti in range(NT):
                    if ti >= 2:
                        e.wait_ge(sop.h, b_sop + (ti - 1) * 2)
                    szw.dma(e.dma_start(
                        ap3(zw[:, (ti % 2) * 2048:(ti % 2 + 1) * 2048], 16, 128),
                        zr[:, :, ti * 128:(ti + 1) * 128]))

            @blk.tensor
            def _(t):
                t.wait_ge(swp.h, b_swp + 256)
                for ti in range(NT):
                    t.wait_ge(szw.h, b_szw + (ti + 1) * 16)
                    for e2 in range(2):
                        oi = ti * 2 + e2
                        buf = oi % 2
                        if oi >= 2:
                            t.wait_ge(sevp.h, b_sevp + oi - 1)
                        for jc in range(16):
                            ins = t.matmul(
                                po[:, buf * 512:(buf + 1) * 512],
                                zw[:, (ti % 2) * 2048 + jc * 128:
                                   (ti % 2) * 2048 + (jc + 1) * 128],
                                pjw[:, jc * 1024 + e2 * 512:jc * 1024 + (e2 + 1) * 512],
                                start=(jc == 0), stop=(jc == 15))
                        sop.inc(ins)

            @blk.vector
            def _(v):
                for ti in range(NT):
                    for e2 in range(2):
                        oi = ti * 2 + e2
                        buf = oi % 2
                        v.wait_ge(sop.h, b_sop + oi + 1)
                        xs = tl(env.X, ti)[:, e2 * 512:(e2 + 1) * 512]
                        sevp.inc(v.scalar_tensor_tensor(
                            xs, po[:, buf * 512:(buf + 1) * 512], 1.0, xs,
                            op0=ALU.mult, op1=ALU.add))
    emit_ln_epilogue(env)



BLOCK_EMITTERS = {}


def build_program(block_names, nseq=2):
    nc = bass.Bass("TRN2", target_bir_lowering=False)
    env = Env(nc)
    with env.es:
        declare_hbm(env, x_batches=nseq)
        alloc_core(env)
        emit_constants(env)
        for seq in range(nseq):
            emit_load_x(env, seq)
            for name in block_names:
                BLOCK_EMITTERS[name](env)
            emit_store_x(env, seq)
    return nc


BLOCK_EMITTERS["l1f"] = emit_l1_ffn
BLOCK_EMITTERS["l0a"] = emit_l0_attn
BLOCK_EMITTERS["l1a"] = emit_l1_attn
BLOCK_EMITTERS["l0f"] = emit_l0_ffn


def kernel(**inputs):
    w = prep_weights(inputs)
    x = inputs["L0_attn_x"]
    nc = build_program(["l0a", "l0f", "l1a", "l1f"])
    in_maps = []
    for c in range(NCORES):
        m = {"x": np.ascontiguousarray(x[2 * c:2 * c + 2])}
        m.update(w)
        in_maps.append(m)
    res = run_bass_kernel_spmd(nc, in_maps, core_ids=list(range(NCORES)))
    out = np.concatenate([r["out"] for r in res.results], axis=0)
    return np.ascontiguousarray(out.astype(np.float32))
